# version 04
# baseline (speedup 1.0000x reference)
import contextlib

import numpy as np
import ml_dtypes

import concourse.bass as bass
import concourse.mybir as mybir
from concourse.bass_utils import run_bass_kernel_spmd

F32 = mybir.dt.float32
BF16 = mybir.dt.bfloat16
AF = mybir.ActivationFunctionType
ALU = mybir.AluOpType
AX = mybir.AxisListType.X
NPBF16 = np.dtype(ml_dtypes.bfloat16)

B = 8
T = 4096
D = 1024
DO = 256
NL = 3
DF = 4096
NB = 16
TOPK = 4
QT = 1024
NQ = 4
SC = 512
KD = D // 128
KO = DO // 128
FT = DF // 128
EPS_LN = 1e-5
EPS_RMS = 1e-6
BIGNEG = 1e30

DEBUG_TAPS = []


def _veccols():
    cols = {}
    off = 0

    def put(name, n):
        nonlocal off
        cols[name] = off
        off += n

    put("ident", 128)
    put("ones", 1)
    put("epsln", 1)
    put("epsrms", 1)
    put("iota16", 16)
    put("iota99", 16)
    put("c8", 8)
    put("cw", 24)
    put("cbv", 6)
    put("wrb", 6)
    put("wib", 6)
    put("lav", 6)
    put("nwv", 6)
    put("b1c", 32)
    put("b2c", 8)
    put("rbc", 1)
    return cols, off


VEC_COLS, VCOLS = _veccols()
CPK_ONES = 0
CPK_BIG = 512
CPK_COLS = 640

ENGS = ("sync", "tensor", "vector", "scalar", "gpsimd")
QSEMS = ("qm", "qx0", "qx1", "qwm0", "qwm1", "qpb", "qpb2", "qs1", "qs2",
         "qw10", "qw11", "qw20", "qw21", "qo")
FENCE = set()


class Buf:
    __slots__ = ("ready", "readers", "name")

    def __init__(self, name=""):
        self.ready = None
        self.readers = []
        self.name = name


class KB:

    def __init__(self):
        self.items = {e: [] for e in ENGS}
        self.cnt = {e: 0 for e in ENGS}
        self.cnt.update({q: 0 for q in QSEMS})
        self.lastw = {e: {} for e in ENGS}
        self.pending = {e: [] for e in ENGS}

    def op(self, eng, fn, reads=(), sreads=(), writes=(), writes_pre=(),
           inc=True, dma=False, q=None):
        assert not (writes and not inc), "final writes must inc"
        waits = []

        def add(tok, hard):
            if tok is None:
                return
            pe, v = tok
            if pe == eng and not hard and eng == "tensor":
                return
            lw = self.lastw[eng]
            if lw.get(pe, 0) >= v:
                return
            lw[pe] = v
            waits.append((pe, v))

        for b in reads:
            add(b.ready, False)
        for b in sreads:
            add(b.ready, True)
        for b in tuple(writes) + tuple(writes_pre):
            add(b.ready, False)
            for t in b.readers:
                add(t, False)
        step = 16 if dma else 1
        sem = q if dma else eng
        assert not dma or q in QSEMS, f"dma needs a stream sem, got {q}"
        if dma and q == "qm" and self.cnt["qm"] > 0:
            if self.lastw[eng].get("qm", 0) < self.cnt["qm"]:
                self.lastw[eng]["qm"] = self.cnt["qm"]
                waits.append(("qm", self.cnt["qm"]))
        tok = None
        if inc:
            self.cnt[sem] += step
            tok = (sem, self.cnt[sem])
        self.items[eng].append((waits, fn, inc, step, sem))
        if inc:
            for b in self.pending[eng]:
                b.readers.append(tok)
            self.pending[eng].clear()
            for b in reads:
                b.readers.append(tok)
            for b in sreads:
                b.readers.append(tok)
            for b in writes:
                b.ready = tok
                b.readers = []
        else:
            for b in reads:
                if b not in self.pending[eng]:
                    self.pending[eng].append(b)
            for b in sreads:
                if b not in self.pending[eng]:
                    self.pending[eng].append(b)
        return tok

    def raw_wait(self, eng, toks, hard=False):
        waits = []
        for (pe, v) in toks:
            lw = self.lastw[eng]
            if pe == eng and not hard:
                continue
            if lw.get(pe, 0) >= v:
                continue
            lw[pe] = v
            waits.append((pe, v))
        if waits:
            self.items[eng].append((waits, None, False, 1, eng))

    def barrier(self, engs=("tensor", "vector", "scalar", "gpsimd")):
        snap = [(e, self.cnt[e]) for e in ENGS if self.cnt[e] > 0]
        for e in engs:
            self.raw_wait(e, [t for t in snap if t[0] != e])

    def replay(self, nc, sems):
        def make(ename):
            def handler(eng):
                for idx, (waits, fn, inc, step, sem) in enumerate(
                        self.items[ename]):
                    for (pe, v) in waits:
                        if pe in FENCE:
                            v = self.cnt[pe]
                        eng.wait_ge(sems[pe], v)
                    if fn is not None:
                        try:
                            inst = fn(eng)
                        except Exception as ex:
                            raise RuntimeError(
                                f"replay fail {ename}[{idx}]: {ex}") from ex
                        if inc:
                            inst.then_inc(sems[sem], step)
            return handler

        with nc.Block() as block:
            block.sync(make("sync"))
            block.tensor(make("tensor"))
            block.vector(make("vector"))
            block.scalar(make("scalar"))
            block.gpsimd(make("gpsimd"))



def _act_set_id(nc, need):
    from concourse.hw_specs import get_activation_tables
    for idx, funcs in enumerate(get_activation_tables(nc.m.arch).values()):
        if all(f in funcs for f in need):
            return idx
    raise ValueError(f"no act table set with {need}")



def build_nc():
    nc = bass.Bass("TRN2", target_bir_lowering=False)
    kb = KB()
    SET_SILU = _act_set_id(nc, [AF.Silu])
    SET_NLE = _act_set_id(nc, [AF.Exp, AF.Ln, AF.Square, AF.Relu, AF.Identity,
                               AF.Abs])
    SET_SIG = _act_set_id(nc, [AF.Sigmoid])

    def preload(eng, set_id):
        return eng.add_instruction(mybir.InstLoadActFuncSet(
            name=nc.get_next_instruction_name(), act_func_set_id=set_id,
            ins=[], outs=[]))

    xT = nc.dram_tensor("xT", [D, T], F32, kind="ExternalInput").ap()
    wmod = nc.dram_tensor("wmod", [D, 6 * D], F32, kind="ExternalInput").ap()
    dwT = nc.dram_tensor("dwT", [D, DO], F32, kind="ExternalInput").ap()
    upT = nc.dram_tensor("upT", [DO, D], F32, kind="ExternalInput").ap()
    wrT = nc.dram_tensor("wrT", [NL, DO, DO], F32, kind="ExternalInput").ap()
    wiT = nc.dram_tensor("wiT", [NL, DO, DO], F32, kind="ExternalInput").ap()
    owT = nc.dram_tensor("owT", [NL, DO, DO], F32, kind="ExternalInput").ap()
    w1T = nc.dram_tensor("w1T", [D, DF], F32, kind="ExternalInput").ap()
    w2T = nc.dram_tensor("w2T", [DF, D], F32, kind="ExternalInput").ap()
    rwT = nc.dram_tensor("rwT", [D, NB], F32, kind="ExternalInput").ap()
    vecd = nc.dram_tensor("vec", [128, VCOLS], F32, kind="ExternalInput").ap()
    cpkd = nc.dram_tensor("cpk", [128, CPK_COLS], BF16, kind="ExternalInput").ap()
    bigseld = nc.dram_tensor("bigsel", [16, NB * 128], BF16,
                             kind="ExternalInput").ap()
    outT = nc.dram_tensor("outT", [D, T], F32, kind="ExternalOutput").ap()
    w1s = nc.dram_tensor("w1s", [128, FT * KD * 128], BF16, kind="Internal").ap()
    w2s = nc.dram_tensor("w2s", [128, FT * KD * 128], BF16, kind="Internal").ap()

    x3 = xT.rearrange("(a p) t -> p a t", p=128)
    o3 = outT.rearrange("(a p) t -> p a t", p=128)
    wmod3 = wmod.rearrange("(a p) n -> p a n", p=128)
    w1T3 = w1T.rearrange("(a p) n -> p a n", p=128)
    w2T3 = w2T.rearrange("(a p) n -> p a n", p=128)
    dwT3 = dwT.rearrange("(a p) n -> p a n", p=128)
    upT3 = upT.rearrange("(a p) n -> p a n", p=128)
    rwT3 = rwT.rearrange("(a p) n -> p a n", p=128)
    wrT3 = wrT.rearrange("i (a p) n -> i p a n", p=128)
    wiT3 = wiT.rearrange("i (a p) n -> i p a n", p=128)
    owT3 = owT.rearrange("i (a p) n -> i p a n", p=128)

    taps = {}
    ctx = contextlib.ExitStack()

    def sb(name, shape, dtype):
        return ctx.enter_context(nc.sbuf_tensor(name, shape, dtype))

    def psu(name, shape, dtype=F32):
        return ctx.enter_context(nc.psum_tensor(name, shape, dtype))

    with ctx:
        vec = sb("vec_s", [128, VCOLS], F32)
        cpk = sb("cpk_s", [128, CPK_COLS], BF16)
        bigsel = sb("bigsel_s", [16, NB * 128], BF16)
        dwTb = sb("dwTb", [128, KD * DO], BF16)
        upTb = sb("upTb", [128, KO * D], BF16)
        wrTb = sb("wrTb", [128, NL * KO * DO], BF16)
        wiTb = sb("wiTb", [128, NL * KO * DO], BF16)
        owTb = sb("owTb", [128, NL * KO * DO], BF16)
        rwTs = sb("rwTs", [128, KD * NB], F32)
        rwTp = sb("rwTp", [128, KD * NB], F32)
        mod48 = sb("mod48", [128, 48], F32)
        modd = sb("modd", [128, 32], F32)
        sapb = sb("sapb", [128, 8], BF16)
        sbpb = sb("sbpb", [128, 8], BF16)
        m8ls = sb("m8ls", [128, NL * KO], F32)
        cb1p = sb("cb1p", [128, KO], F32)
        kcol = sb("kcol", [128, KO], F32)
        scr8 = sb("scr8", [128, 24], F32)
        KB1t = sb("KB1", [128, FT], F32)
        negJrow = sb("negJrow", [1, DO], BF16)
        b2row = sb("b2row", [1, D], BF16)
        onesrowf = sb("onesrowf", [1, 128], F32)
        rowscr = sb("rowscr", [1, SC], F32)
        rowscr2 = sb("rowscr2", [1, SC], F32)
        rowbf = sb("rowbf", [1, SC], BF16)
        kb16 = sb("kb16", [16, 1], F32)
        njr16 = sb("njr16", [16, 1], F32)
        tails = sb("tails", [128, NL, KO, 3], F32)
        state = sb("state", [128, NL * KO], F32)
        identb = sb("identb", [128, 128], BF16)
        genA = sb("genA", [128, 1024], F32)
        genB = sb("genB", [128, 512], F32)
        genC = sb("genC", [128, 1024], F32)
        XBt = sb("XB", [128, KD, QT], F32)
        X2t = sb("X2", [128, KD, 2 * QT], F32)
        AR = sb("AR", [128, 16000], F32)

        def arf(off_w, n_w):
            assert off_w + n_w <= 16000
            return AR[:, off_w:off_w + n_w]

        def arb(off_w, n_bf):
            assert off_w + (n_bf + 1) // 2 <= 16000
            return AR[:, off_w:off_w + (n_bf + 1) // 2].bitcast(BF16)

        p0_stgA = arf(0, 3072)
        p0_stgB = arf(3072, 3072)
        p0_stgbf = arb(6144, 6144)
        dw_stg = arf(0, KD * DO).rearrange("p (a n) -> p a n", a=KD)
        up_stg = arf(0, KO * D).rearrange("p (a n) -> p a n", a=KO)
        at_stg = arf(2048, KO * DO).rearrange("p (a n) -> p a n", a=KO)
        XRA = arf(0, KO * 1027).rearrange("p (o t) -> p o t", o=KO)
        XRB = arf(2054, KO * 1027).rearrange("p (o t) -> p o t", o=KO)
        LAs = [arb(4108, KO * SC).rearrange("p (o t) -> p o t", o=KO),
               arb(4620, KO * SC).rearrange("p (o t) -> p o t", o=KO)]
        AAs = [arf(5132, KO * SC).rearrange("p (o t) -> p o t", o=KO),
               arf(6156, KO * SC).rearrange("p (o t) -> p o t", o=KO)]
        T1s = [arf(7180, KO * SC).rearrange("p (o t) -> p o t", o=KO),
               arf(8204, KO * SC).rearrange("p (o t) -> p o t", o=KO)]
        rstdB = arf(9228, SC)
        XCB = arb(9740, KO * QT).rearrange("p (o t) -> p o t", o=KO)
        Rb = arb(10764, KO * QT).rearrange("p (o t) -> p o t", o=KO)
        Ib = arb(11788, KO * QT).rearrange("p (o t) -> p o t", o=KO)
        HSBs = [arb(12812, KO * SC).rearrange("p (o t) -> p o t", o=KO),
                arb(13324, KO * SC).rearrange("p (o t) -> p o t", o=KO)]
        xbr = arb(13836, 2 * SC).rearrange("p (s t) -> p s t", s=2)
        xsqr = arb(14348, 2 * SC).rearrange("p (s t) -> p s t", s=2)
        OSQ = arb(13836, KO * SC).rearrange("p (o t) -> p o t", o=KO)
        x2n = arb(0, KD * SC).rearrange("p (a t) -> p a t", a=KD)
        hh = arb(2048, FT * SC).rearrange("p (a t) -> p a t", a=FT)
        u32a = arf(10240, 4 * SC).rearrange("p (a t) -> p a t", a=4)
        x2br = arb(12288, 2 * SC).rearrange("p (s t) -> p s t", s=2)
        x2sqr = arb(12812, 2 * SC).rearrange("p (s t) -> p s t", s=2)
        z16 = arf(13324, SC)
        ztm = arf(13836, 64).rearrange("p (a e) -> p a e", a=4)
        mm1 = arf(13900, 64).rearrange("p (a e) -> p a e", a=4)
        tsc = arf(13964, 64).rearrange("p (a e) -> p a e", a=4)
        mask_em = arb(14028, 512)
        mm1b = arb(14284, 64)

        PBT = [psu(f"pb{k}", [128, 512], F32) for k in range(8)]

        bufs = {}

        def B_(name):
            if name not in bufs:
                bufs[name] = Buf(name)
            return bufs[name]

        PB = [B_(f"pb{k}") for k in range(8)]

        def vcol(name, i=0):
            c = VEC_COLS[name] + i
            return vec[:, c:c + 1]

        def vrow0(name, i=0):
            c = VEC_COLS[name] + i
            return vec[0:1, c:c + 1]

        def vcols(name, i, n):
            c = VEC_COLS[name] + i
            return vec[:, c:c + n]

        ident = vec[:, VEC_COLS["ident"]:VEC_COLS["ident"] + 128]
        onescol_bf = cpk[:, CPK_ONES:CPK_ONES + 1]
        onesrow512_bf = cpk[0:1, CPK_ONES:CPK_ONES + 512]

        E = kb.op

        def pe_drain():
            kb.raw_wait("tensor", [("tensor", kb.cnt["tensor"])], hard=True)

        ones11 = vec[0:1, VEC_COLS["ones"]:VEC_COLS["ones"] + 1]

        def dma(dst_ap, src_ap, reads=(), writes=(), writes_pre=(), q="qm"):
            return kb.op("sync",
                         (lambda d, s: (lambda e: e.dma_start(out=d, in_=s)))(
                             dst_ap, src_ap),
                         reads=reads, writes=writes, writes_pre=writes_pre,
                         inc=True, dma=True, q=q)

        def tap(name, src_ap, shape, dtype=F32, reads=()):
            if name not in DEBUG_TAPS:
                return
            t = nc.dram_tensor("tap_" + name, list(shape), dtype,
                               kind="ExternalOutput").ap()
            taps[name] = t
            dma(t, src_ap, reads=reads)

        VB, CB, BS = B_("vec"), B_("cpk"), B_("bigsel")
        dma(vec[:], vecd, writes=[VB])
        dma(cpk[:], cpkd, writes=[CB])
        dma(bigsel[:], bigseld, writes=[BS])
        XBS = [B_("XB0"), B_("XB1")]
        dma(XBt[:, :, 0:SC], x3[:, :, 0:SC], writes=[XBS[0]], q="qx0")
        dma(XBt[:, :, SC:QT], x3[:, :, SC:QT], writes=[XBS[1]], q="qx1")

        SCR = B_("scr8")
        SAPB, SBPB = B_("sapb"), B_("sbpb")
        E("scalar", lambda e: preload(e, SET_SILU), inc=False)
        E("scalar", lambda e: e.activation(out=scr8[:, 0:8], in_=vcols("c8", 0, 8),
                                           func=AF.Silu),
          reads=[VB], writes=[SCR])
        E("vector", lambda e: e.tensor_copy(out=sapb[:], in_=scr8[:, 0:8]),
          reads=[SCR], writes=[SAPB])

        stg = [B_("p0stgA"), B_("p0stgB")]
        stgb = [B_("p0stgbfA"), B_("p0stgbfB")]
        wm_stg = [p0_stgA[:, 0:1024].rearrange("p (a n) -> p a n", a=KD),
                  p0_stgB[:, 0:1024].rearrange("p (a n) -> p a n", a=KD)]
        wm_bf = [arb(6144, 1024).rearrange("p (a n) -> p a n", a=KD),
                 arb(6656, 1024).rearrange("p (a n) -> p a n", a=KD)]
        for oc in range(48):
            sl = oc % 2
            dma(wm_stg[sl], wmod3[:, :, oc * 128:(oc + 1) * 128],
                writes=[stg[sl]], q="qwm0" if sl == 0 else "qwm1")
            E("vector",
              (lambda sl_: lambda e: e.tensor_copy(
                  out=wm_bf[sl_], in_=wm_stg[sl_]))(sl),
              reads=[stg[sl]], writes=[stgb[sl]])
            pe_drain()
            for kt in range(KD):
                E("tensor",
                  (lambda occ, kt_, sl_: lambda e: e.matmul(
                      PBT[5][:, occ:occ + 1], wm_bf[sl_][:, kt_, :],
                      sapb[:, kt_:kt_ + 1],
                      start=(kt_ == 0), stop=(kt_ == KD - 1)))(oc, kt, sl),
                  reads=[stgb[sl], SAPB],
                  writes_pre=([PB[5]] if (kt == 0 and oc == 0) else []),
                  writes=([PB[5]] if (kt == KD - 1 and oc == 47) else []),
                  inc=(kt == KD - 1))
        MODP = B_("mod48")
        E("scalar", lambda e: e.activation(out=mod48[:], in_=PBT[5][:, 0:48],
                                           func=AF.Identity),
          reads=[PB[5]], writes=[MODP])
        tap("mod48", mod48[:], [128, 48], reads=[MODP])

        IDB = B_("identb")
        E("vector", lambda e: e.tensor_copy(out=identb[:], in_=ident),
          reads=[VB], writes=[IDB])
        MODD = B_("modd")
        E("vector", lambda e: e.tensor_scalar(out=modd[:, 0:8], in0=mod48[:, 8:16],
                                              scalar1=1.0, scalar2=None, op0=ALU.add),
          reads=[MODP], writes=[MODD])
        E("vector", lambda e: e.tensor_scalar(out=modd[:, 8:16], in0=mod48[:, 32:40],
                                              scalar1=1.0, scalar2=None, op0=ALU.add),
          reads=[MODP], writes=[MODD])
        for (src0, dstc) in ((16, 16), (40, 24)):
            E("vector", (lambda s0: lambda e: e.tensor_scalar(
                out=scr8[:, 8:16], in0=mod48[:, s0:s0 + 8], scalar1=0.0,
                scalar2=None, op0=ALU.is_ge))(src0),
              reads=[MODP], writes=[SCR])
            E("vector", lambda e: e.tensor_scalar(
                out=scr8[:, 8:16], in0=scr8[:, 8:16], scalar1=2.0, scalar2=-1.0,
                op0=ALU.mult, op1=ALU.add), writes=[SCR])
            E("scalar", (lambda s0: lambda e: e.activation(
                out=scr8[:, 16:24], in_=mod48[:, s0:s0 + 8], func=AF.Abs))(src0),
              reads=[MODP], writes=[SCR])
            E("vector", lambda e: e.tensor_scalar(
                out=scr8[:, 16:24], in0=scr8[:, 16:24], scalar1=0.25,
                scalar2=None, op0=ALU.max), writes=[SCR])
            E("vector", (lambda dc: lambda e: e.tensor_tensor(
                out=modd[:, dc:dc + 8], in0=scr8[:, 8:16], in1=scr8[:, 16:24],
                op=ALU.mult))(dstc),
              reads=[SCR], writes=[MODD])
        E("vector", lambda e: e.reciprocal(out=scr8[:, 8:16], in_=modd[:, 0:8]),
          reads=[MODD], writes=[SCR])
        E("vector", lambda e: e.tensor_tensor(out=scr8[:, 8:16], in0=scr8[:, 8:16],
                                              in1=mod48[:, 0:8], op=ALU.mult),
          reads=[MODP], writes=[SCR])
        E("vector", lambda e: e.tensor_copy(out=sapb[:], in_=scr8[:, 8:16]),
          reads=[SCR], writes=[SAPB])
        E("vector", lambda e: e.reciprocal(out=scr8[:, 16:24], in_=modd[:, 8:16]),
          reads=[MODD], writes=[SCR])
        E("vector", lambda e: e.tensor_tensor(out=scr8[:, 16:24], in0=scr8[:, 16:24],
                                              in1=mod48[:, 24:32], op=ALU.mult),
          reads=[MODP], writes=[SCR])
        E("vector", lambda e: e.tensor_copy(out=sbpb[:], in_=scr8[:, 16:24]),
          reads=[SCR], writes=[SBPB])

        M8 = B_("m8ls")
        E("scalar", lambda e: preload(e, SET_NLE), inc=False)
        E("scalar", lambda e: e.activation(out=scr8[:, 8:14], in_=vcols("lav", 0, 6),
                                           func=AF.Abs),
          reads=[VB], writes=[SCR])
        E("scalar", lambda e: e.activation(out=scr8[:, 14:20], in_=scr8[:, 8:14],
                                           func=AF.Exp, scale=-1.0),
          reads=[SCR], writes=[SCR])
        E("scalar", lambda e: e.activation(out=scr8[:, 14:20], in_=scr8[:, 14:20],
                                           func=AF.Ln, bias=vcol("ones")),
          reads=[VB], writes=[SCR])
        E("vector", lambda e: e.tensor_scalar(out=scr8[:, 8:14], in0=vcols("lav", 0, 6),
                                              scalar1=0.0, scalar2=None, op0=ALU.min),
          reads=[VB], writes=[SCR])
        E("vector", lambda e: e.tensor_tensor(out=scr8[:, 8:14], in0=scr8[:, 8:14],
                                              in1=scr8[:, 14:20], op=ALU.subtract),
          writes=[SCR])
        E("vector", lambda e: e.tensor_scalar(out=m8ls[:], in0=scr8[:, 8:14],
                                              scalar1=8.0, scalar2=None, op0=ALU.mult),
          reads=[SCR], writes=[M8])

        DWS, DWB = B_("dwstg"), B_("dwTb")
        dma(dw_stg, dwT3, writes=[DWS], writes_pre=[stg[0]])
        for kt in range(KD):
            E("vector",
              (lambda kt_: lambda e: e.tensor_scalar(
                  out=dwTb[:, kt_ * DO:(kt_ + 1) * DO], in0=dw_stg[:, kt_, :],
                  scalar1=modd[:, kt_:kt_ + 1], scalar2=None, op0=ALU.mult))(kt),
              reads=[DWS, MODD],
              writes=([DWB] if kt == KD - 1 else []),
              writes_pre=([DWB] if kt == 0 else []),
              inc=(kt == KD - 1))
        NJ = B_("negJrow")
        pe_drain()
        for kt in range(KD):
            E("tensor",
              (lambda kt_: lambda e: e.matmul(
                  PBT[3][0:1, 0:DO], onescol_bf,
                  dwTb[:, kt_ * DO:(kt_ + 1) * DO],
                  start=(kt_ == 0), stop=(kt_ == KD - 1)))(kt),
              reads=[DWB, CB], writes_pre=([PB[3]] if kt == 0 else []),
              writes=([PB[3]] if kt == KD - 1 else []), inc=(kt == KD - 1))
        E("scalar", lambda e: e.activation(out=negJrow[:], in_=PBT[3][0:1, 0:DO],
                                           func=AF.Identity, scale=-1.0),
          reads=[PB[3]], writes=[NJ])
        pe_drain()
        for kt in range(KD):
            E("tensor",
              (lambda kt_: lambda e: e.matmul(
                  PBT[4][0:1, 0:DO], sapb[:, kt_:kt_ + 1],
                  dwTb[:, kt_ * DO:(kt_ + 1) * DO],
                  start=(kt_ == 0), stop=(kt_ == KD - 1)))(kt),
              reads=[DWB, SAPB], writes_pre=([PB[4]] if kt == 0 else []),
              writes=([PB[4]] if kt == KD - 1 else []), inc=(kt == KD - 1))
        RS1, RS2 = B_("rowscr"), B_("rowscr2")
        E("scalar", lambda e: e.activation(out=rowscr[0:1, 0:DO],
                                           in_=PBT[4][0:1, 0:DO], func=AF.Identity),
          reads=[PB[4]], writes=[RS1])
        KC = B_("kcol")
        for ot in range(KO):
            pe_drain()
            E("tensor",
              (lambda ot_: lambda e: e.matmul(
                  PBT[4][:, 64 + ot_:64 + ot_ + 1],
                  rowscr[0:1, ot_ * 128:(ot_ + 1) * 128], ones11,
                  start=True, stop=True))(ot),
              reads=[RS1, VB], writes_pre=([PB[4]] if ot == 0 else []),
              writes=([PB[4]] if ot == KO - 1 else []), inc=True)
        E("vector", lambda e: e.tensor_copy(out=kcol[:], in_=PBT[4][:, 64:64 + KO]),
          reads=[PB[4]], writes=[KC])
        CB1 = B_("cb1p")
        for ot in range(KO):
            E("vector",
              (lambda ot_: lambda e: e.tensor_tensor(
                  out=scr8[:, 8 + ot_:9 + ot_], in0=vcol("cw", 0 * 8 + 0 * 2 + ot_),
                  in1=vcol("cw", 0 * 8 + 1 * 2 + ot_), op=ALU.add))(ot),
              reads=[VB], writes=[SCR])
            E("vector",
              (lambda ot_: lambda e: e.tensor_tensor(
                  out=scr8[:, 10 + ot_:11 + ot_], in0=vcol("cw", 0 * 8 + 2 * 2 + ot_),
                  in1=vcol("cw", 0 * 8 + 3 * 2 + ot_), op=ALU.add))(ot),
              reads=[VB], writes=[SCR])
            E("vector",
              (lambda ot_: lambda e: e.tensor_tensor(
                  out=scr8[:, 8 + ot_:9 + ot_], in0=scr8[:, 8 + ot_:9 + ot_],
                  in1=scr8[:, 10 + ot_:11 + ot_], op=ALU.add))(ot),
              writes=[SCR])
            E("vector",
              (lambda ot_: lambda e: e.tensor_tensor(
                  out=scr8[:, 8 + ot_:9 + ot_], in0=scr8[:, 8 + ot_:9 + ot_],
                  in1=kcol[:, ot_:ot_ + 1], op=ALU.mult))(ot),
              reads=[KC], writes=[SCR])
            E("vector",
              (lambda ot_: lambda e: e.tensor_tensor(
                  out=cb1p[:, ot_:ot_ + 1], in0=scr8[:, 8 + ot_:9 + ot_],
                  in1=vcol("cbv", 0 * 2 + ot_), op=ALU.add))(ot),
              reads=[SCR, VB], writes=[CB1])
        tap("cb1p", cb1p[:], [128, KO], reads=[CB1])
        tap("kcolT", kcol[:], [128, KO], reads=[KC])
        tap("m8lsT", m8ls[:], [128, 6], reads=[M8])
        tap("moddT", modd[:], [128, 32], reads=[MODD])
        TL, ST = B_("tails"), B_("state")
        E("vector", lambda e: e.memset(tails[:], 0.0), writes=[TL])
        for ot in range(KO):
            E("vector",
              (lambda ot_: lambda e: e.tensor_scalar(
                  out=tails[:, 0, ot_, :], in0=tails[:, 0, ot_, :],
                  scalar1=kcol[:, ot_:ot_ + 1], scalar2=None, op0=ALU.subtract))(ot),
              reads=[KC], writes=[TL])
        E("vector", lambda e: e.memset(state[:], 0.0), writes=[ST])

        UPS, UPB = B_("upstg"), B_("upTb")
        dma(up_stg, upT3, writes=[UPS], writes_pre=[DWS])
        E("scalar", lambda e: e.copy(
            out=upTb[:].rearrange("p (a n) -> p a n", a=KO), in_=up_stg),
          reads=[UPS], writes=[UPB])

        ATS = B_("atstg")
        WRB_, WIB_, OWB_ = B_("wrTb"), B_("wiTb"), B_("owTb")
        first_at = True
        for (src3, dstt, dbuf) in ((wrT3, wrTb, WRB_), (wiT3, wiTb, WIB_),
                                   (owT3, owTb, OWB_)):
            for i in range(NL):
                dma(at_stg, src3[i], writes=[ATS],
                    writes_pre=([stg[0]] if first_at else []))
                first_at = False
                E("scalar",
                  (lambda dstt_, i_: lambda e: e.copy(
                      out=dstt_[:, i_ * KO * DO:(i_ + 1) * KO * DO]
                      .rearrange("p (a n) -> p a n", a=KO), in_=at_stg))(dstt, i),
                  reads=[ATS],
                  writes=([dbuf] if i == NL - 1 else []),
                  writes_pre=([dbuf] if i == 0 else []),
                  inc=True)

        RWS, RWP = B_("rwTs"), B_("rwTp")
        rw_stg = rwTs[:].rearrange("p (a n) -> p a n", a=KD)
        dma(rw_stg, rwT3, writes=[RWS])
        pe_drain()
        for kt in range(KD):
            E("tensor",
              (lambda kt_: lambda e: e.matmul(
                  PBT[3][0:1, 0:NB], mod48[:, 24 + kt_:25 + kt_],
                  rwTs[:, kt_ * NB:(kt_ + 1) * NB],
                  start=(kt_ == 0), stop=(kt_ == KD - 1)))(kt),
              reads=[RWS, MODP], writes_pre=([PB[3]] if kt == 0 else []),
              writes=([PB[3]] if kt == KD - 1 else []), inc=(kt == KD - 1))
        E("scalar", lambda e: e.activation(out=rowscr[0:1, 0:NB],
                                           in_=PBT[3][0:1, 0:NB], func=AF.Identity),
          reads=[PB[3]], writes=[RS1])
        pe_drain()
        E("tensor", lambda e: e.matmul(
            PBT[4][0:1, 0:16], vec[0:16, VEC_COLS["rbc"]:VEC_COLS["rbc"] + 1],
            ident[0:16, 0:16], start=True, stop=True),
          reads=[VB], writes_pre=[PB[4]], writes=[PB[4]])
        E("scalar", lambda e: e.activation(out=rowscr2[0:1, 0:16],
                                           in_=PBT[4][0:1, 0:16], func=AF.Identity),
          reads=[PB[4]], writes=[RS2])
        E("vector", lambda e: e.tensor_tensor(out=rowscr[0:1, 0:16],
                                              in0=rowscr[0:1, 0:16],
                                              in1=rowscr2[0:1, 0:16], op=ALU.add),
          reads=[RS2], writes=[RS1])
        pe_drain()
        E("tensor", lambda e: e.matmul(PBT[4][0:16, 16:17], rowscr[0:1, 0:16],
                                       ones11, start=True, stop=True),
          reads=[RS1, VB], writes_pre=[PB[4]], writes=[PB[4]])
        KB16 = B_("kb16")
        E("vector", lambda e: e.tensor_copy(out=kb16[:], in_=PBT[4][0:16, 16:17]),
          reads=[PB[4]], writes=[KB16])
        for kt in range(KD):
            E("vector",
              (lambda kt_: lambda e: e.tensor_scalar(
                  out=rwTp[:, kt_ * NB:(kt_ + 1) * NB],
                  in0=rwTs[:, kt_ * NB:(kt_ + 1) * NB],
                  scalar1=modd[:, 8 + kt_:9 + kt_], scalar2=None, op0=ALU.mult))(kt),
              reads=[RWS, MODD], writes=([RWP] if kt == KD - 1 else []),
              writes_pre=([RWP] if kt == 0 else []), inc=(kt == KD - 1))
        pe_drain()
        for kt in range(KD):
            E("tensor",
              (lambda kt_: lambda e: e.matmul(
                  PBT[3][0:1, 32:32 + NB], vcol("ones"),
                  rwTp[:, kt_ * NB:(kt_ + 1) * NB],
                  start=(kt_ == 0), stop=(kt_ == KD - 1)))(kt),
              reads=[RWP, VB], writes_pre=([PB[3]] if kt == 0 else []),
              writes=([PB[3]] if kt == KD - 1 else []), inc=(kt == KD - 1))
        E("scalar", lambda e: e.activation(out=rowscr[0:1, 0:NB],
                                           in_=PBT[3][0:1, 32:32 + NB],
                                           func=AF.Identity, scale=-1.0),
          reads=[PB[3]], writes=[RS1])
        pe_drain()
        E("tensor", lambda e: e.matmul(PBT[4][0:16, 18:19], rowscr[0:1, 0:16],
                                       ones11, start=True, stop=True),
          reads=[RS1, VB], writes_pre=[PB[4]], writes=[PB[4]])
        NJR = B_("njr16")
        E("vector", lambda e: e.tensor_copy(out=njr16[:], in_=PBT[4][0:16, 18:19]),
          reads=[PB[4]], writes=[NJR])

        ORF = B_("onesrowf")
        pe_drain()
        E("tensor", lambda e: e.matmul(PBT[4][0:1, 128:256], vcol("ones"), ident,
                                       start=True, stop=True),
          reads=[VB], writes_pre=[PB[4]], writes=[PB[4]])
        E("scalar", lambda e: e.activation(out=onesrowf[:], in_=PBT[4][0:1, 128:256],
                                           func=AF.Identity),
          reads=[PB[4]], writes=[ORF])
        B2R = B_("b2row")
        for g in range(2):
            for k in range(4):
                dt = g * 4 + k
                pe_drain()
                E("tensor",
                  (lambda dt_, k_: lambda e: e.matmul(
                      PBT[4][0:1, k_ * 128:(k_ + 1) * 128], vcol("b2c", dt_),
                      ident, start=True, stop=True))(dt, k),
                  reads=[VB], writes_pre=([PB[4]] if k == 0 else []),
                  writes=([PB[4]] if k == 3 else []), inc=True)
            E("scalar",
              (lambda g_: lambda e: e.activation(
                  out=b2row[0:1, g_ * 512:(g_ + 1) * 512],
                  in_=PBT[4][0:1, 0:512], func=AF.Identity))(g),
              reads=[PB[4]],
              writes=([B2R] if g == 1 else []),
              writes_pre=([B2R] if g == 0 else []), inc=True)

        kb.barrier()

        GA, GB_b, GC = B_("genA"), B_("genB"), B_("genC")
        W1S, W2S = B_("w1s"), B_("w2s")
        KB1B = B_("KB1")
        wblk = genC[:].rearrange("p (a n) -> p a n", a=KD)
        wblkb = genB[:].bitcast(BF16).rearrange("p (a n) -> p a n", a=KD)
        w2stg = genA[:]
        w2stgb = genC[:, 0:512].bitcast(BF16)

        def p0b_w1(j):
            dma(wblk, w1T3[:, :, j * 128:(j + 1) * 128], writes=[GC], q="qpb")
            for kt in range(KD):
                E("vector",
                  (lambda kt_: lambda e: e.tensor_scalar(
                      out=wblkb[:, kt_, :], in0=wblk[:, kt_, :],
                      scalar1=modd[:, 8 + kt_:9 + kt_], scalar2=None,
                      op0=ALU.mult))(kt),
                  reads=[GC, MODD],
                  writes=([GB_b] if kt == KD - 1 else []),
                  writes_pre=([GB_b] if kt == 0 else []),
                  inc=(kt == KD - 1))
            pe_drain()
            for kt in range(KD):
                E("tensor",
                  (lambda kt_: lambda e: e.matmul(
                      PBT[3][0:1, 128:256], sbpb[:, kt_:kt_ + 1], wblkb[:, kt_, :],
                      start=(kt_ == 0), stop=(kt_ == KD - 1)))(kt),
                  reads=[GB_b, SBPB], writes_pre=([PB[3]] if kt == 0 else []),
                  writes=([PB[3]] if kt == KD - 1 else []), inc=(kt == KD - 1))
            E("scalar",
              (lambda j_: lambda e: e.activation(
                  out=rowscr2[0:1, (j_ % 4) * 128:(j_ % 4) * 128 + 128],
                  in_=PBT[3][0:1, 128:256], func=AF.Identity))(j),
              reads=[PB[3]], writes=[RS2])
            if j % 4 == 3:
                for k in range(4):
                    pe_drain()
                    E("tensor",
                      (lambda k_: lambda e: e.matmul(
                          PBT[4][:, 300 + k_:301 + k_],
                          rowscr2[0:1, k_ * 128:(k_ + 1) * 128], ones11,
                          start=True, stop=True))(k),
                      reads=[RS2, VB], writes_pre=([PB[4]] if k == 0 else []),
                      writes=([PB[4]] if k == 3 else []), inc=True)
                E("vector",
                  (lambda j_: lambda e: e.tensor_copy(
                      out=KB1t[:, j_ - 3:j_ + 1], in_=PBT[4][:, 300:304]))(j),
                  reads=[PB[4]],
                  writes=([KB1B] if j == FT - 1 else []),
                  writes_pre=([KB1B] if j == 3 else []), inc=True)
            dma(w1s[:, j * 1024:(j + 1) * 1024],
                wblkb.rearrange("p a n -> p (a n)"), reads=[GB_b],
                writes=([W1S] if j == FT - 1 else []),
                writes_pre=([W1S] if j == 0 else []), q="qs1")

        def p0b_w2(ot):
            dma(w2stg, w2T3[:, ot, :], writes=[GA], q="qpb2")
            E("scalar", lambda e: e.copy(out=w2stgb, in_=w2stg),
              reads=[GA], writes=[GC])
            dma(w2s[:, ot * 1024:(ot + 1) * 1024], w2stgb, reads=[GC],
                writes=([W2S] if ot == FT - 1 else []),
                writes_pre=([W2S] if ot == 0 else []), q="qs2")

        p0b_jobs = [lambda j=j: p0b_w1(j) for j in range(FT)] + \
                   [lambda o=o: p0b_w2(o) for o in range(FT)]
        p0b_state = {"i": 0, "done": False}

        def p0b_fill(n):
            i = p0b_state["i"]
            for _ in range(n):
                if i >= len(p0b_jobs):
                    break
                p0b_jobs[i]()
                i += 1
            p0b_state["i"] = i
            if i >= len(p0b_jobs) and not p0b_state["done"]:
                p0b_state["done"] = True
                E("vector", lambda e: e.tensor_tensor(out=KB1t[:], in0=KB1t[:],
                                                      in1=vcols("b1c", 0, 32),
                                                      op=ALU.add),
                  reads=[VB], writes=[KB1B])

        XRAB, XRBB = B_("XRA"), B_("XRB")
        LABs = [B_("LA0"), B_("LA1")]
        AABs = [B_("AA0"), B_("AA1")]
        T1Bs = [B_("T10"), B_("T11")]
        HSBBs = [B_("HSB0"), B_("HSB1")]
        RSTB = B_("rstdB")
        XCBB = B_("XCB")
        RBS = [B_("Rb0"), B_("Rb1")]
        IBS = [B_("Ib0"), B_("Ib1")]
        XBRS = [B_("xbr0"), B_("xbr1")]
        XSQS = [B_("xsq0"), B_("xsq1")]
        OSQB = B_("OSQ")
        X2B = B_("X2")
        ROWB = B_("rowbf")

        def ln1_down(q, s):
            t0 = s * SC
            for kt in range(KD):
                sl = kt % 2
                E("scalar",
                  (lambda kt_, sl_, t0_: lambda e: e.copy(
                      out=xbr[:, sl_, :], in_=XBt[:, kt_, t0_:t0_ + SC]))(kt, sl, t0),
                  reads=[XBS[s]], writes=[XBRS[sl]],
                  writes_pre=([OSQB] if kt < 2 else []))
                E("scalar",
                  (lambda kt_, sl_, t0_: lambda e: e.activation(
                      out=xsqr[:, sl_, :], in_=XBt[:, kt_, t0_:t0_ + SC],
                      func=AF.Square))(kt, sl, t0),
                  reads=[XBS[s]], writes=[XSQS[sl]])
                E("tensor",
                  (lambda kt_, sl_: lambda e: e.matmul(
                      PBT[3][0:1, 0:SC], onescol_bf, xbr[:, sl_, :],
                      start=(kt_ == 0), stop=(kt_ == KD - 1)))(kt, sl),
                  reads=[XBRS[sl], CB], writes_pre=([PB[3]] if kt == 0 else []),
                  writes=([PB[3]] if kt == KD - 1 else []), inc=True)
                E("tensor",
                  (lambda kt_, sl_: lambda e: e.matmul(
                      PBT[7][0:1, 0:SC], onescol_bf, xsqr[:, sl_, :],
                      start=(kt_ == 0), stop=(kt_ == KD - 1)))(kt, sl),
                  reads=[XSQS[sl], CB], writes_pre=([PB[7]] if kt == 0 else []),
                  writes=([PB[7]] if kt == KD - 1 else []), inc=True)
                for ot in range(KO):
                    E("tensor",
                      (lambda kt_, ot_, sl_: lambda e: e.matmul(
                          PBT[ot_][:, 0:SC],
                          dwTb[:, (kt_ * KO + ot_) * 128:(kt_ * KO + ot_ + 1) * 128],
                          xbr[:, sl_, :],
                          start=(kt_ == 0), stop=False))(kt, ot, sl),
                      reads=[XBRS[sl], DWB],
                      writes_pre=([PB[ot]] if kt == 0 else []),
                      inc=True)
            E("vector", lambda e: e.tensor_scalar(out=rowscr[:], in0=PBT[3][0:1, 0:SC],
                                                  scalar1=1.0 / D, scalar2=None,
                                                  op0=ALU.mult),
              reads=[PB[3]], writes=[RS1])
            E("scalar", lambda e: e.activation(out=rowscr2[:], in_=rowscr[:],
                                               func=AF.Square),
              reads=[RS1], writes=[RS2])
            E("vector", lambda e: e.scalar_tensor_tensor(
                out=rowscr2[:], in0=PBT[7][0:1, 0:SC], scalar=1.0 / D,
                in1=rowscr2[:], op0=ALU.mult, op1=ALU.subtract),
              reads=[PB[7]], writes=[RS2])
            if q == 0 and s == 0:
                tap("var00row", rowscr2[:], [1, SC], reads=[RS2])
            E("scalar", lambda e: e.activation(out=rowscr2[:], in_=rowscr2[:],
                                               func=AF.Ln, bias=vrow0("epsln")),
              reads=[VB], writes=[RS2])
            if q == 0 and s == 0:
                tap("lnv00row", rowscr2[:], [1, SC], reads=[RS2])
            E("scalar", lambda e: e.activation(out=rowscr2[:], in_=rowscr2[:],
                                               func=AF.Exp, scale=-0.5),
              writes=[RS2])
            if q == 0 and s == 0:
                tap("m00row", rowscr[:], [1, SC], reads=[RS1])
                tap("rstdrow00", rowscr2[:], [1, SC], reads=[RS2])
                tap("ones00", onesrowf[:], [1, 128], reads=[ORF])
            E("vector", lambda e: e.tensor_copy(out=rowbf[:], in_=rowscr[:]),
              reads=[RS1], writes=[ROWB])
            for ot in range(KO):
                E("tensor",
                  (lambda ot_: lambda e: e.matmul(
                      PBT[ot_][:, 0:SC], negJrow[0:1, ot_ * 128:(ot_ + 1) * 128],
                      rowbf[:], start=False, stop=True))(ot),
                  reads=[ROWB, NJ], writes=[PB[ot]], inc=True)
            pe_drain()
            E("tensor", lambda e: e.matmul(PBT[2][:, 0:SC], onesrowf[:], rowscr2[:],
                                           start=True, stop=True),
              reads=[RS2, ORF], writes_pre=[PB[2]], writes=[PB[2]])
            E("scalar", lambda e: e.activation(out=rstdB[:], in_=PBT[2][:, 0:SC],
                                               func=AF.Identity),
              reads=[PB[2]], writes=[RSTB])
            for ot in range(KO):
                E("vector",
                  (lambda ot_, t0_: lambda e: e.tensor_tensor(
                      out=XRA[:, ot_, 3 + t0_:3 + t0_ + SC], in0=PBT[ot_][:, 0:SC],
                      in1=rstdB[:], op=ALU.mult))(ot, t0),
                  reads=[PB[ot], RSTB],
                  writes=([XRAB] if ot == KO - 1 else []),
                  writes_pre=([XRAB] if ot == 0 else []), inc=True)
            if q == 0 and s == 0:
                tap("xr00", XRA[:, 0, 3:3 + SC], [128, SC], reads=[XRAB])
                tap("rstd00", rstdB[:], [128, SC], reads=[RSTB])

        def layer(q, i):
            IN, INB = (XRA, XRAB) if i % 2 == 0 else (XRB, XRBB)
            OUT, OUTB = (XRB, XRBB) if i % 2 == 0 else (XRA, XRAB)
            wr_off = lambda kt, ot: ((i * KO + kt) * KO + ot) * 128
            E("vector",
              (lambda i_, IN_: lambda e: e.tensor_copy(
                  out=IN_[:, :, 0:3], in_=tails[:, i_, :, :]))(i, IN),
              reads=[TL], writes=[INB])
            E("scalar", lambda e: preload(e, SET_SIG), inc=False)
            for s in range(2):
                t0 = s * SC
                T1v, T1B = T1s[s], T1Bs[s]
                for ot in range(KO):
                    cbap = cb1p[:, ot:ot + 1] if i == 0 else vcol("cbv", i * 2 + ot)
                    E("vector",
                      (lambda i_, ot_, t0_, cbap_, IN_, T1v_: lambda e:
                       e.tensor_scalar(
                          out=T1v_[:, ot_, :], in0=IN_[:, ot_, t0_ + 0:t0_ + SC],
                          scalar1=vcol("cw", i_ * 8 + 0 * 2 + ot_), scalar2=cbap_,
                          op0=ALU.mult, op1=ALU.add))(i, ot, t0, cbap, IN, T1v),
                      reads=[INB, VB, CB1], writes=[T1B])
                    for k in (1, 2, 3):
                        last = (k == 3)
                        E("vector",
                          (lambda i_, ot_, t0_, k_, IN_, last_, T1v_: lambda e:
                           e.scalar_tensor_tensor(
                              out=(XCB[:, ot_, t0_:t0_ + SC] if last_
                                   else T1v_[:, ot_, :]),
                              in0=IN_[:, ot_, t0_ + k_:t0_ + k_ + SC],
                              scalar=vcol("cw", i_ * 8 + k_ * 2 + ot_),
                              in1=T1v_[:, ot_, :], op0=ALU.mult,
                              op1=ALU.add))(i, ot, t0, k, IN, last, T1v),
                          reads=[INB, VB, T1B],
                          writes=([T1B, XCBB] if last else [T1B]))
                for (wt, wbuf, dst, dstb, bname, bank0) in (
                        (wrTb, WRB_, Rb, RBS[s], "wrb", 4),
                        (wiTb, WIB_, Ib, IBS[s], "wib", 6)):
                    for ot in range(KO):
                        bank = bank0 + ot
                        for kt in range(KO):
                            E("tensor",
                              (lambda wt_, kt_, ot_, t0_, bank_: lambda e: e.matmul(
                                  PBT[bank_][:, 0:SC],
                                  wt_[:, wr_off(kt_, ot_):wr_off(kt_, ot_) + 128],
                                  XCB[:, kt_, t0_:t0_ + SC],
                                  start=(kt_ == 0), stop=(kt_ == KO - 1)))(
                                      wt, kt, ot, t0, bank),
                              reads=[XCBB, wbuf],
                              writes_pre=([PB[bank]] if kt == 0 else []),
                              writes=([PB[bank]] if kt == KO - 1 else []),
                              inc=(kt == KO - 1))
                        E("scalar",
                          (lambda dst_, i_, ot_, t0_, bank_, bn_: lambda e:
                           e.activation(
                              out=dst_[:, ot_, t0_:t0_ + SC],
                              in_=PBT[bank_][:, 0:SC], func=AF.Sigmoid,
                              bias=vcol(bn_, i_ * 2 + ot_)))(
                                  dst, i, ot, t0, bank, bname),
                          reads=[PB[bank], VB],
                          writes=([dstb] if ot == KO - 1 else []),
                          writes_pre=([dstb] if ot == 0 else []),
                          inc=True)
            E("scalar", lambda e: preload(e, SET_NLE), inc=False)
            for s in range(2):
                t0 = s * SC
                LAv, LAB = LAs[s], LABs[s]
                AAv, AAB = AAs[s], AABs[s]
                T1v, T1B = T1s[s], T1Bs[s]
                HSB, HSBB = HSBs[s], HSBBs[s]
                E("vector",
                  (lambda t0_, LAv_: lambda e: e.tensor_tensor(
                      out=LAv_[:], in0=Ib[:, :, t0_:t0_ + SC],
                      in1=XCB[:, :, t0_:t0_ + SC], op=ALU.mult))(t0, LAv),
                  reads=[IBS[s], XCBB], writes=[LAB])
                for ot in range(KO):
                    E("scalar",
                      (lambda i_, ot_, t0_, AAv_: lambda e: e.activation(
                          out=AAv_[:, ot_, :], in_=Rb[:, ot_, t0_:t0_ + SC],
                          func=AF.Exp,
                          scale=m8ls[:, i_ * 2 + ot_:i_ * 2 + ot_ + 1]))(
                              i, ot, t0, AAv),
                      reads=[RBS[s]], sreads=[M8],
                      writes=([AAB] if ot == KO - 1 else []),
                      writes_pre=([AAB] if ot == 0 else []), inc=True)
                E("scalar",
                  (lambda AAv_, T1v_: lambda e: e.activation(
                      out=T1v_[:], in_=AAv_[:], func=AF.Square))(AAv, T1v),
                  reads=[AAB], writes=[T1B])
                E("vector",
                  (lambda T1v_: lambda e: e.tensor_scalar(
                      out=T1v_[:], in0=T1v_[:], scalar1=-1.0, scalar2=1.0,
                      op0=ALU.mult, op1=ALU.add))(T1v),
                  writes=[T1B])
                E("vector",
                  (lambda T1v_: lambda e: e.tensor_scalar(
                      out=T1v_[:], in0=T1v_[:], scalar1=1e-38, scalar2=None,
                      op0=ALU.max))(T1v),
                  writes=[T1B])
                E("scalar",
                  (lambda T1v_: lambda e: e.activation(
                      out=T1v_[:], in_=T1v_[:], func=AF.Ln))(T1v),
                  reads=[VB], writes=[T1B])
                E("scalar",
                  (lambda T1v_: lambda e: e.activation(
                      out=T1v_[:], in_=T1v_[:], func=AF.Exp, scale=0.5))(T1v),
                  writes=[T1B])
                if q == 0 and i == 0 and s == 0:
                    tap("a00", AAs[0][:, 0, :], [128, SC], reads=[AABs[0]])
                    tap("gate00", T1s[0][:, 0, :], [128, SC],
                        reads=[T1Bs[0]])
                E("vector",
                  (lambda LAv_, T1v_: lambda e: e.tensor_tensor(
                      out=LAv_[:], in0=LAv_[:], in1=T1v_[:], op=ALU.mult))(
                          LAv, T1v),
                  reads=[T1B], writes=[LAB])
                if q == 0 and i == 0 and s == 0:
                    tap("b00", LAs[0][:, 0, :], [128, SC], BF16, reads=[LABs[0]])
                for ot in range(KO):
                    E("vector",
                      (lambda i_, ot_, AAv_, LAv_, HSB_: lambda e:
                       e.tensor_tensor_scan(
                          out=HSB_[:, ot_, :], data0=AAv_[:, ot_, :],
                          data1=LAv_[:, ot_, :],
                          initial=state[:, i_ * 2 + ot_:i_ * 2 + ot_ + 1],
                          op0=ALU.mult, op1=ALU.add))(i, ot, AAv, LAv, HSB),
                      reads=[AAB, LAB], sreads=[ST],
                      writes=([HSBB] if ot == KO - 1 else []),
                      writes_pre=([HSBB] if ot == 0 else []), inc=True)
                    E("vector",
                      (lambda i_, ot_, HSB_: lambda e: e.tensor_copy(
                          out=state[:, i_ * 2 + ot_:i_ * 2 + ot_ + 1],
                          in_=HSB_[:, ot_, SC - 1:SC]))(i, ot, HSB),
                      writes=[ST])
                if q == 0 and i == 0 and s == 0:
                    tap("hs00", HSBs[0][:, 0, :], [128, SC], BF16,
                        reads=[HSBBs[0]])
                    tap("r00", Rb[:, 0, 0:SC], [128, SC], BF16, reads=[RBS[0]])
                    tap("xcb00", XCB[:, 0, 0:SC], [128, SC], BF16, reads=[XCBB])
                for ot in range(KO):
                    bank = 4 + ot
                    for kt in range(KO):
                        E("tensor",
                          (lambda kt_, ot_, bank_, HSB_: lambda e: e.matmul(
                              PBT[bank_][:, 0:SC],
                              owTb[:, wr_off(kt_, ot_):wr_off(kt_, ot_) + 128],
                              HSB_[:, kt_, :],
                              start=(kt_ == 0), stop=(kt_ == KO - 1)))(
                                  kt, ot, bank, HSB),
                          reads=[HSBB, OWB_],
                          writes_pre=([PB[bank]] if kt == 0 else []),
                          writes=([PB[bank]] if kt == KO - 1 else []),
                          inc=(kt == KO - 1))
                    E("scalar",
                      (lambda ot_, bank_: lambda e: e.activation(
                          out=OSQ[:, ot_, :], in_=PBT[bank_][:, 0:SC],
                          func=AF.Square))(ot, 4 + ot),
                      reads=[PB[4 + ot]],
                      writes=([OSQB] if ot == KO - 1 else []),
                      writes_pre=(([OSQB] + XBRS + XSQS) if ot == 0 else []),
                      inc=True)
                    E("scalar",
                      (lambda ot_, bank_, AAv_: lambda e: e.activation(
                          out=AAv_[:, ot_, :], in_=PBT[bank_][:, 0:SC],
                          func=AF.Identity))(ot, 4 + ot, AAv),
                      reads=[PB[4 + ot]],
                      writes=([AAB] if ot == KO - 1 else []),
                      writes_pre=([AAB] if ot == 0 else []), inc=True)
                for ot in range(KO):
                    E("tensor",
                      (lambda ot_: lambda e: e.matmul(
                          PBT[3][0:1, 0:SC], onescol_bf, OSQ[:, ot_, :],
                          start=(ot_ == 0), stop=(ot_ == KO - 1)))(ot),
                      reads=[OSQB, CB], writes_pre=([PB[3]] if ot == 0 else []),
                      writes=([PB[3]] if ot == KO - 1 else []), inc=(ot == KO - 1))
                E("vector", lambda e: e.tensor_scalar(out=rowscr[:],
                                                      in0=PBT[3][0:1, 0:SC],
                                                      scalar1=1.0 / DO, scalar2=None,
                                                      op0=ALU.mult),
                  reads=[PB[3]], writes=[RS1])
                E("scalar", lambda e: e.activation(out=rowscr[:], in_=rowscr[:],
                                                   func=AF.Ln, bias=vrow0("epsrms")),
                  reads=[RS1, VB], writes=[RS1])
                E("scalar", lambda e: e.activation(out=rowscr[:], in_=rowscr[:],
                                                   func=AF.Exp, scale=-0.5),
                  writes=[RS1])
                pe_drain()
                E("tensor", lambda e: e.matmul(PBT[2][:, 0:SC], onesrowf[:],
                                               rowscr[:], start=True, stop=True),
                  reads=[RS1, ORF], writes_pre=[PB[2]], writes=[PB[2]])
                for ot in range(KO):
                    E("vector",
                      (lambda i_, ot_, t0_, OUT_, AAv_: lambda e:
                       e.scalar_tensor_tensor(
                          out=OUT_[:, ot_, 3 + t0_:3 + t0_ + SC],
                          in0=AAv_[:, ot_, :],
                          scalar=vcol("nwv", i_ * 2 + ot_), in1=PBT[2][:, 0:SC],
                          op0=ALU.mult, op1=ALU.mult))(i, ot, t0, OUT, AAv),
                      reads=[AAB, VB, PB[2]],
                      writes=([OUTB] if ot == KO - 1 else []),
                      writes_pre=([OUTB] if ot == 0 else []), inc=True)
            if q == 0:
                tap(f"xn0L{i}", OUT[:, 0, 3:3 + SC], [128, SC], reads=[OUTB])
            E("vector",
              (lambda i_, IN_: lambda e: e.tensor_copy(
                  out=tails[:, i_, :, :], in_=IN_[:, :, QT:QT + 3]))(i, IN),
              reads=[INB], writes=[TL])

        def up_resid(q, s):
            t0 = s * SC
            h = q % 2
            E("scalar",
              (lambda t0_: lambda e: e.copy(
                  out=XCB[:, :, t0_:t0_ + SC],
                  in_=XRB[:, :, 3 + t0_:3 + t0_ + SC]))(t0),
              reads=[XRBB], writes=[XCBB])
            for dt in range(KD):
                bank = dt % 2
                for kt in range(KO):
                    E("tensor",
                      (lambda kt_, dt_, t0_, bank_: lambda e: e.matmul(
                          PBT[bank_][:, 0:SC],
                          upTb[:, kt_ * D + dt_ * 128:kt_ * D + (dt_ + 1) * 128],
                          XCB[:, kt_, t0_:t0_ + SC],
                          start=(kt_ == 0), stop=(kt_ == KO - 1)))(kt, dt, t0, bank),
                      reads=[XCBB, UPB],
                      writes_pre=([PB[bank]] if kt == 0 else []),
                      writes=([PB[bank]] if kt == KO - 1 else []),
                      inc=(kt == KO - 1))
                E("vector",
                  (lambda dt_, t0_, h_, bank_: lambda e: e.scalar_tensor_tensor(
                      out=X2t[:, dt_, h_ * QT + t0_:h_ * QT + t0_ + SC],
                      in0=PBT[bank_][:, 0:SC], scalar=modd[:, 16 + dt_:17 + dt_],
                      in1=XBt[:, dt_, t0_:t0_ + SC], op0=ALU.mult, op1=ALU.add))(
                          dt, t0, h, dt % 2),
                  reads=[PB[dt % 2], XBS[s]], sreads=[MODD],
                  writes=([X2B] if dt == KD - 1 else []),
                  writes_pre=([X2B] if dt == 0 else []), inc=True)
            if q == 0 and s == 0:
                tap("x2q0", X2t[:, 0, 0:SC], [128, SC], reads=[X2B])

        def s1_quarter(q, filler=None):
            for s in range(2):
                ln1_down(q, s)
            if filler:
                filler(3)
            for i in range(NL):
                layer(q, i)
                if filler:
                    filler(3)
            for s in range(2):
                up_resid(q, s)
                if q + 1 < NQ:
                    dma(XBt[:, :, s * SC:(s + 1) * SC],
                        x3[:, :, (q + 1) * QT + s * SC:(q + 1) * QT + (s + 1) * SC],
                        writes=[XBS[s]], q="qx0" if s == 0 else "qx1")

        X2NB, HHB = B_("x2n"), B_("hh")
        U32B = B_("u32a")
        X2BS = [B_("x2br0"), B_("x2br1")]
        X2SQS = [B_("x2sq0"), B_("x2sq1")]
        Z16B, ZTMB, MM1B, TSCB, MEMB = (B_("z16"), B_("ztm"), B_("mm1"),
                                        B_("tsc"), B_("mask_em"))
        MM1BB = B_("mm1b")
        w1bufb = genA[:].bitcast(BF16).rearrange("p (a n) -> p a n", a=2)
        w2bufb = genC[:].bitcast(BF16).rearrange("p (a n) -> p a n", a=2)
        W1SL = [B_("w1sl0"), B_("w1sl1")]
        W2SL = [B_("w2sl0"), B_("w2sl1")]

        def s2_chunk(q, c, first=False):
            h = q % 2
            t0 = h * QT + c * SC
            gtok = q * QT + c * SC
            for kt in range(KD):
                sl = kt % 2
                E("scalar",
                  (lambda kt_, sl_, t0_: lambda e: e.copy(
                      out=x2br[:, sl_, :], in_=X2t[:, kt_, t0_:t0_ + SC]))(kt, sl, t0),
                  reads=[X2B], writes=[X2BS[sl]])
                E("scalar",
                  (lambda kt_, sl_, t0_: lambda e: e.activation(
                      out=x2sqr[:, sl_, :], in_=X2t[:, kt_, t0_:t0_ + SC],
                      func=AF.Square))(kt, sl, t0),
                  reads=[X2B], writes=[X2SQS[sl]])
                E("tensor",
                  (lambda kt_, sl_: lambda e: e.matmul(
                      PBT[3][0:1, 0:SC], onescol_bf, x2br[:, sl_, :],
                      start=(kt_ == 0), stop=(kt_ == KD - 1)))(kt, sl),
                  reads=[X2BS[sl], CB], writes_pre=([PB[3]] if kt == 0 else []),
                  writes=([PB[3]] if kt == KD - 1 else []), inc=True)
                E("tensor",
                  (lambda kt_, sl_: lambda e: e.matmul(
                      PBT[7][0:1, 0:SC], onescol_bf, x2sqr[:, sl_, :],
                      start=(kt_ == 0), stop=(kt_ == KD - 1)))(kt, sl),
                  reads=[X2SQS[sl], CB], writes_pre=([PB[7]] if kt == 0 else []),
                  writes=([PB[7]] if kt == KD - 1 else []), inc=True)
                E("tensor",
                  (lambda kt_, t0_: lambda e: e.matmul(
                      PBT[6][0:16, 0:SC], rwTp[:, kt_ * NB:(kt_ + 1) * NB],
                      X2t[:, kt_, t0_:t0_ + SC],
                      start=(kt_ == 0), stop=(kt_ == KD - 1)))(kt, t0),
                  reads=[X2B, RWP], writes_pre=([PB[6]] if kt == 0 else []),
                  writes=([PB[6]] if kt == KD - 1 else []), inc=True)
            E("vector", lambda e: e.tensor_scalar(out=rowscr[:], in0=PBT[3][0:1, 0:SC],
                                                  scalar1=1.0 / D, scalar2=None,
                                                  op0=ALU.mult),
              reads=[PB[3]], writes=[RS1])
            E("scalar", lambda e: e.activation(out=rowscr2[:], in_=rowscr[:],
                                               func=AF.Square),
              reads=[RS1], writes=[RS2])
            E("vector", lambda e: e.scalar_tensor_tensor(
                out=rowscr2[:], in0=PBT[7][0:1, 0:SC], scalar=1.0 / D,
                in1=rowscr2[:], op0=ALU.mult, op1=ALU.subtract),
              reads=[PB[7]], writes=[RS2])
            E("scalar", lambda e: e.activation(out=rowscr2[:], in_=rowscr2[:],
                                               func=AF.Ln, bias=vrow0("epsln")),
              reads=[VB], writes=[RS2])
            E("scalar", lambda e: e.activation(out=rowscr2[:], in_=rowscr2[:],
                                               func=AF.Exp, scale=-0.5),
              writes=[RS2])
            E("vector", lambda e: e.tensor_tensor(out=rowscr[:], in0=rowscr[:],
                                                  in1=rowscr2[:], op=ALU.mult),
              reads=[RS2], writes=[RS1])
            pe_drain()
            E("tensor", lambda e: e.matmul(PBT[2][:, 0:SC], onesrowf[:], rowscr2[:],
                                           start=True, stop=True),
              reads=[RS2, ORF], writes_pre=[PB[2]], writes=[PB[2]])
            E("tensor", lambda e: e.matmul(PBT[5][:, 0:SC], onesrowf[:], rowscr[:],
                                           start=True, stop=True),
              reads=[RS1, ORF], writes_pre=[PB[5]], writes=[PB[5]])
            for hbk in range(2):
                E("vector",
                  (lambda hbk_, t0_: lambda e: e.tensor_tensor(
                      out=u32a[:], in0=X2t[:, hbk_ * 4:(hbk_ + 1) * 4, t0_:t0_ + SC],
                      in1=PBT[2][:, 0:SC].unsqueeze(1).broadcast_to([128, 4, SC]),
                      op=ALU.mult))(hbk, t0),
                  reads=[X2B, PB[2]], writes=[U32B])
                E("vector",
                  (lambda hbk_: lambda e: e.tensor_tensor(
                      out=x2n[:, hbk_ * 4:(hbk_ + 1) * 4, :], in0=u32a[:],
                      in1=PBT[5][:, 0:SC].unsqueeze(1).broadcast_to([128, 4, SC]),
                      op=ALU.subtract))(hbk),
                  reads=[U32B, PB[5]],
                  writes=([X2NB] if hbk == 1 else []),
                  writes_pre=([X2NB] if hbk == 0 else []), inc=True)
            E("scalar", lambda e: e.activation(out=z16[0:16, :],
                                               in_=PBT[6][0:16, 0:SC],
                                               func=AF.Identity),
              reads=[PB[6]], writes=[Z16B])
            E("vector", lambda e: e.tensor_tensor(out=z16[0:16, :], in0=z16[0:16, :],
                                                  in1=PBT[2][0:16, 0:SC],
                                                  op=ALU.mult),
              reads=[PB[2]], writes=[Z16B])
            E("vector", lambda e: e.scalar_tensor_tensor(
                out=z16[0:16, :], in0=PBT[5][0:16, 0:SC], scalar=njr16[:, 0:1],
                in1=z16[0:16, :], op0=ALU.mult, op1=ALU.add),
              reads=[PB[5]], sreads=[NJR], writes=[Z16B])
            E("vector", lambda e: e.tensor_scalar(out=z16[0:16, :], in0=z16[0:16, :],
                                                  scalar1=kb16[:, 0:1], scalar2=None,
                                                  op0=ALU.add),
              sreads=[KB16], writes=[Z16B])
            for tt in range(4):
                pe_drain()
                E("tensor",
                  (lambda tt_: lambda e: e.matmul(
                      PBT[4][:, tt_ * 16:(tt_ + 1) * 16],
                      z16[0:16, tt_ * 128:(tt_ + 1) * 128], ident[0:16, 0:16],
                      start=True, stop=True))(tt),
                  reads=[Z16B, VB], writes_pre=([PB[4]] if tt == 0 else []),
                  writes=([PB[4]] if tt == 3 else []), inc=True)
            E("vector", lambda e: e.tensor_copy(
                out=ztm[:].rearrange("p a e -> p (a e)"), in_=PBT[4][:, 0:64]),
              reads=[PB[4]], writes=[ZTMB])
            E("vector", lambda e: e.memset(mm1[:].rearrange("p a e -> p (a e)"),
                                           -1.0),
              writes=[MM1B])
            for rnd in range(TOPK):
                E("vector", lambda e: e.tensor_reduce(out=scr8[:, 8:12], in_=ztm[:],
                                                      axis=AX, op=ALU.max),
                  reads=[ZTMB], writes=[SCR])
                E("vector", lambda e: e.tensor_tensor(
                    out=tsc[:], in0=ztm[:],
                    in1=scr8[:, 8:12].unsqueeze(2).broadcast_to([128, 4, 16]),
                    op=ALU.is_ge),
                  reads=[SCR], writes=[TSCB])
                E("vector", lambda e: e.scalar_tensor_tensor(
                    out=tsc[:], in0=tsc[:], scalar=-99.0,
                    in1=vcols("iota99", 0, 16).unsqueeze(1)
                    .broadcast_to([128, 4, 16]),
                    op0=ALU.mult, op1=ALU.add),
                  reads=[VB], writes=[TSCB])
                E("vector", lambda e: e.tensor_reduce(out=scr8[:, 12:16], in_=tsc[:],
                                                      axis=AX, op=ALU.min),
                  reads=[TSCB], writes=[SCR])
                E("vector", lambda e: e.tensor_tensor(
                    out=tsc[:],
                    in0=vcols("iota16", 0, 16).unsqueeze(1)
                    .broadcast_to([128, 4, 16]),
                    in1=scr8[:, 12:16].unsqueeze(2).broadcast_to([128, 4, 16]),
                    op=ALU.is_equal),
                  reads=[SCR, VB], writes=[TSCB])
                E("vector", lambda e: e.tensor_tensor(out=mm1[:], in0=mm1[:],
                                                      in1=tsc[:], op=ALU.add),
                  reads=[TSCB], writes=[MM1B])
                if rnd < TOPK - 1:
                    E("vector", lambda e: e.scalar_tensor_tensor(
                        out=ztm[:], in0=tsc[:], scalar=-BIGNEG, in1=ztm[:],
                        op0=ALU.mult, op1=ALU.add),
                      reads=[TSCB], writes=[ZTMB])
            E("vector", lambda e: e.tensor_copy(
                out=mm1b.rearrange("p (a e) -> p a e", a=4), in_=mm1[:]),
              reads=[MM1B], writes=[MM1BB])
            for tt in range(4):
                pe_drain()
                E("tensor",
                  (lambda tt_: lambda e: e.matmul(
                      PBT[4][0:16, tt_ * 128:(tt_ + 1) * 128],
                      mm1b.rearrange("p (a e) -> p a e", a=4)[:, tt_, :],
                      identb[:], start=True, stop=True))(tt),
                  reads=[MM1BB, IDB], writes_pre=([PB[4]] if tt == 0 else []),
                  writes=([PB[4]] if tt == 3 else []), inc=True)
            E("scalar", lambda e: e.activation(out=mask_em[0:16, :],
                                               in_=PBT[4][0:16, 0:512],
                                               func=AF.Identity),
              reads=[PB[4]], writes=[MEMB])
            if q == 0 and c == 0:
                tap("z00", z16[0:16, :], [16, SC], reads=[Z16B])
                tap("mask00", mm1[:].rearrange("p a e -> p (a e)"), [128, 64],
                    reads=[MM1B])
                tap("x2n00", x2n[:, 0, :], [128, SC], BF16, reads=[X2NB])
            for j in range(FT):
                sl = j % 2
                dma(w1bufb[:, sl, :], w1s[:, j * 1024:(j + 1) * 1024],
                    reads=([W1S, GA, GC] if (first and j < 2) else [W1S]),
                    writes=[W1SL[sl]], q="qw10" if sl == 0 else "qw11")
                bank = j % 4
                blk = j // 2
                for kt in range(KD):
                    E("tensor",
                      (lambda sl_, kt_, bank_: lambda e: e.matmul(
                          PBT[bank_][:, 0:SC],
                          w1bufb[:, sl_, kt_ * 128:(kt_ + 1) * 128],
                          x2n[:, kt_, :],
                          start=(kt_ == 0), stop=False))(sl, kt, bank),
                      reads=[W1SL[sl], X2NB],
                      writes_pre=([PB[bank]] if kt == 0 else []),
                      inc=(kt == KD - 1))
                E("tensor",
                  (lambda blk_, bank_: lambda e: e.matmul(
                      PBT[bank_][:, 0:SC], bigsel[:, blk_ * 128:(blk_ + 1) * 128],
                      mask_em[0:16, :], start=False, stop=True))(blk, bank),
                  reads=[MEMB, BS], writes=[PB[bank]], inc=True)
                E("scalar",
                  (lambda j_, bank_: lambda e: e.activation(
                      out=hh[:, j_, :], in_=PBT[bank_][:, 0:SC], func=AF.Relu,
                      bias=KB1t[:, j_:j_ + 1]))(j, bank),
                  reads=[PB[bank], KB1B],
                  writes=([HHB] if j == FT - 1 else []),
                  writes_pre=([HHB] if j == 0 else []), inc=True)
            for ot in range(FT):
                sl = ot % 2
                dma(w2bufb[:, sl, :], w2s[:, ot * 1024:(ot + 1) * 1024],
                    reads=([W2S, GC] if (first and ot < 2) else [W2S]),
                    writes=[W2SL[sl]], q="qw20" if sl == 0 else "qw21")
                for dt in range(KD):
                    E("tensor",
                      (lambda sl_, ot_, dt_: lambda e: e.matmul(
                          PBT[dt_][:, 0:SC],
                          w2bufb[:, sl_, dt_ * 128:(dt_ + 1) * 128],
                          hh[:, ot_, :],
                          start=(ot_ == 0), stop=False))(sl, ot, dt),
                      reads=[W2SL[sl], HHB],
                      writes_pre=([PB[dt]] if ot == 0 else []),
                      inc=(dt == KD - 1))
            for dt in range(KD):
                E("tensor",
                  (lambda dt_: lambda e: e.matmul(
                      PBT[dt_][:, 0:SC], b2row[0:1, dt_ * 128:(dt_ + 1) * 128],
                      onesrow512_bf, start=False, stop=True))(dt),
                  reads=[B2R, CB], writes=[PB[dt]], inc=True)
            if q == 0 and c == 0:
                tap("hh00", hh[:, 0, :], [128, SC], BF16, reads=[HHB])
            for dt in range(KD):
                E("vector",
                  (lambda dt_, t0_: lambda e: e.scalar_tensor_tensor(
                      out=X2t[:, dt_, t0_:t0_ + SC], in0=PBT[dt_][:, 0:SC],
                      scalar=modd[:, 24 + dt_:25 + dt_],
                      in1=X2t[:, dt_, t0_:t0_ + SC], op0=ALU.mult,
                      op1=ALU.add))(dt, t0),
                  reads=[PB[dt]], sreads=[MODD],
                  writes=([X2B] if dt == KD - 1 else []),
                  writes_pre=([X2B] if dt == 0 else []), inc=True)
            dma(o3[:, :, gtok:gtok + SC], X2t[:, :, t0:t0 + SC], reads=[X2B], q="qo")

        steps = [
            lambda: s1_quarter(0, filler=p0b_fill),
            lambda: s1_quarter(1, filler=p0b_fill),
            lambda: (p0b_fill(1000), kb.barrier()),
            lambda: s2_chunk(0, 0, first=True),
            lambda: s2_chunk(0, 1),
            lambda: kb.barrier(),
            lambda: s1_quarter(2),
            lambda: kb.barrier(),
            lambda: s2_chunk(1, 0),
            lambda: s2_chunk(1, 1),
            lambda: kb.barrier(),
            lambda: s1_quarter(3),
            lambda: kb.barrier(),
            lambda: s2_chunk(2, 0),
            lambda: s2_chunk(2, 1),
            lambda: s2_chunk(3, 0),
            lambda: s2_chunk(3, 1),
        ]
        import os as _os
        lim = int(_os.environ.get("KMK_STEPS", str(len(steps))))
        for st in steps[:lim]:
            st()

        kb.raw_wait("sync", [(qq, kb.cnt[qq]) for qq in QSEMS
                             if kb.cnt[qq] > 0], hard=True)

        sems = {}
        for e in list(ENGS) + list(QSEMS):
            sems[e] = ctx.enter_context(nc.semaphore("sem_" + e))
        kb.replay(nc, sems)

    return nc



def _host_vec(c_row, conv_w, conv_b, wr_b, wi_b, log_a, norm_w, b1, b2,
              router_b):
    v = np.zeros((128, VCOLS), np.float32)
    C = VEC_COLS
    v[:, C["ident"]:C["ident"] + 128] = np.eye(128, dtype=np.float32)
    v[:, C["ones"]] = 1.0
    v[:, C["epsln"]] = EPS_LN
    v[:, C["epsrms"]] = EPS_RMS
    for j in range(16):
        v[:, C["iota16"] + j] = float(j)
        v[:, C["iota99"] + j] = float(j) + 99.0
    v[:, C["c8"]:C["c8"] + 8] = c_row.reshape(8, 128).T
    for i in range(NL):
        for k in range(4):
            for ot in range(KO):
                v[:, C["cw"] + i * 8 + k * 2 + ot] = \
                    conv_w[i, ot * 128:(ot + 1) * 128, k]
        for ot in range(KO):
            v[:, C["cbv"] + i * 2 + ot] = conv_b[i, ot * 128:(ot + 1) * 128]
            v[:, C["wrb"] + i * 2 + ot] = wr_b[i, ot * 128:(ot + 1) * 128]
            v[:, C["wib"] + i * 2 + ot] = wi_b[i, ot * 128:(ot + 1) * 128]
            v[:, C["lav"] + i * 2 + ot] = log_a[i, ot * 128:(ot + 1) * 128]
            v[:, C["nwv"] + i * 2 + ot] = norm_w[i, ot * 128:(ot + 1) * 128]
    v[:, C["b1c"]:C["b1c"] + 32] = b1.reshape(32, 128).T
    v[:, C["b2c"]:C["b2c"] + 8] = b2.reshape(8, 128).T
    v[0:16, C["rbc"]] = router_b
    return v


def _host_const():
    cpk = np.zeros((128, CPK_COLS), np.float32)
    cpk[:, CPK_ONES:CPK_ONES + 512] = 1.0
    cpk[:, CPK_BIG:CPK_BIG + 128] = BIGNEG
    bigsel = np.zeros((16, NB * 128), np.float32)
    for j in range(NB):
        bigsel[j, j * 128:(j + 1) * 128] = BIGNEG
    return cpk.astype(NPBF16), bigsel.astype(NPBF16)


def kernel(L0_attn_h, L0_attn_down_w, L0_attn_up_w, L0_attn_conv_w,
           L0_attn_conv_b, L0_attn_Wr_w, L0_attn_Wr_b, L0_attn_Wi_w,
           L0_attn_Wi_b, L0_attn_log_a, L0_attn_out_w, L0_attn_norm_w,
           L0_ffn_router_w, L0_ffn_router_b, L0_ffn_w1, L0_ffn_b1,
           L0_ffn_w2, L0_ffn_b2, L0_c, L0_wmod):
    f32 = np.float32
    asc = np.ascontiguousarray
    wmod_a = asc(np.asarray(L0_wmod, f32))
    dwT_a = asc(np.asarray(L0_attn_down_w, f32).T)
    upT_a = asc(np.asarray(L0_attn_up_w, f32).T)
    wrT_a = asc(np.asarray(L0_attn_Wr_w, f32).transpose(0, 2, 1))
    wiT_a = asc(np.asarray(L0_attn_Wi_w, f32).transpose(0, 2, 1))
    owT_a = asc(np.asarray(L0_attn_out_w, f32).transpose(0, 2, 1))
    w1T_a = asc(np.asarray(L0_ffn_w1, f32).T)
    w2T_a = asc(np.asarray(L0_ffn_w2, f32).T)
    rwT_a = asc(np.asarray(L0_ffn_router_w, f32).T)
    cpk_a, bigsel_a = _host_const()
    x = np.asarray(L0_attn_h, f32)

    nc = build_nc()
    in_maps = []
    for b in range(B):
        vec = _host_vec(np.asarray(L0_c, f32)[b], np.asarray(L0_attn_conv_w, f32),
                        np.asarray(L0_attn_conv_b, f32),
                        np.asarray(L0_attn_Wr_b, f32),
                        np.asarray(L0_attn_Wi_b, f32),
                        np.asarray(L0_attn_log_a, f32),
                        np.asarray(L0_attn_norm_w, f32),
                        np.asarray(L0_ffn_b1, f32), np.asarray(L0_ffn_b2, f32),
                        np.asarray(L0_ffn_router_b, f32))
        in_maps.append({
            "xT": asc(x[b].T),
            "wmod": wmod_a, "dwT": dwT_a, "upT": upT_a,
            "wrT": wrT_a, "wiT": wiT_a, "owT": owT_a,
            "w1T": w1T_a, "w2T": w2T_a, "rwT": rwT_a,
            "vec": vec, "cpk": cpk_a, "bigsel": bigsel_a,
        })
    res = run_bass_kernel_spmd(nc, in_maps, core_ids=list(range(B)))
    out = np.stack([np.ascontiguousarray(r["outT"].T) for r in res.results])
    if DEBUG_TAPS:
        kernel.last_results = res.results
    return out


# version 05
# speedup vs baseline: 1.0295x; 1.0295x over previous
import contextlib

import numpy as np
import ml_dtypes

import concourse.bass as bass
import concourse.mybir as mybir
from concourse.bass_utils import run_bass_kernel_spmd

F32 = mybir.dt.float32
BF16 = mybir.dt.bfloat16
AF = mybir.ActivationFunctionType
ALU = mybir.AluOpType
AX = mybir.AxisListType.X
NPBF16 = np.dtype(ml_dtypes.bfloat16)

B = 8
T = 4096
D = 1024
DO = 256
NL = 3
DF = 4096
NB = 16
TOPK = 4
QT = 1024
NQ = 4
SC = 512
KD = D // 128
KO = DO // 128
FT = DF // 128
EPS_LN = 1e-5
EPS_RMS = 1e-6
BIGNEG = 1e30

DEBUG_TAPS = []


def _veccols():
    cols = {}
    off = 0

    def put(name, n):
        nonlocal off
        cols[name] = off
        off += n

    put("ident", 128)
    put("ones", 1)
    put("epsln", 1)
    put("epsrms", 1)
    put("iota16", 16)
    put("iota99", 16)
    put("c8", 8)
    put("cw", 24)
    put("cbv", 6)
    put("wrb", 6)
    put("wib", 6)
    put("lav", 6)
    put("nwv", 6)
    put("b1c", 32)
    put("b2c", 8)
    put("rbc", 1)
    return cols, off


VEC_COLS, VCOLS = _veccols()
CPK_ONES = 0
CPK_BIG = 512
CPK_COLS = 640

ENGS = ("sync", "tensor", "vector", "scalar", "gpsimd")
QSEMS = ("qm", "qx0", "qx1", "qwm0", "qwm1", "qpb", "qpb2", "qs1", "qs2",
         "qw10", "qw11", "qw20", "qw21", "qo")
FENCE = set()


class Buf:
    __slots__ = ("ready", "readers", "name")

    def __init__(self, name=""):
        self.ready = None
        self.readers = []
        self.name = name


class KB:

    def __init__(self):
        self.items = {e: [] for e in ENGS}
        self.cnt = {e: 0 for e in ENGS}
        self.cnt.update({q: 0 for q in QSEMS})
        self.lastw = {e: {} for e in ENGS}
        self.pending = {e: [] for e in ENGS}

    def op(self, eng, fn, reads=(), sreads=(), writes=(), writes_pre=(),
           inc=True, dma=False, q=None):
        assert not (writes and not inc), "final writes must inc"
        waits = []

        def add(tok, hard):
            if tok is None:
                return
            pe, v = tok
            if pe == eng and not hard and eng == "tensor":
                return
            lw = self.lastw[eng]
            if lw.get(pe, 0) >= v:
                return
            lw[pe] = v
            waits.append((pe, v))

        for b in reads:
            add(b.ready, False)
        for b in sreads:
            add(b.ready, True)
        for b in tuple(writes) + tuple(writes_pre):
            add(b.ready, False)
            for t in b.readers:
                add(t, False)
        step = 16 if dma else 1
        sem = q if dma else eng
        assert not dma or q in QSEMS, f"dma needs a stream sem, got {q}"
        if dma and q == "qm" and self.cnt["qm"] > 0:
            if self.lastw[eng].get("qm", 0) < self.cnt["qm"]:
                self.lastw[eng]["qm"] = self.cnt["qm"]
                waits.append(("qm", self.cnt["qm"]))
        tok = None
        if inc:
            self.cnt[sem] += step
            tok = (sem, self.cnt[sem])
        self.items[eng].append((waits, fn, inc, step, sem))
        if inc:
            for b in self.pending[eng]:
                b.readers.append(tok)
            self.pending[eng].clear()
            for b in reads:
                b.readers.append(tok)
            for b in sreads:
                b.readers.append(tok)
            for b in writes:
                b.ready = tok
                b.readers = []
        else:
            for b in reads:
                if b not in self.pending[eng]:
                    self.pending[eng].append(b)
            for b in sreads:
                if b not in self.pending[eng]:
                    self.pending[eng].append(b)
        return tok

    def raw_wait(self, eng, toks, hard=False):
        waits = []
        for (pe, v) in toks:
            lw = self.lastw[eng]
            if pe == eng and not hard:
                continue
            if lw.get(pe, 0) >= v:
                continue
            lw[pe] = v
            waits.append((pe, v))
        if waits:
            self.items[eng].append((waits, None, False, 1, eng))

    def barrier(self, engs=("tensor", "vector", "scalar", "gpsimd")):
        snap = [(e, self.cnt[e]) for e in ENGS if self.cnt[e] > 0]
        for e in engs:
            self.raw_wait(e, [t for t in snap if t[0] != e])

    def replay(self, nc, sems):
        def make(ename):
            def handler(eng):
                for idx, (waits, fn, inc, step, sem) in enumerate(
                        self.items[ename]):
                    for (pe, v) in waits:
                        if pe in FENCE:
                            v = self.cnt[pe]
                        eng.wait_ge(sems[pe], v)
                    if fn is not None:
                        try:
                            inst = fn(eng)
                        except Exception as ex:
                            raise RuntimeError(
                                f"replay fail {ename}[{idx}]: {ex}") from ex
                        if inc:
                            inst.then_inc(sems[sem], step)
            return handler

        with nc.Block() as block:
            block.sync(make("sync"))
            block.tensor(make("tensor"))
            block.vector(make("vector"))
            block.scalar(make("scalar"))
            block.gpsimd(make("gpsimd"))



def _act_set_id(nc, need):
    from concourse.hw_specs import get_activation_tables
    for idx, funcs in enumerate(get_activation_tables(nc.m.arch).values()):
        if all(f in funcs for f in need):
            return idx
    raise ValueError(f"no act table set with {need}")



def build_nc():
    nc = bass.Bass("TRN2", target_bir_lowering=False)
    kb = KB()
    SET_SILU = _act_set_id(nc, [AF.Silu])
    SET_NLE = _act_set_id(nc, [AF.Exp, AF.Ln, AF.Square, AF.Relu, AF.Identity,
                               AF.Abs])
    SET_SIG = _act_set_id(nc, [AF.Sigmoid])

    def preload(eng, set_id):
        return eng.add_instruction(mybir.InstLoadActFuncSet(
            name=nc.get_next_instruction_name(), act_func_set_id=set_id,
            ins=[], outs=[]))

    xT = nc.dram_tensor("xT", [D, T], F32, kind="ExternalInput").ap()
    wmod = nc.dram_tensor("wmod", [D, 6 * D], F32, kind="ExternalInput").ap()
    dwT = nc.dram_tensor("dwT", [D, DO], F32, kind="ExternalInput").ap()
    upT = nc.dram_tensor("upT", [DO, D], F32, kind="ExternalInput").ap()
    wrT = nc.dram_tensor("wrT", [NL, DO, DO], F32, kind="ExternalInput").ap()
    wiT = nc.dram_tensor("wiT", [NL, DO, DO], F32, kind="ExternalInput").ap()
    owT = nc.dram_tensor("owT", [NL, DO, DO], F32, kind="ExternalInput").ap()
    w1T = nc.dram_tensor("w1T", [D, DF], F32, kind="ExternalInput").ap()
    w2T = nc.dram_tensor("w2T", [DF, D], F32, kind="ExternalInput").ap()
    rwT = nc.dram_tensor("rwT", [D, NB], F32, kind="ExternalInput").ap()
    vecd = nc.dram_tensor("vec", [128, VCOLS], F32, kind="ExternalInput").ap()
    cpkd = nc.dram_tensor("cpk", [128, CPK_COLS], BF16, kind="ExternalInput").ap()
    bigseld = nc.dram_tensor("bigsel", [16, NB * 128], BF16,
                             kind="ExternalInput").ap()
    outT = nc.dram_tensor("outT", [D, T], F32, kind="ExternalOutput").ap()
    w1s = nc.dram_tensor("w1s", [128, FT * KD * 128], BF16, kind="Internal").ap()
    w2s = nc.dram_tensor("w2s", [128, FT * KD * 128], BF16, kind="Internal").ap()

    x3 = xT.rearrange("(a p) t -> p a t", p=128)
    o3 = outT.rearrange("(a p) t -> p a t", p=128)
    wmod3 = wmod.rearrange("(a p) n -> p a n", p=128)
    w1T3 = w1T.rearrange("(a p) n -> p a n", p=128)
    w2T3 = w2T.rearrange("(a p) n -> p a n", p=128)
    dwT3 = dwT.rearrange("(a p) n -> p a n", p=128)
    upT3 = upT.rearrange("(a p) n -> p a n", p=128)
    rwT3 = rwT.rearrange("(a p) n -> p a n", p=128)
    wrT3 = wrT.rearrange("i (a p) n -> i p a n", p=128)
    wiT3 = wiT.rearrange("i (a p) n -> i p a n", p=128)
    owT3 = owT.rearrange("i (a p) n -> i p a n", p=128)

    taps = {}
    ctx = contextlib.ExitStack()

    def sb(name, shape, dtype):
        return ctx.enter_context(nc.sbuf_tensor(name, shape, dtype))

    def psu(name, shape, dtype=F32):
        return ctx.enter_context(nc.psum_tensor(name, shape, dtype))

    with ctx:
        vec = sb("vec_s", [128, VCOLS], F32)
        cpk = sb("cpk_s", [128, CPK_COLS], BF16)
        bigsel = sb("bigsel_s", [16, NB * 128], BF16)
        dwTb = sb("dwTb", [128, KD * DO], BF16)
        upTb = sb("upTb", [128, KO * D], BF16)
        wrTb = sb("wrTb", [128, NL * KO * DO], BF16)
        wiTb = sb("wiTb", [128, NL * KO * DO], BF16)
        owTb = sb("owTb", [128, NL * KO * DO], BF16)
        rwTs = sb("rwTs", [128, KD * NB], F32)
        rwTp = sb("rwTp", [128, KD * NB], F32)
        mod48 = sb("mod48", [128, 48], F32)
        modd = sb("modd", [128, 32], F32)
        sapb = sb("sapb", [128, 8], BF16)
        sbpb = sb("sbpb", [128, 8], BF16)
        m8ls = sb("m8ls", [128, NL * KO], F32)
        cb1p = sb("cb1p", [128, KO], F32)
        kcol = sb("kcol", [128, KO], F32)
        scr8 = sb("scr8", [128, 24], F32)
        KB1t = sb("KB1", [128, FT], F32)
        negJrow = sb("negJrow", [1, DO], BF16)
        b2row = sb("b2row", [1, D], BF16)
        onesrowf = sb("onesrowf", [1, 128], F32)
        rowscr = sb("rowscr", [1, SC], F32)
        rowscr2 = sb("rowscr2", [1, SC], F32)
        rowscr3 = sb("rowscr3", [1, SC], F32)
        rowbf = sb("rowbf", [1, SC], BF16)
        kb16 = sb("kb16", [16, 1], F32)
        njr16 = sb("njr16", [16, 1], F32)
        tails = sb("tails", [128, NL, KO, 3], F32)
        state = sb("state", [128, NL * KO], F32)
        identb = sb("identb", [128, 128], BF16)
        genA = sb("genA", [128, 1024], F32)
        genB = sb("genB", [128, 512], F32)
        genC = sb("genC", [128, 1024], F32)
        XBt = sb("XB", [128, KD, QT], F32)
        X2t = sb("X2", [128, KD, 2 * QT], F32)
        AR = sb("AR", [128, 16000], F32)

        def arf(off_w, n_w):
            assert off_w + n_w <= 16000
            return AR[:, off_w:off_w + n_w]

        def arb(off_w, n_bf):
            assert off_w + (n_bf + 1) // 2 <= 16000
            return AR[:, off_w:off_w + (n_bf + 1) // 2].bitcast(BF16)

        p0_stgA = arf(0, 3072)
        p0_stgB = arf(3072, 3072)
        p0_stgbf = arb(6144, 6144)
        dw_stg = arf(0, KD * DO).rearrange("p (a n) -> p a n", a=KD)
        up_stg = arf(0, KO * D).rearrange("p (a n) -> p a n", a=KO)
        at_stg = arf(2048, KO * DO).rearrange("p (a n) -> p a n", a=KO)
        XRA = arf(0, KO * 1027).rearrange("p (o t) -> p o t", o=KO)
        XRB = arf(2054, KO * 1027).rearrange("p (o t) -> p o t", o=KO)
        LAs = [arb(4108, KO * SC).rearrange("p (o t) -> p o t", o=KO),
               arb(4620, KO * SC).rearrange("p (o t) -> p o t", o=KO)]
        AAs = [arf(5132, KO * SC).rearrange("p (o t) -> p o t", o=KO),
               arf(6156, KO * SC).rearrange("p (o t) -> p o t", o=KO)]
        T1s = [arf(7180, KO * SC).rearrange("p (o t) -> p o t", o=KO),
               arf(8204, KO * SC).rearrange("p (o t) -> p o t", o=KO)]
        rstdB = arf(9228, SC)
        XCB = arb(9740, KO * QT).rearrange("p (o t) -> p o t", o=KO)
        Rb = arb(10764, KO * QT).rearrange("p (o t) -> p o t", o=KO)
        Ib = arb(11788, KO * QT).rearrange("p (o t) -> p o t", o=KO)
        HSBs = [arb(12812, KO * SC).rearrange("p (o t) -> p o t", o=KO),
                arb(13324, KO * SC).rearrange("p (o t) -> p o t", o=KO)]
        xbr = arb(13836, 2 * SC).rearrange("p (s t) -> p s t", s=2)
        xsqr = arb(14348, 2 * SC).rearrange("p (s t) -> p s t", s=2)
        OSQ = arb(13836, KO * SC).rearrange("p (o t) -> p o t", o=KO)
        x2n = arb(0, KD * SC).rearrange("p (a t) -> p a t", a=KD)
        hh = arb(2048, FT * SC).rearrange("p (a t) -> p a t", a=FT)
        u32a = arf(10240, 4 * SC).rearrange("p (a t) -> p a t", a=4)
        x2br = arb(12288, 2 * SC).rearrange("p (s t) -> p s t", s=2)
        x2sqr = arb(12812, 2 * SC).rearrange("p (s t) -> p s t", s=2)
        z16 = arf(13324, SC)
        ztm = arf(13836, 64).rearrange("p (a e) -> p a e", a=4)
        mm1 = arf(13900, 64).rearrange("p (a e) -> p a e", a=4)
        tsc = arf(13964, 64).rearrange("p (a e) -> p a e", a=4)
        mask_em = arb(14028, 512)
        mm1b = arb(14284, 64)

        PBT = [psu(f"pb{k}", [128, 512], F32) for k in range(8)]

        bufs = {}

        def B_(name):
            if name not in bufs:
                bufs[name] = Buf(name)
            return bufs[name]

        PB = [B_(f"pb{k}") for k in range(8)]

        def vcol(name, i=0):
            c = VEC_COLS[name] + i
            return vec[:, c:c + 1]

        def vrow0(name, i=0):
            c = VEC_COLS[name] + i
            return vec[0:1, c:c + 1]

        def vcols(name, i, n):
            c = VEC_COLS[name] + i
            return vec[:, c:c + n]

        ident = vec[:, VEC_COLS["ident"]:VEC_COLS["ident"] + 128]
        onescol_bf = cpk[:, CPK_ONES:CPK_ONES + 1]
        onesrow512_bf = cpk[0:1, CPK_ONES:CPK_ONES + 512]

        E = kb.op

        def pe_drain():
            kb.raw_wait("tensor", [("tensor", kb.cnt["tensor"])], hard=True)

        ones11 = vec[0:1, VEC_COLS["ones"]:VEC_COLS["ones"] + 1]

        def dma(dst_ap, src_ap, reads=(), writes=(), writes_pre=(), q="qm"):
            return kb.op("sync",
                         (lambda d, s: (lambda e: e.dma_start(out=d, in_=s)))(
                             dst_ap, src_ap),
                         reads=reads, writes=writes, writes_pre=writes_pre,
                         inc=True, dma=True, q=q)

        def tap(name, src_ap, shape, dtype=F32, reads=()):
            if name not in DEBUG_TAPS:
                return
            t = nc.dram_tensor("tap_" + name, list(shape), dtype,
                               kind="ExternalOutput").ap()
            taps[name] = t
            dma(t, src_ap, reads=reads)

        VB, CB, BS = B_("vec"), B_("cpk"), B_("bigsel")
        dma(vec[:], vecd, writes=[VB])
        dma(cpk[:], cpkd, writes=[CB])
        dma(bigsel[:], bigseld, writes=[BS])
        XBS = [B_("XB0"), B_("XB1")]
        dma(XBt[:, :, 0:SC], x3[:, :, 0:SC], writes=[XBS[0]], q="qx0")
        dma(XBt[:, :, SC:QT], x3[:, :, SC:QT], writes=[XBS[1]], q="qx1")

        SCR = B_("scr8")
        SAPB, SBPB = B_("sapb"), B_("sbpb")
        E("scalar", lambda e: preload(e, SET_SILU), inc=False)
        E("scalar", lambda e: e.activation(out=scr8[:, 0:8], in_=vcols("c8", 0, 8),
                                           func=AF.Silu),
          reads=[VB], writes=[SCR])
        E("vector", lambda e: e.tensor_copy(out=sapb[:], in_=scr8[:, 0:8]),
          reads=[SCR], writes=[SAPB])

        stg = [B_("p0stgA"), B_("p0stgB")]
        stgb = [B_("p0stgbfA"), B_("p0stgbfB")]
        wm_stg = [p0_stgA[:, 0:1024].rearrange("p (a n) -> p a n", a=KD),
                  p0_stgB[:, 0:1024].rearrange("p (a n) -> p a n", a=KD)]
        wm_bf = [arb(6144, 1024).rearrange("p (a n) -> p a n", a=KD),
                 arb(6656, 1024).rearrange("p (a n) -> p a n", a=KD)]
        for oc in range(48):
            sl = oc % 2
            dma(wm_stg[sl], wmod3[:, :, oc * 128:(oc + 1) * 128],
                writes=[stg[sl]], q="qwm0" if sl == 0 else "qwm1")
            E("vector",
              (lambda sl_: lambda e: e.tensor_copy(
                  out=wm_bf[sl_], in_=wm_stg[sl_]))(sl),
              reads=[stg[sl]], writes=[stgb[sl]])
            pe_drain()
            for kt in range(KD):
                E("tensor",
                  (lambda occ, kt_, sl_: lambda e: e.matmul(
                      PBT[5][:, occ:occ + 1], wm_bf[sl_][:, kt_, :],
                      sapb[:, kt_:kt_ + 1],
                      start=(kt_ == 0), stop=(kt_ == KD - 1)))(oc, kt, sl),
                  reads=[stgb[sl], SAPB],
                  writes_pre=([PB[5]] if (kt == 0 and oc == 0) else []),
                  writes=([PB[5]] if (kt == KD - 1 and oc == 47) else []),
                  inc=(kt == KD - 1))
        MODP = B_("mod48")
        E("scalar", lambda e: e.activation(out=mod48[:], in_=PBT[5][:, 0:48],
                                           func=AF.Identity),
          reads=[PB[5]], writes=[MODP])
        tap("mod48", mod48[:], [128, 48], reads=[MODP])

        IDB = B_("identb")
        E("vector", lambda e: e.tensor_copy(out=identb[:], in_=ident),
          reads=[VB], writes=[IDB])
        MODD = B_("modd")
        E("vector", lambda e: e.tensor_scalar(out=modd[:, 0:8], in0=mod48[:, 8:16],
                                              scalar1=1.0, scalar2=None, op0=ALU.add),
          reads=[MODP], writes=[MODD])
        E("vector", lambda e: e.tensor_scalar(out=modd[:, 8:16], in0=mod48[:, 32:40],
                                              scalar1=1.0, scalar2=None, op0=ALU.add),
          reads=[MODP], writes=[MODD])
        for (src0, dstc) in ((16, 16), (40, 24)):
            E("vector", (lambda s0: lambda e: e.tensor_scalar(
                out=scr8[:, 8:16], in0=mod48[:, s0:s0 + 8], scalar1=0.0,
                scalar2=None, op0=ALU.is_ge))(src0),
              reads=[MODP], writes=[SCR])
            E("vector", lambda e: e.tensor_scalar(
                out=scr8[:, 8:16], in0=scr8[:, 8:16], scalar1=2.0, scalar2=-1.0,
                op0=ALU.mult, op1=ALU.add), writes=[SCR])
            E("scalar", (lambda s0: lambda e: e.activation(
                out=scr8[:, 16:24], in_=mod48[:, s0:s0 + 8], func=AF.Abs))(src0),
              reads=[MODP], writes=[SCR])
            E("vector", lambda e: e.tensor_scalar(
                out=scr8[:, 16:24], in0=scr8[:, 16:24], scalar1=0.25,
                scalar2=None, op0=ALU.max), writes=[SCR])
            E("vector", (lambda dc: lambda e: e.tensor_tensor(
                out=modd[:, dc:dc + 8], in0=scr8[:, 8:16], in1=scr8[:, 16:24],
                op=ALU.mult))(dstc),
              reads=[SCR], writes=[MODD])
        E("vector", lambda e: e.reciprocal(out=scr8[:, 8:16], in_=modd[:, 0:8]),
          reads=[MODD], writes=[SCR])
        E("vector", lambda e: e.tensor_tensor(out=scr8[:, 8:16], in0=scr8[:, 8:16],
                                              in1=mod48[:, 0:8], op=ALU.mult),
          reads=[MODP], writes=[SCR])
        E("vector", lambda e: e.tensor_copy(out=sapb[:], in_=scr8[:, 8:16]),
          reads=[SCR], writes=[SAPB])
        E("vector", lambda e: e.reciprocal(out=scr8[:, 16:24], in_=modd[:, 8:16]),
          reads=[MODD], writes=[SCR])
        E("vector", lambda e: e.tensor_tensor(out=scr8[:, 16:24], in0=scr8[:, 16:24],
                                              in1=mod48[:, 24:32], op=ALU.mult),
          reads=[MODP], writes=[SCR])
        E("vector", lambda e: e.tensor_copy(out=sbpb[:], in_=scr8[:, 16:24]),
          reads=[SCR], writes=[SBPB])

        M8 = B_("m8ls")
        E("scalar", lambda e: preload(e, SET_NLE), inc=False)
        E("scalar", lambda e: e.activation(out=scr8[:, 8:14], in_=vcols("lav", 0, 6),
                                           func=AF.Abs),
          reads=[VB], writes=[SCR])
        E("scalar", lambda e: e.activation(out=scr8[:, 14:20], in_=scr8[:, 8:14],
                                           func=AF.Exp, scale=-1.0),
          reads=[SCR], writes=[SCR])
        E("scalar", lambda e: e.activation(out=scr8[:, 14:20], in_=scr8[:, 14:20],
                                           func=AF.Ln, bias=vcol("ones")),
          reads=[VB], writes=[SCR])
        E("vector", lambda e: e.tensor_scalar(out=scr8[:, 8:14], in0=vcols("lav", 0, 6),
                                              scalar1=0.0, scalar2=None, op0=ALU.min),
          reads=[VB], writes=[SCR])
        E("vector", lambda e: e.tensor_tensor(out=scr8[:, 8:14], in0=scr8[:, 8:14],
                                              in1=scr8[:, 14:20], op=ALU.subtract),
          writes=[SCR])
        E("vector", lambda e: e.tensor_scalar(out=m8ls[:], in0=scr8[:, 8:14],
                                              scalar1=8.0, scalar2=None, op0=ALU.mult),
          reads=[SCR], writes=[M8])

        DWS, DWB = B_("dwstg"), B_("dwTb")
        dma(dw_stg, dwT3, writes=[DWS], writes_pre=[stg[0]])
        for kt in range(KD):
            E("vector",
              (lambda kt_: lambda e: e.tensor_scalar(
                  out=dwTb[:, kt_ * DO:(kt_ + 1) * DO], in0=dw_stg[:, kt_, :],
                  scalar1=modd[:, kt_:kt_ + 1], scalar2=None, op0=ALU.mult))(kt),
              reads=[DWS, MODD],
              writes=([DWB] if kt == KD - 1 else []),
              writes_pre=([DWB] if kt == 0 else []),
              inc=(kt == KD - 1))
        NJ = B_("negJrow")
        pe_drain()
        for kt in range(KD):
            E("tensor",
              (lambda kt_: lambda e: e.matmul(
                  PBT[3][0:1, 0:DO], onescol_bf,
                  dwTb[:, kt_ * DO:(kt_ + 1) * DO],
                  start=(kt_ == 0), stop=(kt_ == KD - 1)))(kt),
              reads=[DWB, CB], writes_pre=([PB[3]] if kt == 0 else []),
              writes=([PB[3]] if kt == KD - 1 else []), inc=(kt == KD - 1))
        E("scalar", lambda e: e.activation(out=negJrow[:], in_=PBT[3][0:1, 0:DO],
                                           func=AF.Identity, scale=-1.0),
          reads=[PB[3]], writes=[NJ])
        pe_drain()
        for kt in range(KD):
            E("tensor",
              (lambda kt_: lambda e: e.matmul(
                  PBT[4][0:1, 0:DO], sapb[:, kt_:kt_ + 1],
                  dwTb[:, kt_ * DO:(kt_ + 1) * DO],
                  start=(kt_ == 0), stop=(kt_ == KD - 1)))(kt),
              reads=[DWB, SAPB], writes_pre=([PB[4]] if kt == 0 else []),
              writes=([PB[4]] if kt == KD - 1 else []), inc=(kt == KD - 1))
        RS1, RS2 = B_("rowscr"), B_("rowscr2")
        RS3 = B_("rowscr3")
        E("scalar", lambda e: e.activation(out=rowscr[0:1, 0:DO],
                                           in_=PBT[4][0:1, 0:DO], func=AF.Identity),
          reads=[PB[4]], writes=[RS1])
        KC = B_("kcol")
        for ot in range(KO):
            pe_drain()
            E("tensor",
              (lambda ot_: lambda e: e.matmul(
                  PBT[4][:, 64 + ot_:64 + ot_ + 1],
                  rowscr[0:1, ot_ * 128:(ot_ + 1) * 128], ones11,
                  start=True, stop=True))(ot),
              reads=[RS1, VB], writes_pre=([PB[4]] if ot == 0 else []),
              writes=([PB[4]] if ot == KO - 1 else []), inc=True)
        E("vector", lambda e: e.tensor_copy(out=kcol[:], in_=PBT[4][:, 64:64 + KO]),
          reads=[PB[4]], writes=[KC])
        CB1 = B_("cb1p")
        for ot in range(KO):
            E("vector",
              (lambda ot_: lambda e: e.tensor_tensor(
                  out=scr8[:, 8 + ot_:9 + ot_], in0=vcol("cw", 0 * 8 + 0 * 2 + ot_),
                  in1=vcol("cw", 0 * 8 + 1 * 2 + ot_), op=ALU.add))(ot),
              reads=[VB], writes=[SCR])
            E("vector",
              (lambda ot_: lambda e: e.tensor_tensor(
                  out=scr8[:, 10 + ot_:11 + ot_], in0=vcol("cw", 0 * 8 + 2 * 2 + ot_),
                  in1=vcol("cw", 0 * 8 + 3 * 2 + ot_), op=ALU.add))(ot),
              reads=[VB], writes=[SCR])
            E("vector",
              (lambda ot_: lambda e: e.tensor_tensor(
                  out=scr8[:, 8 + ot_:9 + ot_], in0=scr8[:, 8 + ot_:9 + ot_],
                  in1=scr8[:, 10 + ot_:11 + ot_], op=ALU.add))(ot),
              writes=[SCR])
            E("vector",
              (lambda ot_: lambda e: e.tensor_tensor(
                  out=scr8[:, 8 + ot_:9 + ot_], in0=scr8[:, 8 + ot_:9 + ot_],
                  in1=kcol[:, ot_:ot_ + 1], op=ALU.mult))(ot),
              reads=[KC], writes=[SCR])
            E("vector",
              (lambda ot_: lambda e: e.tensor_tensor(
                  out=cb1p[:, ot_:ot_ + 1], in0=scr8[:, 8 + ot_:9 + ot_],
                  in1=vcol("cbv", 0 * 2 + ot_), op=ALU.add))(ot),
              reads=[SCR, VB], writes=[CB1])
        tap("cb1p", cb1p[:], [128, KO], reads=[CB1])
        tap("kcolT", kcol[:], [128, KO], reads=[KC])
        tap("m8lsT", m8ls[:], [128, 6], reads=[M8])
        tap("moddT", modd[:], [128, 32], reads=[MODD])
        TL, ST = B_("tails"), B_("state")
        E("vector", lambda e: e.memset(tails[:], 0.0), writes=[TL])
        for ot in range(KO):
            E("vector",
              (lambda ot_: lambda e: e.tensor_scalar(
                  out=tails[:, 0, ot_, :], in0=tails[:, 0, ot_, :],
                  scalar1=kcol[:, ot_:ot_ + 1], scalar2=None, op0=ALU.subtract))(ot),
              reads=[KC], writes=[TL])
        E("vector", lambda e: e.memset(state[:], 0.0), writes=[ST])

        UPS, UPB = B_("upstg"), B_("upTb")
        dma(up_stg, upT3, writes=[UPS], writes_pre=[DWS])
        E("scalar", lambda e: e.copy(
            out=upTb[:].rearrange("p (a n) -> p a n", a=KO), in_=up_stg),
          reads=[UPS], writes=[UPB])

        ATS = B_("atstg")
        WRB_, WIB_, OWB_ = B_("wrTb"), B_("wiTb"), B_("owTb")
        first_at = True
        for (src3, dstt, dbuf) in ((wrT3, wrTb, WRB_), (wiT3, wiTb, WIB_),
                                   (owT3, owTb, OWB_)):
            for i in range(NL):
                dma(at_stg, src3[i], writes=[ATS],
                    writes_pre=([stg[0]] if first_at else []))
                first_at = False
                E("scalar",
                  (lambda dstt_, i_: lambda e: e.copy(
                      out=dstt_[:, i_ * KO * DO:(i_ + 1) * KO * DO]
                      .rearrange("p (a n) -> p a n", a=KO), in_=at_stg))(dstt, i),
                  reads=[ATS],
                  writes=([dbuf] if i == NL - 1 else []),
                  writes_pre=([dbuf] if i == 0 else []),
                  inc=True)

        RWS, RWP = B_("rwTs"), B_("rwTp")
        rw_stg = rwTs[:].rearrange("p (a n) -> p a n", a=KD)
        dma(rw_stg, rwT3, writes=[RWS])
        pe_drain()
        for kt in range(KD):
            E("tensor",
              (lambda kt_: lambda e: e.matmul(
                  PBT[3][0:1, 0:NB], mod48[:, 24 + kt_:25 + kt_],
                  rwTs[:, kt_ * NB:(kt_ + 1) * NB],
                  start=(kt_ == 0), stop=(kt_ == KD - 1)))(kt),
              reads=[RWS, MODP], writes_pre=([PB[3]] if kt == 0 else []),
              writes=([PB[3]] if kt == KD - 1 else []), inc=(kt == KD - 1))
        E("scalar", lambda e: e.activation(out=rowscr[0:1, 0:NB],
                                           in_=PBT[3][0:1, 0:NB], func=AF.Identity),
          reads=[PB[3]], writes=[RS1])
        pe_drain()
        E("tensor", lambda e: e.matmul(
            PBT[4][0:1, 0:16], vec[0:16, VEC_COLS["rbc"]:VEC_COLS["rbc"] + 1],
            ident[0:16, 0:16], start=True, stop=True),
          reads=[VB], writes_pre=[PB[4]], writes=[PB[4]])
        E("scalar", lambda e: e.activation(out=rowscr2[0:1, 0:16],
                                           in_=PBT[4][0:1, 0:16], func=AF.Identity),
          reads=[PB[4]], writes=[RS2])
        E("vector", lambda e: e.tensor_tensor(out=rowscr[0:1, 0:16],
                                              in0=rowscr[0:1, 0:16],
                                              in1=rowscr2[0:1, 0:16], op=ALU.add),
          reads=[RS2], writes=[RS1])
        pe_drain()
        E("tensor", lambda e: e.matmul(PBT[4][0:16, 16:17], rowscr[0:1, 0:16],
                                       ones11, start=True, stop=True),
          reads=[RS1, VB], writes_pre=[PB[4]], writes=[PB[4]])
        KB16 = B_("kb16")
        E("vector", lambda e: e.tensor_copy(out=kb16[:], in_=PBT[4][0:16, 16:17]),
          reads=[PB[4]], writes=[KB16])
        for kt in range(KD):
            E("vector",
              (lambda kt_: lambda e: e.tensor_scalar(
                  out=rwTp[:, kt_ * NB:(kt_ + 1) * NB],
                  in0=rwTs[:, kt_ * NB:(kt_ + 1) * NB],
                  scalar1=modd[:, 8 + kt_:9 + kt_], scalar2=None, op0=ALU.mult))(kt),
              reads=[RWS, MODD], writes=([RWP] if kt == KD - 1 else []),
              writes_pre=([RWP] if kt == 0 else []), inc=(kt == KD - 1))
        pe_drain()
        for kt in range(KD):
            E("tensor",
              (lambda kt_: lambda e: e.matmul(
                  PBT[3][0:1, 32:32 + NB], vcol("ones"),
                  rwTp[:, kt_ * NB:(kt_ + 1) * NB],
                  start=(kt_ == 0), stop=(kt_ == KD - 1)))(kt),
              reads=[RWP, VB], writes_pre=([PB[3]] if kt == 0 else []),
              writes=([PB[3]] if kt == KD - 1 else []), inc=(kt == KD - 1))
        E("scalar", lambda e: e.activation(out=rowscr[0:1, 0:NB],
                                           in_=PBT[3][0:1, 32:32 + NB],
                                           func=AF.Identity, scale=-1.0),
          reads=[PB[3]], writes=[RS1])
        pe_drain()
        E("tensor", lambda e: e.matmul(PBT[4][0:16, 18:19], rowscr[0:1, 0:16],
                                       ones11, start=True, stop=True),
          reads=[RS1, VB], writes_pre=[PB[4]], writes=[PB[4]])
        NJR = B_("njr16")
        E("vector", lambda e: e.tensor_copy(out=njr16[:], in_=PBT[4][0:16, 18:19]),
          reads=[PB[4]], writes=[NJR])

        ORF = B_("onesrowf")
        pe_drain()
        E("tensor", lambda e: e.matmul(PBT[4][0:1, 128:256], vcol("ones"), ident,
                                       start=True, stop=True),
          reads=[VB], writes_pre=[PB[4]], writes=[PB[4]])
        E("scalar", lambda e: e.activation(out=onesrowf[:], in_=PBT[4][0:1, 128:256],
                                           func=AF.Identity),
          reads=[PB[4]], writes=[ORF])
        B2R = B_("b2row")
        for g in range(2):
            for k in range(4):
                dt = g * 4 + k
                pe_drain()
                E("tensor",
                  (lambda dt_, k_: lambda e: e.matmul(
                      PBT[4][0:1, k_ * 128:(k_ + 1) * 128], vcol("b2c", dt_),
                      ident, start=True, stop=True))(dt, k),
                  reads=[VB], writes_pre=([PB[4]] if k == 0 else []),
                  writes=([PB[4]] if k == 3 else []), inc=True)
            E("scalar",
              (lambda g_: lambda e: e.activation(
                  out=b2row[0:1, g_ * 512:(g_ + 1) * 512],
                  in_=PBT[4][0:1, 0:512], func=AF.Identity))(g),
              reads=[PB[4]],
              writes=([B2R] if g == 1 else []),
              writes_pre=([B2R] if g == 0 else []), inc=True)

        kb.barrier()

        GA, GB_b, GC = B_("genA"), B_("genB"), B_("genC")
        W1S, W2S = B_("w1s"), B_("w2s")
        KB1B = B_("KB1")
        wblk = genC[:].rearrange("p (a n) -> p a n", a=KD)
        wblkb = genB[:].bitcast(BF16).rearrange("p (a n) -> p a n", a=KD)
        w2stg = genA[:]
        w2stgb = genC[:, 0:512].bitcast(BF16)

        def p0b_w1(j):
            dma(wblk, w1T3[:, :, j * 128:(j + 1) * 128], writes=[GC], q="qpb")
            for kt in range(KD):
                E("vector",
                  (lambda kt_: lambda e: e.tensor_scalar(
                      out=wblkb[:, kt_, :], in0=wblk[:, kt_, :],
                      scalar1=modd[:, 8 + kt_:9 + kt_], scalar2=None,
                      op0=ALU.mult))(kt),
                  reads=[GC, MODD],
                  writes=([GB_b] if kt == KD - 1 else []),
                  writes_pre=([GB_b] if kt == 0 else []),
                  inc=(kt == KD - 1))
            pe_drain()
            for kt in range(KD):
                E("tensor",
                  (lambda kt_: lambda e: e.matmul(
                      PBT[3][0:1, 128:256], sbpb[:, kt_:kt_ + 1], wblkb[:, kt_, :],
                      start=(kt_ == 0), stop=(kt_ == KD - 1)))(kt),
                  reads=[GB_b, SBPB], writes_pre=([PB[3]] if kt == 0 else []),
                  writes=([PB[3]] if kt == KD - 1 else []), inc=(kt == KD - 1))
            E("scalar",
              (lambda j_: lambda e: e.activation(
                  out=rowscr2[0:1, (j_ % 4) * 128:(j_ % 4) * 128 + 128],
                  in_=PBT[3][0:1, 128:256], func=AF.Identity))(j),
              reads=[PB[3]], writes=[RS2])
            if j % 4 == 3:
                for k in range(4):
                    pe_drain()
                    E("tensor",
                      (lambda k_: lambda e: e.matmul(
                          PBT[4][:, 300 + k_:301 + k_],
                          rowscr2[0:1, k_ * 128:(k_ + 1) * 128], ones11,
                          start=True, stop=True))(k),
                      reads=[RS2, VB], writes_pre=([PB[4]] if k == 0 else []),
                      writes=([PB[4]] if k == 3 else []), inc=True)
                E("vector",
                  (lambda j_: lambda e: e.tensor_copy(
                      out=KB1t[:, j_ - 3:j_ + 1], in_=PBT[4][:, 300:304]))(j),
                  reads=[PB[4]],
                  writes=([KB1B] if j == FT - 1 else []),
                  writes_pre=([KB1B] if j == 3 else []), inc=True)
            dma(w1s[:, j * 1024:(j + 1) * 1024],
                wblkb.rearrange("p a n -> p (a n)"), reads=[GB_b],
                writes=([W1S] if j == FT - 1 else []),
                writes_pre=([W1S] if j == 0 else []), q="qs1")

        def p0b_w2(ot):
            dma(w2stg, w2T3[:, ot, :], writes=[GA], q="qpb2")
            E("scalar", lambda e: e.copy(out=w2stgb, in_=w2stg),
              reads=[GA], writes=[GC])
            dma(w2s[:, ot * 1024:(ot + 1) * 1024], w2stgb, reads=[GC],
                writes=([W2S] if ot == FT - 1 else []),
                writes_pre=([W2S] if ot == 0 else []), q="qs2")

        p0b_jobs = [lambda j=j: p0b_w1(j) for j in range(FT)] + \
                   [lambda o=o: p0b_w2(o) for o in range(FT)]
        p0b_state = {"i": 0, "done": False}

        def p0b_fill(n):
            i = p0b_state["i"]
            for _ in range(n):
                if i >= len(p0b_jobs):
                    break
                p0b_jobs[i]()
                i += 1
            p0b_state["i"] = i
            if i >= len(p0b_jobs) and not p0b_state["done"]:
                p0b_state["done"] = True
                E("vector", lambda e: e.tensor_tensor(out=KB1t[:], in0=KB1t[:],
                                                      in1=vcols("b1c", 0, 32),
                                                      op=ALU.add),
                  reads=[VB], writes=[KB1B])

        XRAB, XRBB = B_("XRA"), B_("XRB")
        LABs = [B_("LA0"), B_("LA1")]
        AABs = [B_("AA0"), B_("AA1")]
        T1Bs = [B_("T10"), B_("T11")]
        HSBBs = [B_("HSB0"), B_("HSB1")]
        RSTB = B_("rstdB")
        XCBB = B_("XCB")
        RBS = [B_("Rb0"), B_("Rb1")]
        IBS = [B_("Ib0"), B_("Ib1")]
        XBRS = [B_("xbr0"), B_("xbr1")]
        XSQS = [B_("xsq0"), B_("xsq1")]
        OSQB = B_("OSQ")
        X2B = B_("X2")
        ROWB = B_("rowbf")

        def ln1_down(q, s):
            t0 = s * SC
            for kt in range(KD):
                sl = kt % 2
                E("scalar",
                  (lambda kt_, sl_, t0_: lambda e: e.copy(
                      out=xbr[:, sl_, :], in_=XBt[:, kt_, t0_:t0_ + SC]))(kt, sl, t0),
                  reads=[XBS[s]], writes=[XBRS[sl]],
                  writes_pre=([OSQB] if kt < 2 else []))
                E("scalar",
                  (lambda kt_, sl_, t0_: lambda e: e.activation(
                      out=xsqr[:, sl_, :], in_=XBt[:, kt_, t0_:t0_ + SC],
                      func=AF.Square))(kt, sl, t0),
                  reads=[XBS[s]], writes=[XSQS[sl]])
                E("tensor",
                  (lambda kt_, sl_: lambda e: e.matmul(
                      PBT[3][0:1, 0:SC], onescol_bf, xbr[:, sl_, :],
                      start=(kt_ == 0), stop=(kt_ == KD - 1)))(kt, sl),
                  reads=[XBRS[sl], CB], writes_pre=([PB[3]] if kt == 0 else []),
                  writes=([PB[3]] if kt == KD - 1 else []), inc=True)
                E("tensor",
                  (lambda kt_, sl_: lambda e: e.matmul(
                      PBT[7][0:1, 0:SC], onescol_bf, xsqr[:, sl_, :],
                      start=(kt_ == 0), stop=(kt_ == KD - 1)))(kt, sl),
                  reads=[XSQS[sl], CB], writes_pre=([PB[7]] if kt == 0 else []),
                  writes=([PB[7]] if kt == KD - 1 else []), inc=True)
                for ot in range(KO):
                    E("tensor",
                      (lambda kt_, ot_, sl_: lambda e: e.matmul(
                          PBT[ot_][:, 0:SC],
                          dwTb[:, (kt_ * KO + ot_) * 128:(kt_ * KO + ot_ + 1) * 128],
                          xbr[:, sl_, :],
                          start=(kt_ == 0), stop=False))(kt, ot, sl),
                      reads=[XBRS[sl], DWB],
                      writes_pre=([PB[ot]] if kt == 0 else []),
                      inc=True)
            E("vector", lambda e: e.tensor_scalar(out=rowscr[:], in0=PBT[3][0:1, 0:SC],
                                                  scalar1=1.0 / D, scalar2=None,
                                                  op0=ALU.mult),
              reads=[PB[3]], writes=[RS1])
            E("scalar", lambda e: e.activation(out=rowscr2[:], in_=rowscr[:],
                                               func=AF.Square),
              reads=[RS1], writes=[RS2])
            E("vector", lambda e: e.scalar_tensor_tensor(
                out=rowscr2[:], in0=PBT[7][0:1, 0:SC], scalar=1.0 / D,
                in1=rowscr2[:], op0=ALU.mult, op1=ALU.subtract),
              reads=[PB[7]], writes=[RS2])
            if q == 0 and s == 0:
                tap("var00row", rowscr2[:], [1, SC], reads=[RS2])
            E("scalar", lambda e: e.activation(out=rowscr2[:], in_=rowscr2[:],
                                               func=AF.Ln, bias=vrow0("epsln")),
              reads=[VB], writes=[RS2])
            if q == 0 and s == 0:
                tap("lnv00row", rowscr2[:], [1, SC], reads=[RS2])
            E("scalar", lambda e: e.activation(out=rowscr2[:], in_=rowscr2[:],
                                               func=AF.Exp, scale=-0.5),
              writes=[RS2])
            if q == 0 and s == 0:
                tap("m00row", rowscr[:], [1, SC], reads=[RS1])
                tap("rstdrow00", rowscr2[:], [1, SC], reads=[RS2])
                tap("ones00", onesrowf[:], [1, 128], reads=[ORF])
            E("vector", lambda e: e.tensor_copy(out=rowbf[:], in_=rowscr[:]),
              reads=[RS1], writes=[ROWB])
            for ot in range(KO):
                E("tensor",
                  (lambda ot_: lambda e: e.matmul(
                      PBT[ot_][:, 0:SC], negJrow[0:1, ot_ * 128:(ot_ + 1) * 128],
                      rowbf[:], start=False, stop=True))(ot),
                  reads=[ROWB, NJ], writes=[PB[ot]], inc=True)
            pe_drain()
            E("tensor", lambda e: e.matmul(PBT[2][:, 0:SC], onesrowf[:], rowscr2[:],
                                           start=True, stop=True),
              reads=[RS2, ORF], writes_pre=[PB[2]], writes=[PB[2]])
            E("scalar", lambda e: e.activation(out=rstdB[:], in_=PBT[2][:, 0:SC],
                                               func=AF.Identity),
              reads=[PB[2]], writes=[RSTB])
            for ot in range(KO):
                E("vector",
                  (lambda ot_, t0_: lambda e: e.tensor_tensor(
                      out=XRA[:, ot_, 3 + t0_:3 + t0_ + SC], in0=PBT[ot_][:, 0:SC],
                      in1=rstdB[:], op=ALU.mult))(ot, t0),
                  reads=[PB[ot], RSTB],
                  writes=([XRAB] if ot == KO - 1 else []),
                  writes_pre=([XRAB] if ot == 0 else []), inc=True)
            if q == 0 and s == 0:
                tap("xr00", XRA[:, 0, 3:3 + SC], [128, SC], reads=[XRAB])
                tap("rstd00", rstdB[:], [128, SC], reads=[RSTB])

        def layer(q, i):
            IN, INB = (XRA, XRAB) if i % 2 == 0 else (XRB, XRBB)
            OUT, OUTB = (XRB, XRBB) if i % 2 == 0 else (XRA, XRAB)
            wr_off = lambda kt, ot: ((i * KO + kt) * KO + ot) * 128
            E("vector",
              (lambda i_, IN_: lambda e: e.tensor_copy(
                  out=IN_[:, :, 0:3], in_=tails[:, i_, :, :]))(i, IN),
              reads=[TL], writes=[INB])
            E("scalar", lambda e: preload(e, SET_SIG), inc=False)
            for s in range(2):
                t0 = s * SC
                T1v, T1B = T1s[s], T1Bs[s]
                for ot in range(KO):
                    cbap = cb1p[:, ot:ot + 1] if i == 0 else vcol("cbv", i * 2 + ot)
                    E("vector",
                      (lambda i_, ot_, t0_, cbap_, IN_, T1v_: lambda e:
                       e.tensor_scalar(
                          out=T1v_[:, ot_, :], in0=IN_[:, ot_, t0_ + 0:t0_ + SC],
                          scalar1=vcol("cw", i_ * 8 + 0 * 2 + ot_), scalar2=cbap_,
                          op0=ALU.mult, op1=ALU.add))(i, ot, t0, cbap, IN, T1v),
                      reads=[INB, VB, CB1], writes=[T1B])
                    for k in (1, 2, 3):
                        last = (k == 3)
                        E("vector",
                          (lambda i_, ot_, t0_, k_, IN_, last_, T1v_: lambda e:
                           e.scalar_tensor_tensor(
                              out=(XCB[:, ot_, t0_:t0_ + SC] if last_
                                   else T1v_[:, ot_, :]),
                              in0=IN_[:, ot_, t0_ + k_:t0_ + k_ + SC],
                              scalar=vcol("cw", i_ * 8 + k_ * 2 + ot_),
                              in1=T1v_[:, ot_, :], op0=ALU.mult,
                              op1=ALU.add))(i, ot, t0, k, IN, last, T1v),
                          reads=[INB, VB, T1B],
                          writes=([T1B, XCBB] if last else [T1B]))
                for (wt, wbuf, dst, dstb, bname, bank0) in (
                        (wrTb, WRB_, Rb, RBS[s], "wrb", 4),
                        (wiTb, WIB_, Ib, IBS[s], "wib", 6)):
                    for ot in range(KO):
                        bank = bank0 + ot
                        for kt in range(KO):
                            E("tensor",
                              (lambda wt_, kt_, ot_, t0_, bank_: lambda e: e.matmul(
                                  PBT[bank_][:, 0:SC],
                                  wt_[:, wr_off(kt_, ot_):wr_off(kt_, ot_) + 128],
                                  XCB[:, kt_, t0_:t0_ + SC],
                                  start=(kt_ == 0), stop=(kt_ == KO - 1)))(
                                      wt, kt, ot, t0, bank),
                              reads=[XCBB, wbuf],
                              writes_pre=([PB[bank]] if kt == 0 else []),
                              writes=([PB[bank]] if kt == KO - 1 else []),
                              inc=(kt == KO - 1))
                        E("scalar",
                          (lambda dst_, i_, ot_, t0_, bank_, bn_: lambda e:
                           e.activation(
                              out=dst_[:, ot_, t0_:t0_ + SC],
                              in_=PBT[bank_][:, 0:SC], func=AF.Sigmoid,
                              bias=vcol(bn_, i_ * 2 + ot_)))(
                                  dst, i, ot, t0, bank, bname),
                          reads=[PB[bank], VB],
                          writes=([dstb] if ot == KO - 1 else []),
                          writes_pre=([dstb] if ot == 0 else []),
                          inc=True)
            E("scalar", lambda e: preload(e, SET_NLE), inc=False)
            wr = lambda s: dict(LAv=LAs[s], LAB=LABs[s], AAv=AAs[s],
                                AAB=AABs[s], T1v=T1s[s], T1B=T1Bs[s],
                                HSB=HSBs[s], HSBB=HSBBs[s], t0=s * SC)
            C = [wr(0), wr(1)]
            for s in range(2):
                c = C[s]
                E("vector",
                  (lambda t0_, LAv_: lambda e: e.tensor_tensor(
                      out=LAv_[:], in0=Ib[:, :, t0_:t0_ + SC],
                      in1=XCB[:, :, t0_:t0_ + SC], op=ALU.mult))(c["t0"], c["LAv"]),
                  reads=[IBS[s], XCBB], writes=[c["LAB"]])
            for s in range(2):
                c = C[s]
                for ot in range(KO):
                    E("scalar",
                      (lambda i_, ot_, t0_, AAv_: lambda e: e.activation(
                          out=AAv_[:, ot_, :], in_=Rb[:, ot_, t0_:t0_ + SC],
                          func=AF.Exp,
                          scale=m8ls[:, i_ * 2 + ot_:i_ * 2 + ot_ + 1]))(
                              i, ot, c["t0"], c["AAv"]),
                      reads=[RBS[s]], sreads=[M8],
                      writes=([c["AAB"]] if ot == KO - 1 else []),
                      writes_pre=([c["AAB"]] if ot == 0 else []), inc=True)
            for s in range(2):
                c = C[s]
                E("scalar",
                  (lambda AAv_, T1v_: lambda e: e.activation(
                      out=T1v_[:], in_=AAv_[:], func=AF.Square))(c["AAv"], c["T1v"]),
                  reads=[c["AAB"]], writes=[c["T1B"]])
            for s in range(2):
                c = C[s]
                E("vector",
                  (lambda T1v_: lambda e: e.tensor_scalar(
                      out=T1v_[:], in0=T1v_[:], scalar1=-1.0, scalar2=1.0,
                      op0=ALU.mult, op1=ALU.add))(c["T1v"]),
                  writes=[c["T1B"]])
                E("vector",
                  (lambda T1v_: lambda e: e.tensor_scalar(
                      out=T1v_[:], in0=T1v_[:], scalar1=1e-38, scalar2=None,
                      op0=ALU.max))(c["T1v"]),
                  writes=[c["T1B"]])
            for s in range(2):
                c = C[s]
                E("scalar",
                  (lambda T1v_: lambda e: e.activation(
                      out=T1v_[:], in_=T1v_[:], func=AF.Ln))(c["T1v"]),
                  reads=[VB], writes=[c["T1B"]])
                E("scalar",
                  (lambda T1v_: lambda e: e.activation(
                      out=T1v_[:], in_=T1v_[:], func=AF.Exp, scale=0.5))(c["T1v"]),
                  writes=[c["T1B"]])
            if q == 0 and i == 0:
                tap("a00", AAs[0][:, 0, :], [128, SC], reads=[AABs[0]])
                tap("gate00", T1s[0][:, 0, :], [128, SC], reads=[T1Bs[0]])
            for s in range(2):
                c = C[s]
                E("vector",
                  (lambda LAv_, T1v_: lambda e: e.tensor_tensor(
                      out=LAv_[:], in0=LAv_[:], in1=T1v_[:], op=ALU.mult))(
                          c["LAv"], c["T1v"]),
                  reads=[c["T1B"]], writes=[c["LAB"]])
            if q == 0 and i == 0:
                tap("b00", LAs[0][:, 0, :], [128, SC], BF16, reads=[LABs[0]])
            for s in range(2):
                c = C[s]
                for ot in range(KO):
                    E("vector",
                      (lambda i_, ot_, AAv_, LAv_, HSB_: lambda e:
                       e.tensor_tensor_scan(
                          out=HSB_[:, ot_, :], data0=AAv_[:, ot_, :],
                          data1=LAv_[:, ot_, :],
                          initial=state[:, i_ * 2 + ot_:i_ * 2 + ot_ + 1],
                          op0=ALU.mult, op1=ALU.add))(
                              i, ot, c["AAv"], c["LAv"], c["HSB"]),
                      reads=[c["AAB"], c["LAB"]], sreads=[ST],
                      writes=([c["HSBB"]] if ot == KO - 1 else []),
                      writes_pre=([c["HSBB"]] if ot == 0 else []), inc=True)
                    E("vector",
                      (lambda i_, ot_, HSB_: lambda e: e.tensor_copy(
                          out=state[:, i_ * 2 + ot_:i_ * 2 + ot_ + 1],
                          in_=HSB_[:, ot_, SC - 1:SC]))(i, ot, c["HSB"]),
                      writes=[ST])
            if q == 0 and i == 0:
                tap("hs00", HSBs[0][:, 0, :], [128, SC], BF16, reads=[HSBBs[0]])
                tap("r00", Rb[:, 0, 0:SC], [128, SC], BF16, reads=[RBS[0]])
                tap("xcb00", XCB[:, 0, 0:SC], [128, SC], BF16, reads=[XCBB])
            for s in range(2):
                c = C[s]
                for ot in range(KO):
                    bank = 4 + 2 * s + ot
                    for kt in range(KO):
                        E("tensor",
                          (lambda kt_, ot_, bank_, HSB_: lambda e: e.matmul(
                              PBT[bank_][:, 0:SC],
                              owTb[:, wr_off(kt_, ot_):wr_off(kt_, ot_) + 128],
                              HSB_[:, kt_, :],
                              start=(kt_ == 0), stop=(kt_ == KO - 1)))(
                                  kt, ot, bank, c["HSB"]),
                          reads=[c["HSBB"], OWB_],
                          writes_pre=([PB[bank]] if kt == 0 else []),
                          writes=([PB[bank]] if kt == KO - 1 else []),
                          inc=(kt == KO - 1))
            for s in range(2):
                c = C[s]
                for ot in range(KO):
                    bank = 4 + 2 * s + ot
                    E("scalar",
                      (lambda ot_, bank_, s_: lambda e: e.activation(
                          out=OSQ[:, ot_, s_ * SC // 2:s_ * SC // 2 + SC],
                          in_=PBT[bank_][:, 0:SC],
                          func=AF.Square))(ot, bank, 0),
                      reads=[PB[bank]],
                      writes=([OSQB] if ot == KO - 1 else []),
                      writes_pre=(([OSQB] + XBRS + XSQS) if ot == 0 else []),
                      inc=True) if False else None
                    E("scalar",
                      (lambda ot_, bank_: lambda e: e.activation(
                          out=OSQ[:, ot_, :], in_=PBT[bank_][:, 0:SC],
                          func=AF.Square))(ot, bank),
                      reads=[PB[bank]],
                      writes=([OSQB] if ot == KO - 1 else []),
                      writes_pre=(([OSQB] + XBRS + XSQS) if ot == 0 else []),
                      inc=True)
                    E("scalar",
                      (lambda ot_, bank_, AAv_: lambda e: e.activation(
                          out=AAv_[:, ot_, :], in_=PBT[bank_][:, 0:SC],
                          func=AF.Identity))(ot, bank, c["AAv"]),
                      reads=[PB[bank]],
                      writes=([c["AAB"]] if ot == KO - 1 else []),
                      writes_pre=([c["AAB"]] if ot == 0 else []), inc=True)
                rowrms = rowscr if s == 0 else rowscr3
                RSX = RS1 if s == 0 else RS3
                for ot in range(KO):
                    E("tensor",
                      (lambda ot_, s_: lambda e: e.matmul(
                          PBT[3][32 * s_:32 * s_ + 1, 0:SC], onescol_bf,
                          OSQ[:, ot_, :],
                          start=(ot_ == 0), stop=(ot_ == KO - 1)))(ot, s),
                      reads=[OSQB, CB], writes_pre=([PB[3]] if ot == 0 else []),
                      writes=([PB[3]] if ot == KO - 1 else []), inc=(ot == KO - 1))
                E("vector",
                  (lambda rr, s_: lambda e: e.tensor_scalar(
                      out=rr[:], in0=PBT[3][32 * s_:32 * s_ + 1, 0:SC],
                      scalar1=1.0 / DO, scalar2=None, op0=ALU.mult))(rowrms, s),
                  reads=[PB[3]], writes=[RSX])
                E("scalar",
                  (lambda rr: lambda e: e.activation(
                      out=rr[:], in_=rr[:], func=AF.Ln,
                      bias=vrow0("epsrms")))(rowrms),
                  reads=[RSX, VB], writes=[RSX])
                E("scalar",
                  (lambda rr: lambda e: e.activation(
                      out=rr[:], in_=rr[:], func=AF.Exp, scale=-0.5))(rowrms),
                  writes=[RSX])
                pe_drain()
                E("tensor",
                  (lambda rr, s_: lambda e: e.matmul(
                      PBT[2][:, 0:SC], onesrowf[:], rr[:],
                      start=True, stop=True))(rowrms, s),
                  reads=[RSX, ORF], writes_pre=[PB[2]], writes=[PB[2]])
                for ot in range(KO):
                    E("vector",
                      (lambda i_, ot_, t0_, OUT_, AAv_: lambda e:
                       e.scalar_tensor_tensor(
                          out=OUT_[:, ot_, 3 + t0_:3 + t0_ + SC],
                          in0=AAv_[:, ot_, :],
                          scalar=vcol("nwv", i_ * 2 + ot_), in1=PBT[2][:, 0:SC],
                          op0=ALU.mult, op1=ALU.mult))(i, ot, c["t0"], OUT,
                                                       c["AAv"]),
                      reads=[c["AAB"], VB, PB[2]],
                      writes=([OUTB] if ot == KO - 1 else []),
                      writes_pre=([OUTB] if ot == 0 else []), inc=True)
            if q == 0:
                tap(f"xn0L{i}", OUT[:, 0, 3:3 + SC], [128, SC], reads=[OUTB])
            E("vector",
              (lambda i_, IN_: lambda e: e.tensor_copy(
                  out=tails[:, i_, :, :], in_=IN_[:, :, QT:QT + 3]))(i, IN),
              reads=[INB], writes=[TL])

        def up_resid(q, s):
            t0 = s * SC
            h = q % 2
            E("scalar",
              (lambda t0_: lambda e: e.copy(
                  out=XCB[:, :, t0_:t0_ + SC],
                  in_=XRB[:, :, 3 + t0_:3 + t0_ + SC]))(t0),
              reads=[XRBB], writes=[XCBB])
            for dt in range(KD):
                bank = dt % 2
                for kt in range(KO):
                    E("tensor",
                      (lambda kt_, dt_, t0_, bank_: lambda e: e.matmul(
                          PBT[bank_][:, 0:SC],
                          upTb[:, kt_ * D + dt_ * 128:kt_ * D + (dt_ + 1) * 128],
                          XCB[:, kt_, t0_:t0_ + SC],
                          start=(kt_ == 0), stop=(kt_ == KO - 1)))(kt, dt, t0, bank),
                      reads=[XCBB, UPB],
                      writes_pre=([PB[bank]] if kt == 0 else []),
                      writes=([PB[bank]] if kt == KO - 1 else []),
                      inc=(kt == KO - 1))
                E("vector",
                  (lambda dt_, t0_, h_, bank_: lambda e: e.scalar_tensor_tensor(
                      out=X2t[:, dt_, h_ * QT + t0_:h_ * QT + t0_ + SC],
                      in0=PBT[bank_][:, 0:SC], scalar=modd[:, 16 + dt_:17 + dt_],
                      in1=XBt[:, dt_, t0_:t0_ + SC], op0=ALU.mult, op1=ALU.add))(
                          dt, t0, h, dt % 2),
                  reads=[PB[dt % 2], XBS[s]], sreads=[MODD],
                  writes=([X2B] if dt == KD - 1 else []),
                  writes_pre=([X2B] if dt == 0 else []), inc=True)
            if q == 0 and s == 0:
                tap("x2q0", X2t[:, 0, 0:SC], [128, SC], reads=[X2B])

        def s1_quarter(q, filler=None):
            for s in range(2):
                ln1_down(q, s)
            if filler:
                filler(3)
            for i in range(NL):
                layer(q, i)
                if filler:
                    filler(3)
            for s in range(2):
                up_resid(q, s)
                if q + 1 < NQ:
                    dma(XBt[:, :, s * SC:(s + 1) * SC],
                        x3[:, :, (q + 1) * QT + s * SC:(q + 1) * QT + (s + 1) * SC],
                        writes=[XBS[s]], q="qx0" if s == 0 else "qx1")

        X2NB, HHB = B_("x2n"), B_("hh")
        U32B = B_("u32a")
        X2BS = [B_("x2br0"), B_("x2br1")]
        X2SQS = [B_("x2sq0"), B_("x2sq1")]
        Z16B, ZTMB, MM1B, TSCB, MEMB = (B_("z16"), B_("ztm"), B_("mm1"),
                                        B_("tsc"), B_("mask_em"))
        MM1BB = B_("mm1b")
        w1bufb = genA[:].bitcast(BF16).rearrange("p (a n) -> p a n", a=2)
        w2bufb = genC[:].bitcast(BF16).rearrange("p (a n) -> p a n", a=2)
        W1SL = [B_("w1sl0"), B_("w1sl1")]
        W2SL = [B_("w2sl0"), B_("w2sl1")]

        def s2_chunk(q, c, first=False):
            h = q % 2
            t0 = h * QT + c * SC
            gtok = q * QT + c * SC
            for kt in range(KD):
                sl = kt % 2
                E("scalar",
                  (lambda kt_, sl_, t0_: lambda e: e.copy(
                      out=x2br[:, sl_, :], in_=X2t[:, kt_, t0_:t0_ + SC]))(kt, sl, t0),
                  reads=[X2B], writes=[X2BS[sl]])
                E("scalar",
                  (lambda kt_, sl_, t0_: lambda e: e.activation(
                      out=x2sqr[:, sl_, :], in_=X2t[:, kt_, t0_:t0_ + SC],
                      func=AF.Square))(kt, sl, t0),
                  reads=[X2B], writes=[X2SQS[sl]])
                E("tensor",
                  (lambda kt_, sl_: lambda e: e.matmul(
                      PBT[3][0:1, 0:SC], onescol_bf, x2br[:, sl_, :],
                      start=(kt_ == 0), stop=(kt_ == KD - 1)))(kt, sl),
                  reads=[X2BS[sl], CB], writes_pre=([PB[3]] if kt == 0 else []),
                  writes=([PB[3]] if kt == KD - 1 else []), inc=True)
                E("tensor",
                  (lambda kt_, sl_: lambda e: e.matmul(
                      PBT[7][0:1, 0:SC], onescol_bf, x2sqr[:, sl_, :],
                      start=(kt_ == 0), stop=(kt_ == KD - 1)))(kt, sl),
                  reads=[X2SQS[sl], CB], writes_pre=([PB[7]] if kt == 0 else []),
                  writes=([PB[7]] if kt == KD - 1 else []), inc=True)
                E("tensor",
                  (lambda kt_, t0_: lambda e: e.matmul(
                      PBT[6][0:16, 0:SC], rwTp[:, kt_ * NB:(kt_ + 1) * NB],
                      X2t[:, kt_, t0_:t0_ + SC],
                      start=(kt_ == 0), stop=(kt_ == KD - 1)))(kt, t0),
                  reads=[X2B, RWP], writes_pre=([PB[6]] if kt == 0 else []),
                  writes=([PB[6]] if kt == KD - 1 else []), inc=True)
            E("vector", lambda e: e.tensor_scalar(out=rowscr[:], in0=PBT[3][0:1, 0:SC],
                                                  scalar1=1.0 / D, scalar2=None,
                                                  op0=ALU.mult),
              reads=[PB[3]], writes=[RS1])
            E("scalar", lambda e: e.activation(out=rowscr2[:], in_=rowscr[:],
                                               func=AF.Square),
              reads=[RS1], writes=[RS2])
            E("vector", lambda e: e.scalar_tensor_tensor(
                out=rowscr2[:], in0=PBT[7][0:1, 0:SC], scalar=1.0 / D,
                in1=rowscr2[:], op0=ALU.mult, op1=ALU.subtract),
              reads=[PB[7]], writes=[RS2])
            E("scalar", lambda e: e.activation(out=rowscr2[:], in_=rowscr2[:],
                                               func=AF.Ln, bias=vrow0("epsln")),
              reads=[VB], writes=[RS2])
            E("scalar", lambda e: e.activation(out=rowscr2[:], in_=rowscr2[:],
                                               func=AF.Exp, scale=-0.5),
              writes=[RS2])
            E("vector", lambda e: e.tensor_tensor(out=rowscr[:], in0=rowscr[:],
                                                  in1=rowscr2[:], op=ALU.mult),
              reads=[RS2], writes=[RS1])
            pe_drain()
            E("tensor", lambda e: e.matmul(PBT[2][:, 0:SC], onesrowf[:], rowscr2[:],
                                           start=True, stop=True),
              reads=[RS2, ORF], writes_pre=[PB[2]], writes=[PB[2]])
            E("tensor", lambda e: e.matmul(PBT[5][:, 0:SC], onesrowf[:], rowscr[:],
                                           start=True, stop=True),
              reads=[RS1, ORF], writes_pre=[PB[5]], writes=[PB[5]])
            for hbk in range(2):
                E("vector",
                  (lambda hbk_, t0_: lambda e: e.tensor_tensor(
                      out=u32a[:], in0=X2t[:, hbk_ * 4:(hbk_ + 1) * 4, t0_:t0_ + SC],
                      in1=PBT[2][:, 0:SC].unsqueeze(1).broadcast_to([128, 4, SC]),
                      op=ALU.mult))(hbk, t0),
                  reads=[X2B, PB[2]], writes=[U32B])
                E("vector",
                  (lambda hbk_: lambda e: e.tensor_tensor(
                      out=x2n[:, hbk_ * 4:(hbk_ + 1) * 4, :], in0=u32a[:],
                      in1=PBT[5][:, 0:SC].unsqueeze(1).broadcast_to([128, 4, SC]),
                      op=ALU.subtract))(hbk),
                  reads=[U32B, PB[5]],
                  writes=([X2NB] if hbk == 1 else []),
                  writes_pre=([X2NB] if hbk == 0 else []), inc=True)
            E("scalar", lambda e: e.activation(out=z16[0:16, :],
                                               in_=PBT[6][0:16, 0:SC],
                                               func=AF.Identity),
              reads=[PB[6]], writes=[Z16B])
            E("vector", lambda e: e.tensor_tensor(out=z16[0:16, :], in0=z16[0:16, :],
                                                  in1=PBT[2][0:16, 0:SC],
                                                  op=ALU.mult),
              reads=[PB[2]], writes=[Z16B])
            E("vector", lambda e: e.scalar_tensor_tensor(
                out=z16[0:16, :], in0=PBT[5][0:16, 0:SC], scalar=njr16[:, 0:1],
                in1=z16[0:16, :], op0=ALU.mult, op1=ALU.add),
              reads=[PB[5]], sreads=[NJR], writes=[Z16B])
            E("vector", lambda e: e.tensor_scalar(out=z16[0:16, :], in0=z16[0:16, :],
                                                  scalar1=kb16[:, 0:1], scalar2=None,
                                                  op0=ALU.add),
              sreads=[KB16], writes=[Z16B])
            for tt in range(4):
                pe_drain()
                E("tensor",
                  (lambda tt_: lambda e: e.matmul(
                      PBT[4][:, tt_ * 16:(tt_ + 1) * 16],
                      z16[0:16, tt_ * 128:(tt_ + 1) * 128], ident[0:16, 0:16],
                      start=True, stop=True))(tt),
                  reads=[Z16B, VB], writes_pre=([PB[4]] if tt == 0 else []),
                  writes=([PB[4]] if tt == 3 else []), inc=True)
            E("vector", lambda e: e.tensor_copy(
                out=ztm[:].rearrange("p a e -> p (a e)"), in_=PBT[4][:, 0:64]),
              reads=[PB[4]], writes=[ZTMB])
            E("vector", lambda e: e.memset(mm1[:].rearrange("p a e -> p (a e)"),
                                           -1.0),
              writes=[MM1B])
            for rnd in range(TOPK):
                E("vector", lambda e: e.tensor_reduce(out=scr8[:, 8:12], in_=ztm[:],
                                                      axis=AX, op=ALU.max),
                  reads=[ZTMB], writes=[SCR])
                E("vector", lambda e: e.tensor_tensor(
                    out=tsc[:], in0=ztm[:],
                    in1=scr8[:, 8:12].unsqueeze(2).broadcast_to([128, 4, 16]),
                    op=ALU.is_ge),
                  reads=[SCR], writes=[TSCB])
                E("vector", lambda e: e.scalar_tensor_tensor(
                    out=tsc[:], in0=tsc[:], scalar=-99.0,
                    in1=vcols("iota99", 0, 16).unsqueeze(1)
                    .broadcast_to([128, 4, 16]),
                    op0=ALU.mult, op1=ALU.add),
                  reads=[VB], writes=[TSCB])
                E("vector", lambda e: e.tensor_reduce(out=scr8[:, 12:16], in_=tsc[:],
                                                      axis=AX, op=ALU.min),
                  reads=[TSCB], writes=[SCR])
                E("vector", lambda e: e.tensor_tensor(
                    out=tsc[:],
                    in0=vcols("iota16", 0, 16).unsqueeze(1)
                    .broadcast_to([128, 4, 16]),
                    in1=scr8[:, 12:16].unsqueeze(2).broadcast_to([128, 4, 16]),
                    op=ALU.is_equal),
                  reads=[SCR, VB], writes=[TSCB])
                E("vector", lambda e: e.tensor_tensor(out=mm1[:], in0=mm1[:],
                                                      in1=tsc[:], op=ALU.add),
                  reads=[TSCB], writes=[MM1B])
                if rnd < TOPK - 1:
                    E("vector", lambda e: e.scalar_tensor_tensor(
                        out=ztm[:], in0=tsc[:], scalar=-BIGNEG, in1=ztm[:],
                        op0=ALU.mult, op1=ALU.add),
                      reads=[TSCB], writes=[ZTMB])
            E("vector", lambda e: e.tensor_copy(
                out=mm1b.rearrange("p (a e) -> p a e", a=4), in_=mm1[:]),
              reads=[MM1B], writes=[MM1BB])
            for tt in range(4):
                pe_drain()
                E("tensor",
                  (lambda tt_: lambda e: e.matmul(
                      PBT[4][0:16, tt_ * 128:(tt_ + 1) * 128],
                      mm1b.rearrange("p (a e) -> p a e", a=4)[:, tt_, :],
                      identb[:], start=True, stop=True))(tt),
                  reads=[MM1BB, IDB], writes_pre=([PB[4]] if tt == 0 else []),
                  writes=([PB[4]] if tt == 3 else []), inc=True)
            E("scalar", lambda e: e.activation(out=mask_em[0:16, :],
                                               in_=PBT[4][0:16, 0:512],
                                               func=AF.Identity),
              reads=[PB[4]], writes=[MEMB])
            if q == 0 and c == 0:
                tap("z00", z16[0:16, :], [16, SC], reads=[Z16B])
                tap("mask00", mm1[:].rearrange("p a e -> p (a e)"), [128, 64],
                    reads=[MM1B])
                tap("x2n00", x2n[:, 0, :], [128, SC], BF16, reads=[X2NB])
            for j in range(FT):
                sl = j % 2
                dma(w1bufb[:, sl, :], w1s[:, j * 1024:(j + 1) * 1024],
                    reads=([W1S, GA, GC] if (first and j < 2) else [W1S]),
                    writes=[W1SL[sl]], q="qw10" if sl == 0 else "qw11")
                bank = j % 4
                blk = j // 2
                for kt in range(KD):
                    E("tensor",
                      (lambda sl_, kt_, bank_: lambda e: e.matmul(
                          PBT[bank_][:, 0:SC],
                          w1bufb[:, sl_, kt_ * 128:(kt_ + 1) * 128],
                          x2n[:, kt_, :],
                          start=(kt_ == 0), stop=False))(sl, kt, bank),
                      reads=[W1SL[sl], X2NB],
                      writes_pre=([PB[bank]] if kt == 0 else []),
                      inc=(kt == KD - 1))
                E("tensor",
                  (lambda blk_, bank_: lambda e: e.matmul(
                      PBT[bank_][:, 0:SC], bigsel[:, blk_ * 128:(blk_ + 1) * 128],
                      mask_em[0:16, :], start=False, stop=True))(blk, bank),
                  reads=[MEMB, BS], writes=[PB[bank]], inc=True)
                E("scalar",
                  (lambda j_, bank_: lambda e: e.activation(
                      out=hh[:, j_, :], in_=PBT[bank_][:, 0:SC], func=AF.Relu,
                      bias=KB1t[:, j_:j_ + 1]))(j, bank),
                  reads=[PB[bank], KB1B],
                  writes=([HHB] if j == FT - 1 else []),
                  writes_pre=([HHB] if j == 0 else []), inc=True)
            for ot in range(FT):
                sl = ot % 2
                dma(w2bufb[:, sl, :], w2s[:, ot * 1024:(ot + 1) * 1024],
                    reads=([W2S, GC] if (first and ot < 2) else [W2S]),
                    writes=[W2SL[sl]], q="qw20" if sl == 0 else "qw21")
                for dt in range(KD):
                    E("tensor",
                      (lambda sl_, ot_, dt_: lambda e: e.matmul(
                          PBT[dt_][:, 0:SC],
                          w2bufb[:, sl_, dt_ * 128:(dt_ + 1) * 128],
                          hh[:, ot_, :],
                          start=(ot_ == 0), stop=False))(sl, ot, dt),
                      reads=[W2SL[sl], HHB],
                      writes_pre=([PB[dt]] if ot == 0 else []),
                      inc=(dt == KD - 1))
            for dt in range(KD):
                E("tensor",
                  (lambda dt_: lambda e: e.matmul(
                      PBT[dt_][:, 0:SC], b2row[0:1, dt_ * 128:(dt_ + 1) * 128],
                      onesrow512_bf, start=False, stop=True))(dt),
                  reads=[B2R, CB], writes=[PB[dt]], inc=True)
            if q == 0 and c == 0:
                tap("hh00", hh[:, 0, :], [128, SC], BF16, reads=[HHB])
            for dt in range(KD):
                E("vector",
                  (lambda dt_, t0_: lambda e: e.scalar_tensor_tensor(
                      out=X2t[:, dt_, t0_:t0_ + SC], in0=PBT[dt_][:, 0:SC],
                      scalar=modd[:, 24 + dt_:25 + dt_],
                      in1=X2t[:, dt_, t0_:t0_ + SC], op0=ALU.mult,
                      op1=ALU.add))(dt, t0),
                  reads=[PB[dt]], sreads=[MODD],
                  writes=([X2B] if dt == KD - 1 else []),
                  writes_pre=([X2B] if dt == 0 else []), inc=True)
            dma(o3[:, :, gtok:gtok + SC], X2t[:, :, t0:t0 + SC], reads=[X2B], q="qo")

        steps = [
            lambda: s1_quarter(0, filler=p0b_fill),
            lambda: s1_quarter(1, filler=p0b_fill),
            lambda: (p0b_fill(1000), kb.barrier()),
            lambda: s2_chunk(0, 0, first=True),
            lambda: s2_chunk(0, 1),
            lambda: kb.barrier(),
            lambda: s1_quarter(2),
            lambda: kb.barrier(),
            lambda: s2_chunk(1, 0),
            lambda: s2_chunk(1, 1),
            lambda: kb.barrier(),
            lambda: s1_quarter(3),
            lambda: kb.barrier(),
            lambda: s2_chunk(2, 0),
            lambda: s2_chunk(2, 1),
            lambda: s2_chunk(3, 0),
            lambda: s2_chunk(3, 1),
        ]
        import os as _os
        lim = int(_os.environ.get("KMK_STEPS", str(len(steps))))
        for st in steps[:lim]:
            st()

        kb.raw_wait("sync", [(qq, kb.cnt[qq]) for qq in QSEMS
                             if kb.cnt[qq] > 0], hard=True)

        sems = {}
        for e in list(ENGS) + list(QSEMS):
            sems[e] = ctx.enter_context(nc.semaphore("sem_" + e))
        kb.replay(nc, sems)

    return nc



def _host_vec(c_row, conv_w, conv_b, wr_b, wi_b, log_a, norm_w, b1, b2,
              router_b):
    v = np.zeros((128, VCOLS), np.float32)
    C = VEC_COLS
    v[:, C["ident"]:C["ident"] + 128] = np.eye(128, dtype=np.float32)
    v[:, C["ones"]] = 1.0
    v[:, C["epsln"]] = EPS_LN
    v[:, C["epsrms"]] = EPS_RMS
    for j in range(16):
        v[:, C["iota16"] + j] = float(j)
        v[:, C["iota99"] + j] = float(j) + 99.0
    v[:, C["c8"]:C["c8"] + 8] = c_row.reshape(8, 128).T
    for i in range(NL):
        for k in range(4):
            for ot in range(KO):
                v[:, C["cw"] + i * 8 + k * 2 + ot] = \
                    conv_w[i, ot * 128:(ot + 1) * 128, k]
        for ot in range(KO):
            v[:, C["cbv"] + i * 2 + ot] = conv_b[i, ot * 128:(ot + 1) * 128]
            v[:, C["wrb"] + i * 2 + ot] = wr_b[i, ot * 128:(ot + 1) * 128]
            v[:, C["wib"] + i * 2 + ot] = wi_b[i, ot * 128:(ot + 1) * 128]
            v[:, C["lav"] + i * 2 + ot] = log_a[i, ot * 128:(ot + 1) * 128]
            v[:, C["nwv"] + i * 2 + ot] = norm_w[i, ot * 128:(ot + 1) * 128]
    v[:, C["b1c"]:C["b1c"] + 32] = b1.reshape(32, 128).T
    v[:, C["b2c"]:C["b2c"] + 8] = b2.reshape(8, 128).T
    v[0:16, C["rbc"]] = router_b
    return v


def _host_const():
    cpk = np.zeros((128, CPK_COLS), np.float32)
    cpk[:, CPK_ONES:CPK_ONES + 512] = 1.0
    cpk[:, CPK_BIG:CPK_BIG + 128] = BIGNEG
    bigsel = np.zeros((16, NB * 128), np.float32)
    for j in range(NB):
        bigsel[j, j * 128:(j + 1) * 128] = BIGNEG
    return cpk.astype(NPBF16), bigsel.astype(NPBF16)


def kernel(L0_attn_h, L0_attn_down_w, L0_attn_up_w, L0_attn_conv_w,
           L0_attn_conv_b, L0_attn_Wr_w, L0_attn_Wr_b, L0_attn_Wi_w,
           L0_attn_Wi_b, L0_attn_log_a, L0_attn_out_w, L0_attn_norm_w,
           L0_ffn_router_w, L0_ffn_router_b, L0_ffn_w1, L0_ffn_b1,
           L0_ffn_w2, L0_ffn_b2, L0_c, L0_wmod):
    f32 = np.float32
    asc = np.ascontiguousarray
    wmod_a = asc(np.asarray(L0_wmod, f32))
    dwT_a = asc(np.asarray(L0_attn_down_w, f32).T)
    upT_a = asc(np.asarray(L0_attn_up_w, f32).T)
    wrT_a = asc(np.asarray(L0_attn_Wr_w, f32).transpose(0, 2, 1))
    wiT_a = asc(np.asarray(L0_attn_Wi_w, f32).transpose(0, 2, 1))
    owT_a = asc(np.asarray(L0_attn_out_w, f32).transpose(0, 2, 1))
    w1T_a = asc(np.asarray(L0_ffn_w1, f32).T)
    w2T_a = asc(np.asarray(L0_ffn_w2, f32).T)
    rwT_a = asc(np.asarray(L0_ffn_router_w, f32).T)
    cpk_a, bigsel_a = _host_const()
    x = np.asarray(L0_attn_h, f32)

    nc = build_nc()
    in_maps = []
    for b in range(B):
        vec = _host_vec(np.asarray(L0_c, f32)[b], np.asarray(L0_attn_conv_w, f32),
                        np.asarray(L0_attn_conv_b, f32),
                        np.asarray(L0_attn_Wr_b, f32),
                        np.asarray(L0_attn_Wi_b, f32),
                        np.asarray(L0_attn_log_a, f32),
                        np.asarray(L0_attn_norm_w, f32),
                        np.asarray(L0_ffn_b1, f32), np.asarray(L0_ffn_b2, f32),
                        np.asarray(L0_ffn_router_b, f32))
        in_maps.append({
            "xT": asc(x[b].T),
            "wmod": wmod_a, "dwT": dwT_a, "upT": upT_a,
            "wrT": wrT_a, "wiT": wiT_a, "owT": owT_a,
            "w1T": w1T_a, "w2T": w2T_a, "rwT": rwT_a,
            "vec": vec, "cpk": cpk_a, "bigsel": bigsel_a,
        })
    res = run_bass_kernel_spmd(nc, in_maps, core_ids=list(range(B)))
    out = np.stack([np.ascontiguousarray(r["outT"].T) for r in res.results])
    if DEBUG_TAPS:
        kernel.last_results = res.results
    return out


# version 06 (final)
# speedup vs baseline: 1.0883x; 1.0571x over previous
import contextlib

import numpy as np
import ml_dtypes

import concourse.bass as bass
import concourse.mybir as mybir
from concourse.bass_utils import run_bass_kernel_spmd

F32 = mybir.dt.float32
BF16 = mybir.dt.bfloat16
AF = mybir.ActivationFunctionType
ALU = mybir.AluOpType
AX = mybir.AxisListType.X
NPBF16 = np.dtype(ml_dtypes.bfloat16)

B = 8
T = 4096
D = 1024
DO = 256
NL = 3
DF = 4096
NB = 16
TOPK = 4
QT = 1024
NQ = 4
SC = 512
KD = D // 128
KO = DO // 128
FT = DF // 128
EPS_LN = 1e-5
EPS_RMS = 1e-6
BIGNEG = 1e30

DEBUG_TAPS = []


def _veccols():
    cols = {}
    off = 0

    def put(name, n):
        nonlocal off
        cols[name] = off
        off += n

    put("ident", 128)
    put("ones", 1)
    put("epsln", 1)
    put("epsrms", 1)
    put("epsrmsd", 1)
    put("iota16", 16)
    put("iota99", 16)
    put("c8", 8)
    put("cw", 24)
    put("cbv", 6)
    put("wrb", 6)
    put("wib", 6)
    put("lav", 6)
    put("nwv", 6)
    put("b1c", 32)
    put("b2c", 8)
    put("rbc", 1)
    return cols, off


VEC_COLS, VCOLS = _veccols()
CPK_ONES = 0
CPK_BIG = 512
CPK_COLS = 640

ENGS = ("sync", "tensor", "vector", "scalar", "gpsimd")
QSEMS = ("qm", "qx0", "qx1", "qwm0", "qwm1", "qpb", "qpb2", "qs1", "qs2",
         "qw10", "qw11", "qw12", "qw20", "qw21", "qo")
FENCE = set()


class Buf:
    __slots__ = ("ready", "readers", "name")

    def __init__(self, name=""):
        self.ready = None
        self.readers = []
        self.name = name


class KB:

    def __init__(self):
        self.items = {e: [] for e in ENGS}
        self.cnt = {e: 0 for e in ENGS}
        self.cnt.update({q: 0 for q in QSEMS})
        self.lastw = {e: {} for e in ENGS}
        self.pending = {e: [] for e in ENGS}

    def op(self, eng, fn, reads=(), sreads=(), writes=(), writes_pre=(),
           inc=True, dma=False, q=None):
        assert not (writes and not inc), "final writes must inc"
        waits = []

        def add(tok, hard):
            if tok is None:
                return
            pe, v = tok
            if pe == eng and not hard and eng == "tensor":
                return
            lw = self.lastw[eng]
            if lw.get(pe, 0) >= v:
                return
            lw[pe] = v
            waits.append((pe, v))

        for b in reads:
            add(b.ready, False)
        for b in sreads:
            add(b.ready, True)
        for b in tuple(writes) + tuple(writes_pre):
            add(b.ready, False)
            for t in b.readers:
                add(t, False)
        step = 16 if dma else 1
        sem = q if dma else eng
        assert not dma or q in QSEMS, f"dma needs a stream sem, got {q}"
        if dma and q == "qm" and self.cnt["qm"] > 0:
            if self.lastw[eng].get("qm", 0) < self.cnt["qm"]:
                self.lastw[eng]["qm"] = self.cnt["qm"]
                waits.append(("qm", self.cnt["qm"]))
        tok = None
        if inc:
            self.cnt[sem] += step
            tok = (sem, self.cnt[sem])
        self.items[eng].append((waits, fn, inc, step, sem))
        if inc:
            for b in self.pending[eng]:
                b.readers.append(tok)
            self.pending[eng].clear()
            for b in reads:
                b.readers.append(tok)
            for b in sreads:
                b.readers.append(tok)
            for b in writes:
                b.ready = tok
                b.readers = []
        else:
            for b in reads:
                if b not in self.pending[eng]:
                    self.pending[eng].append(b)
            for b in sreads:
                if b not in self.pending[eng]:
                    self.pending[eng].append(b)
        return tok

    def raw_wait(self, eng, toks, hard=False):
        waits = []
        for (pe, v) in toks:
            lw = self.lastw[eng]
            if pe == eng and not hard:
                continue
            if lw.get(pe, 0) >= v:
                continue
            lw[pe] = v
            waits.append((pe, v))
        if waits:
            self.items[eng].append((waits, None, False, 1, eng))

    def barrier(self, engs=("tensor", "vector", "scalar", "gpsimd")):
        snap = [(e, self.cnt[e]) for e in ENGS if self.cnt[e] > 0]
        for e in engs:
            self.raw_wait(e, [t for t in snap if t[0] != e])

    def replay(self, nc, sems):
        def make(ename):
            def handler(eng):
                for idx, (waits, fn, inc, step, sem) in enumerate(
                        self.items[ename]):
                    for (pe, v) in waits:
                        if pe in FENCE:
                            v = self.cnt[pe]
                        eng.wait_ge(sems[pe], v)
                    if fn is not None:
                        try:
                            inst = fn(eng)
                        except Exception as ex:
                            raise RuntimeError(
                                f"replay fail {ename}[{idx}]: {ex}") from ex
                        if inc:
                            inst.then_inc(sems[sem], step)
            return handler

        with nc.Block() as block:
            block.sync(make("sync"))
            block.tensor(make("tensor"))
            block.vector(make("vector"))
            block.scalar(make("scalar"))
            block.gpsimd(make("gpsimd"))



def _act_set_id(nc, need):
    from concourse.hw_specs import get_activation_tables
    for idx, funcs in enumerate(get_activation_tables(nc.m.arch).values()):
        if all(f in funcs for f in need):
            return idx
    raise ValueError(f"no act table set with {need}")



def build_nc():
    nc = bass.Bass("TRN2", target_bir_lowering=False)
    kb = KB()
    SET_SILU = _act_set_id(nc, [AF.Silu])
    SET_NLE = _act_set_id(nc, [AF.Exp, AF.Ln, AF.Square, AF.Relu, AF.Identity,
                               AF.Abs])
    SET_SIG = _act_set_id(nc, [AF.Sigmoid])

    def preload(eng, set_id):
        return eng.add_instruction(mybir.InstLoadActFuncSet(
            name=nc.get_next_instruction_name(), act_func_set_id=set_id,
            ins=[], outs=[]))

    xT = nc.dram_tensor("xT", [D, T], F32, kind="ExternalInput").ap()
    wmod = nc.dram_tensor("wmod", [D, 6 * D], F32, kind="ExternalInput").ap()
    dwT = nc.dram_tensor("dwT", [D, DO], F32, kind="ExternalInput").ap()
    upT = nc.dram_tensor("upT", [DO, D], F32, kind="ExternalInput").ap()
    wrT = nc.dram_tensor("wrT", [NL, DO, DO], F32, kind="ExternalInput").ap()
    wiT = nc.dram_tensor("wiT", [NL, DO, DO], F32, kind="ExternalInput").ap()
    owT = nc.dram_tensor("owT", [NL, DO, DO], F32, kind="ExternalInput").ap()
    w1T = nc.dram_tensor("w1T", [D, DF], F32, kind="ExternalInput").ap()
    w2T = nc.dram_tensor("w2T", [DF, D], F32, kind="ExternalInput").ap()
    rwT = nc.dram_tensor("rwT", [D, NB], F32, kind="ExternalInput").ap()
    vecd = nc.dram_tensor("vec", [128, VCOLS], F32, kind="ExternalInput").ap()
    cpkd = nc.dram_tensor("cpk", [128, CPK_COLS], BF16, kind="ExternalInput").ap()
    bigseld = nc.dram_tensor("bigsel", [16, NB * 128], BF16,
                             kind="ExternalInput").ap()
    outT = nc.dram_tensor("outT", [D, T], F32, kind="ExternalOutput").ap()
    w1s = nc.dram_tensor("w1s", [128, FT * KD * 128], BF16, kind="Internal").ap()
    w2s = nc.dram_tensor("w2s", [128, FT * KD * 128], BF16, kind="Internal").ap()

    x3 = xT.rearrange("(a p) t -> p a t", p=128)
    o3 = outT.rearrange("(a p) t -> p a t", p=128)
    wmod3 = wmod.rearrange("(a p) n -> p a n", p=128)
    w1T3 = w1T.rearrange("(a p) n -> p a n", p=128)
    w2T3 = w2T.rearrange("(a p) n -> p a n", p=128)
    dwT3 = dwT.rearrange("(a p) n -> p a n", p=128)
    upT3 = upT.rearrange("(a p) n -> p a n", p=128)
    rwT3 = rwT.rearrange("(a p) n -> p a n", p=128)
    wrT3 = wrT.rearrange("i (a p) n -> i p a n", p=128)
    wiT3 = wiT.rearrange("i (a p) n -> i p a n", p=128)
    owT3 = owT.rearrange("i (a p) n -> i p a n", p=128)

    taps = {}
    ctx = contextlib.ExitStack()

    def sb(name, shape, dtype):
        return ctx.enter_context(nc.sbuf_tensor(name, shape, dtype))

    def psu(name, shape, dtype=F32):
        return ctx.enter_context(nc.psum_tensor(name, shape, dtype))

    with ctx:
        vec = sb("vec_s", [128, VCOLS], F32)
        cpk = sb("cpk_s", [128, CPK_COLS], BF16)
        bigsel = sb("bigsel_s", [16, NB * 128], BF16)
        dwTb = sb("dwTb", [128, KD * DO], BF16)
        upTb = sb("upTb", [128, KO * D], BF16)
        wrTb = sb("wrTb", [128, NL * KO * DO], BF16)
        wiTb = sb("wiTb", [128, NL * KO * DO], BF16)
        owTb = sb("owTb", [128, NL * KO * DO], BF16)
        rwTs = sb("rwTs", [128, KD * NB], F32)
        rwTp = sb("rwTp", [128, KD * NB], F32)
        mod48 = sb("mod48", [128, 48], F32)
        modd = sb("modd", [128, 32], F32)
        sapb = sb("sapb", [128, 8], BF16)
        sbpb = sb("sbpb", [128, 8], BF16)
        m8ls = sb("m8ls", [128, NL * KO], F32)
        cb1p = sb("cb1p", [128, KO], F32)
        kcol = sb("kcol", [128, KO], F32)
        scr8 = sb("scr8", [128, 24], F32)
        KB1t = sb("KB1", [128, FT], F32)
        negJrow = sb("negJrow", [1, DO], BF16)
        b2row = sb("b2row", [1, D], BF16)
        onesrowf = sb("onesrowf", [1, 128], F32)
        rowscr = sb("rowscr", [1, SC], F32)
        rowscr2 = sb("rowscr2", [1, SC], F32)
        rowscr3 = sb("rowscr3", [1, SC], F32)
        rowbf = sb("rowbf", [1, SC], BF16)
        kb16 = sb("kb16", [16, 1], F32)
        njr16 = sb("njr16", [16, 1], F32)
        tails = sb("tails", [128, NL, KO, 3], F32)
        state = sb("state", [128, NL * KO], F32)
        identb = sb("identb", [128, 128], BF16)
        genA = sb("genA", [128, 1024], F32)
        genB = sb("genB", [128, 512], F32)
        genC = sb("genC", [128, 1024], F32)
        XBt = sb("XB", [128, KD, QT], F32)
        X2t = sb("X2", [128, KD, 2 * QT], F32)
        AR = sb("AR", [128, 16000], F32)

        def arf(off_w, n_w):
            assert off_w + n_w <= 16000
            return AR[:, off_w:off_w + n_w]

        def arb(off_w, n_bf):
            assert off_w + (n_bf + 1) // 2 <= 16000
            return AR[:, off_w:off_w + (n_bf + 1) // 2].bitcast(BF16)

        p0_stgA = arf(0, 3072)
        p0_stgB = arf(3072, 3072)
        p0_stgbf = arb(6144, 6144)
        dw_stg = arf(0, KD * DO).rearrange("p (a n) -> p a n", a=KD)
        up_stg = arf(0, KO * D).rearrange("p (a n) -> p a n", a=KO)
        at_stg = arf(2048, KO * DO).rearrange("p (a n) -> p a n", a=KO)
        XRA = arf(0, KO * 1027).rearrange("p (o t) -> p o t", o=KO)
        XRB = arf(2054, KO * 1027).rearrange("p (o t) -> p o t", o=KO)
        LAs = [arb(4108, KO * SC).rearrange("p (o t) -> p o t", o=KO),
               arb(4620, KO * SC).rearrange("p (o t) -> p o t", o=KO)]
        AAs = [arf(5132, KO * SC).rearrange("p (o t) -> p o t", o=KO),
               arf(6156, KO * SC).rearrange("p (o t) -> p o t", o=KO)]
        T1s = [arf(7180, KO * SC).rearrange("p (o t) -> p o t", o=KO),
               arf(8204, KO * SC).rearrange("p (o t) -> p o t", o=KO)]
        rstdB = arf(9228, SC)
        XCB = arb(9740, KO * QT).rearrange("p (o t) -> p o t", o=KO)
        Rb = arb(10764, KO * QT).rearrange("p (o t) -> p o t", o=KO)
        Ib = arb(11788, KO * QT).rearrange("p (o t) -> p o t", o=KO)
        HSBs = [arb(12812, KO * SC).rearrange("p (o t) -> p o t", o=KO),
                arb(13324, KO * SC).rearrange("p (o t) -> p o t", o=KO)]
        xbr = arb(13836, 2 * SC).rearrange("p (s t) -> p s t", s=2)
        xsqr = arb(14348, 2 * SC).rearrange("p (s t) -> p s t", s=2)
        OSQ = arb(13836, KO * SC).rearrange("p (o t) -> p o t", o=KO)
        x2n = arb(0, KD * SC).rearrange("p (a t) -> p a t", a=KD)
        hh = arb(2048, FT * SC).rearrange("p (a t) -> p a t", a=FT)
        u32a = arf(10240, 4 * SC).rearrange("p (a t) -> p a t", a=4)
        x2br = arb(12288, 2 * SC).rearrange("p (s t) -> p s t", s=2)
        x2sqr = arb(12812, 2 * SC).rearrange("p (s t) -> p s t", s=2)
        z16 = arf(13324, SC)
        ztm = arf(13836, 64).rearrange("p (a e) -> p a e", a=4)
        mm1 = arf(13900, 64).rearrange("p (a e) -> p a e", a=4)
        tsc = arf(13964, 64).rearrange("p (a e) -> p a e", a=4)
        mask_em = arb(14028, 512)
        mm1b = arb(14284, 64)

        PBT = [psu(f"pb{k}", [128, 512], F32) for k in range(8)]

        bufs = {}

        def B_(name):
            if name not in bufs:
                bufs[name] = Buf(name)
            return bufs[name]

        PB = [B_(f"pb{k}") for k in range(8)]

        def vcol(name, i=0):
            c = VEC_COLS[name] + i
            return vec[:, c:c + 1]

        def vrow0(name, i=0):
            c = VEC_COLS[name] + i
            return vec[0:1, c:c + 1]

        def vcols(name, i, n):
            c = VEC_COLS[name] + i
            return vec[:, c:c + n]

        ident = vec[:, VEC_COLS["ident"]:VEC_COLS["ident"] + 128]
        onescol_bf = cpk[:, CPK_ONES:CPK_ONES + 1]
        onesrow512_bf = cpk[0:1, CPK_ONES:CPK_ONES + 512]

        E = kb.op

        def pe_drain():
            kb.raw_wait("tensor", [("tensor", kb.cnt["tensor"])], hard=True)

        ones11 = vec[0:1, VEC_COLS["ones"]:VEC_COLS["ones"] + 1]

        def dma(dst_ap, src_ap, reads=(), writes=(), writes_pre=(), q="qm"):
            return kb.op("sync",
                         (lambda d, s: (lambda e: e.dma_start(out=d, in_=s)))(
                             dst_ap, src_ap),
                         reads=reads, writes=writes, writes_pre=writes_pre,
                         inc=True, dma=True, q=q)

        def tap(name, src_ap, shape, dtype=F32, reads=()):
            if name not in DEBUG_TAPS:
                return
            t = nc.dram_tensor("tap_" + name, list(shape), dtype,
                               kind="ExternalOutput").ap()
            taps[name] = t
            dma(t, src_ap, reads=reads)

        VB, CB, BS = B_("vec"), B_("cpk"), B_("bigsel")
        dma(vec[:], vecd, writes=[VB])
        dma(cpk[:], cpkd, writes=[CB])
        dma(bigsel[:], bigseld, writes=[BS])
        XBS = [B_("XB0"), B_("XB1")]
        dma(XBt[:, :, 0:SC], x3[:, :, 0:SC], writes=[XBS[0]], q="qx0")
        dma(XBt[:, :, SC:QT], x3[:, :, SC:QT], writes=[XBS[1]], q="qx1")

        SCR = B_("scr8")
        SAPB, SBPB = B_("sapb"), B_("sbpb")
        E("scalar", lambda e: preload(e, SET_SILU), inc=False)
        E("scalar", lambda e: e.activation(out=scr8[:, 0:8], in_=vcols("c8", 0, 8),
                                           func=AF.Silu),
          reads=[VB], writes=[SCR])
        E("vector", lambda e: e.tensor_copy(out=sapb[:], in_=scr8[:, 0:8]),
          reads=[SCR], writes=[SAPB])

        stg = [B_("p0stgA"), B_("p0stgB")]
        stgb = [B_("p0stgbfA"), B_("p0stgbfB")]
        wm_stg = [p0_stgA[:, 0:1024].rearrange("p (a n) -> p a n", a=KD),
                  p0_stgB[:, 0:1024].rearrange("p (a n) -> p a n", a=KD)]
        wm_bf = [arb(6144, 1024).rearrange("p (a n) -> p a n", a=KD),
                 arb(6656, 1024).rearrange("p (a n) -> p a n", a=KD)]
        for oc in range(48):
            sl = oc % 2
            dma(wm_stg[sl], wmod3[:, :, oc * 128:(oc + 1) * 128],
                writes=[stg[sl]], q="qwm0" if sl == 0 else "qwm1")
            E("vector",
              (lambda sl_: lambda e: e.tensor_copy(
                  out=wm_bf[sl_], in_=wm_stg[sl_]))(sl),
              reads=[stg[sl]], writes=[stgb[sl]])
            pe_drain()
            for kt in range(KD):
                E("tensor",
                  (lambda occ, kt_, sl_: lambda e: e.matmul(
                      PBT[5][:, occ:occ + 1], wm_bf[sl_][:, kt_, :],
                      sapb[:, kt_:kt_ + 1],
                      start=(kt_ == 0), stop=(kt_ == KD - 1)))(oc, kt, sl),
                  reads=[stgb[sl], SAPB],
                  writes_pre=([PB[5]] if (kt == 0 and oc == 0) else []),
                  writes=([PB[5]] if (kt == KD - 1 and oc == 47) else []),
                  inc=(kt == KD - 1))
        MODP = B_("mod48")
        E("scalar", lambda e: e.activation(out=mod48[:], in_=PBT[5][:, 0:48],
                                           func=AF.Identity),
          reads=[PB[5]], writes=[MODP])
        tap("mod48", mod48[:], [128, 48], reads=[MODP])

        IDB = B_("identb")
        E("vector", lambda e: e.tensor_copy(out=identb[:], in_=ident),
          reads=[VB], writes=[IDB])
        MODD = B_("modd")
        E("vector", lambda e: e.tensor_scalar(out=modd[:, 0:8], in0=mod48[:, 8:16],
                                              scalar1=1.0, scalar2=None, op0=ALU.add),
          reads=[MODP], writes=[MODD])
        E("vector", lambda e: e.tensor_scalar(out=modd[:, 8:16], in0=mod48[:, 32:40],
                                              scalar1=1.0, scalar2=None, op0=ALU.add),
          reads=[MODP], writes=[MODD])
        for (src0, dstc) in ((16, 16), (40, 24)):
            E("vector", (lambda s0: lambda e: e.tensor_scalar(
                out=scr8[:, 8:16], in0=mod48[:, s0:s0 + 8], scalar1=0.0,
                scalar2=None, op0=ALU.is_ge))(src0),
              reads=[MODP], writes=[SCR])
            E("vector", lambda e: e.tensor_scalar(
                out=scr8[:, 8:16], in0=scr8[:, 8:16], scalar1=2.0, scalar2=-1.0,
                op0=ALU.mult, op1=ALU.add), writes=[SCR])
            E("scalar", (lambda s0: lambda e: e.activation(
                out=scr8[:, 16:24], in_=mod48[:, s0:s0 + 8], func=AF.Abs))(src0),
              reads=[MODP], writes=[SCR])
            E("vector", lambda e: e.tensor_scalar(
                out=scr8[:, 16:24], in0=scr8[:, 16:24], scalar1=0.25,
                scalar2=None, op0=ALU.max), writes=[SCR])
            E("vector", (lambda dc: lambda e: e.tensor_tensor(
                out=modd[:, dc:dc + 8], in0=scr8[:, 8:16], in1=scr8[:, 16:24],
                op=ALU.mult))(dstc),
              reads=[SCR], writes=[MODD])
        E("vector", lambda e: e.reciprocal(out=scr8[:, 8:16], in_=modd[:, 0:8]),
          reads=[MODD], writes=[SCR])
        E("vector", lambda e: e.tensor_tensor(out=scr8[:, 8:16], in0=scr8[:, 8:16],
                                              in1=mod48[:, 0:8], op=ALU.mult),
          reads=[MODP], writes=[SCR])
        E("vector", lambda e: e.tensor_copy(out=sapb[:], in_=scr8[:, 8:16]),
          reads=[SCR], writes=[SAPB])
        E("vector", lambda e: e.reciprocal(out=scr8[:, 16:24], in_=modd[:, 8:16]),
          reads=[MODD], writes=[SCR])
        E("vector", lambda e: e.tensor_tensor(out=scr8[:, 16:24], in0=scr8[:, 16:24],
                                              in1=mod48[:, 24:32], op=ALU.mult),
          reads=[MODP], writes=[SCR])
        E("vector", lambda e: e.tensor_copy(out=sbpb[:], in_=scr8[:, 16:24]),
          reads=[SCR], writes=[SBPB])

        M8 = B_("m8ls")
        E("scalar", lambda e: preload(e, SET_NLE), inc=False)
        E("scalar", lambda e: e.activation(out=scr8[:, 8:14], in_=vcols("lav", 0, 6),
                                           func=AF.Abs),
          reads=[VB], writes=[SCR])
        E("scalar", lambda e: e.activation(out=scr8[:, 14:20], in_=scr8[:, 8:14],
                                           func=AF.Exp, scale=-1.0),
          reads=[SCR], writes=[SCR])
        E("scalar", lambda e: e.activation(out=scr8[:, 14:20], in_=scr8[:, 14:20],
                                           func=AF.Ln, bias=vcol("ones")),
          reads=[VB], writes=[SCR])
        E("vector", lambda e: e.tensor_scalar(out=scr8[:, 8:14], in0=vcols("lav", 0, 6),
                                              scalar1=0.0, scalar2=None, op0=ALU.min),
          reads=[VB], writes=[SCR])
        E("vector", lambda e: e.tensor_tensor(out=scr8[:, 8:14], in0=scr8[:, 8:14],
                                              in1=scr8[:, 14:20], op=ALU.subtract),
          writes=[SCR])
        E("vector", lambda e: e.tensor_scalar(out=m8ls[:], in0=scr8[:, 8:14],
                                              scalar1=8.0, scalar2=None, op0=ALU.mult),
          reads=[SCR], writes=[M8])

        DWS, DWB = B_("dwstg"), B_("dwTb")
        dma(dw_stg, dwT3, writes=[DWS], writes_pre=[stg[0]])
        for kt in range(KD):
            E("vector",
              (lambda kt_: lambda e: e.tensor_scalar(
                  out=dwTb[:, kt_ * DO:(kt_ + 1) * DO], in0=dw_stg[:, kt_, :],
                  scalar1=modd[:, kt_:kt_ + 1], scalar2=None, op0=ALU.mult))(kt),
              reads=[DWS, MODD],
              writes=([DWB] if kt == KD - 1 else []),
              writes_pre=([DWB] if kt == 0 else []),
              inc=(kt == KD - 1))
        NJ = B_("negJrow")
        pe_drain()
        for kt in range(KD):
            E("tensor",
              (lambda kt_: lambda e: e.matmul(
                  PBT[3][0:1, 0:DO], onescol_bf,
                  dwTb[:, kt_ * DO:(kt_ + 1) * DO],
                  start=(kt_ == 0), stop=(kt_ == KD - 1)))(kt),
              reads=[DWB, CB], writes_pre=([PB[3]] if kt == 0 else []),
              writes=([PB[3]] if kt == KD - 1 else []), inc=(kt == KD - 1))
        E("scalar", lambda e: e.activation(out=negJrow[:], in_=PBT[3][0:1, 0:DO],
                                           func=AF.Identity, scale=-1.0),
          reads=[PB[3]], writes=[NJ])
        pe_drain()
        for kt in range(KD):
            E("tensor",
              (lambda kt_: lambda e: e.matmul(
                  PBT[4][0:1, 0:DO], sapb[:, kt_:kt_ + 1],
                  dwTb[:, kt_ * DO:(kt_ + 1) * DO],
                  start=(kt_ == 0), stop=(kt_ == KD - 1)))(kt),
              reads=[DWB, SAPB], writes_pre=([PB[4]] if kt == 0 else []),
              writes=([PB[4]] if kt == KD - 1 else []), inc=(kt == KD - 1))
        RS1, RS2 = B_("rowscr"), B_("rowscr2")
        RS3 = B_("rowscr3")
        E("scalar", lambda e: e.activation(out=rowscr[0:1, 0:DO],
                                           in_=PBT[4][0:1, 0:DO], func=AF.Identity),
          reads=[PB[4]], writes=[RS1])
        KC = B_("kcol")
        for ot in range(KO):
            pe_drain()
            E("tensor",
              (lambda ot_: lambda e: e.matmul(
                  PBT[4][:, 64 + ot_:64 + ot_ + 1],
                  rowscr[0:1, ot_ * 128:(ot_ + 1) * 128], ones11,
                  start=True, stop=True))(ot),
              reads=[RS1, VB], writes_pre=([PB[4]] if ot == 0 else []),
              writes=([PB[4]] if ot == KO - 1 else []), inc=True)
        E("vector", lambda e: e.tensor_copy(out=kcol[:], in_=PBT[4][:, 64:64 + KO]),
          reads=[PB[4]], writes=[KC])
        CB1 = B_("cb1p")
        for ot in range(KO):
            E("vector",
              (lambda ot_: lambda e: e.tensor_tensor(
                  out=scr8[:, 8 + ot_:9 + ot_], in0=vcol("cw", 0 * 8 + 0 * 2 + ot_),
                  in1=vcol("cw", 0 * 8 + 1 * 2 + ot_), op=ALU.add))(ot),
              reads=[VB], writes=[SCR])
            E("vector",
              (lambda ot_: lambda e: e.tensor_tensor(
                  out=scr8[:, 10 + ot_:11 + ot_], in0=vcol("cw", 0 * 8 + 2 * 2 + ot_),
                  in1=vcol("cw", 0 * 8 + 3 * 2 + ot_), op=ALU.add))(ot),
              reads=[VB], writes=[SCR])
            E("vector",
              (lambda ot_: lambda e: e.tensor_tensor(
                  out=scr8[:, 8 + ot_:9 + ot_], in0=scr8[:, 8 + ot_:9 + ot_],
                  in1=scr8[:, 10 + ot_:11 + ot_], op=ALU.add))(ot),
              writes=[SCR])
            E("vector",
              (lambda ot_: lambda e: e.tensor_tensor(
                  out=scr8[:, 8 + ot_:9 + ot_], in0=scr8[:, 8 + ot_:9 + ot_],
                  in1=kcol[:, ot_:ot_ + 1], op=ALU.mult))(ot),
              reads=[KC], writes=[SCR])
            E("vector",
              (lambda ot_: lambda e: e.tensor_tensor(
                  out=cb1p[:, ot_:ot_ + 1], in0=scr8[:, 8 + ot_:9 + ot_],
                  in1=vcol("cbv", 0 * 2 + ot_), op=ALU.add))(ot),
              reads=[SCR, VB], writes=[CB1])
        tap("cb1p", cb1p[:], [128, KO], reads=[CB1])
        tap("kcolT", kcol[:], [128, KO], reads=[KC])
        tap("m8lsT", m8ls[:], [128, 6], reads=[M8])
        tap("moddT", modd[:], [128, 32], reads=[MODD])
        TL, ST = B_("tails"), B_("state")
        E("vector", lambda e: e.memset(tails[:], 0.0), writes=[TL])
        for ot in range(KO):
            E("vector",
              (lambda ot_: lambda e: e.tensor_scalar(
                  out=tails[:, 0, ot_, :], in0=tails[:, 0, ot_, :],
                  scalar1=kcol[:, ot_:ot_ + 1], scalar2=None, op0=ALU.subtract))(ot),
              reads=[KC], writes=[TL])
        E("vector", lambda e: e.memset(state[:], 0.0), writes=[ST])

        UPS, UPB = B_("upstg"), B_("upTb")
        dma(up_stg, upT3, writes=[UPS], writes_pre=[DWS])
        E("scalar", lambda e: e.copy(
            out=upTb[:].rearrange("p (a n) -> p a n", a=KO), in_=up_stg),
          reads=[UPS], writes=[UPB])

        ATS = B_("atstg")
        WRB_, WIB_, OWB_ = B_("wrTb"), B_("wiTb"), B_("owTb")
        first_at = True
        for (src3, dstt, dbuf) in ((wrT3, wrTb, WRB_), (wiT3, wiTb, WIB_),
                                   (owT3, owTb, OWB_)):
            for i in range(NL):
                dma(at_stg, src3[i], writes=[ATS],
                    writes_pre=([stg[0]] if first_at else []))
                first_at = False
                E("scalar",
                  (lambda dstt_, i_: lambda e: e.copy(
                      out=dstt_[:, i_ * KO * DO:(i_ + 1) * KO * DO]
                      .rearrange("p (a n) -> p a n", a=KO), in_=at_stg))(dstt, i),
                  reads=[ATS],
                  writes=([dbuf] if i == NL - 1 else []),
                  writes_pre=([dbuf] if i == 0 else []),
                  inc=True)

        RWS, RWP = B_("rwTs"), B_("rwTp")
        rw_stg = rwTs[:].rearrange("p (a n) -> p a n", a=KD)
        dma(rw_stg, rwT3, writes=[RWS])
        pe_drain()
        for kt in range(KD):
            E("tensor",
              (lambda kt_: lambda e: e.matmul(
                  PBT[3][0:1, 0:NB], mod48[:, 24 + kt_:25 + kt_],
                  rwTs[:, kt_ * NB:(kt_ + 1) * NB],
                  start=(kt_ == 0), stop=(kt_ == KD - 1)))(kt),
              reads=[RWS, MODP], writes_pre=([PB[3]] if kt == 0 else []),
              writes=([PB[3]] if kt == KD - 1 else []), inc=(kt == KD - 1))
        E("scalar", lambda e: e.activation(out=rowscr[0:1, 0:NB],
                                           in_=PBT[3][0:1, 0:NB], func=AF.Identity),
          reads=[PB[3]], writes=[RS1])
        pe_drain()
        E("tensor", lambda e: e.matmul(
            PBT[4][0:1, 0:16], vec[0:16, VEC_COLS["rbc"]:VEC_COLS["rbc"] + 1],
            ident[0:16, 0:16], start=True, stop=True),
          reads=[VB], writes_pre=[PB[4]], writes=[PB[4]])
        E("scalar", lambda e: e.activation(out=rowscr2[0:1, 0:16],
                                           in_=PBT[4][0:1, 0:16], func=AF.Identity),
          reads=[PB[4]], writes=[RS2])
        E("vector", lambda e: e.tensor_tensor(out=rowscr[0:1, 0:16],
                                              in0=rowscr[0:1, 0:16],
                                              in1=rowscr2[0:1, 0:16], op=ALU.add),
          reads=[RS2], writes=[RS1])
        pe_drain()
        E("tensor", lambda e: e.matmul(PBT[4][0:16, 16:17], rowscr[0:1, 0:16],
                                       ones11, start=True, stop=True),
          reads=[RS1, VB], writes_pre=[PB[4]], writes=[PB[4]])
        KB16 = B_("kb16")
        E("vector", lambda e: e.tensor_copy(out=kb16[:], in_=PBT[4][0:16, 16:17]),
          reads=[PB[4]], writes=[KB16])
        for kt in range(KD):
            E("vector",
              (lambda kt_: lambda e: e.tensor_scalar(
                  out=rwTp[:, kt_ * NB:(kt_ + 1) * NB],
                  in0=rwTs[:, kt_ * NB:(kt_ + 1) * NB],
                  scalar1=modd[:, 8 + kt_:9 + kt_], scalar2=None, op0=ALU.mult))(kt),
              reads=[RWS, MODD], writes=([RWP] if kt == KD - 1 else []),
              writes_pre=([RWP] if kt == 0 else []), inc=(kt == KD - 1))
        pe_drain()
        for kt in range(KD):
            E("tensor",
              (lambda kt_: lambda e: e.matmul(
                  PBT[3][0:1, 32:32 + NB], vcol("ones"),
                  rwTp[:, kt_ * NB:(kt_ + 1) * NB],
                  start=(kt_ == 0), stop=(kt_ == KD - 1)))(kt),
              reads=[RWP, VB], writes_pre=([PB[3]] if kt == 0 else []),
              writes=([PB[3]] if kt == KD - 1 else []), inc=(kt == KD - 1))
        E("scalar", lambda e: e.activation(out=rowscr[0:1, 0:NB],
                                           in_=PBT[3][0:1, 32:32 + NB],
                                           func=AF.Identity, scale=-1.0),
          reads=[PB[3]], writes=[RS1])
        pe_drain()
        E("tensor", lambda e: e.matmul(PBT[4][0:16, 18:19], rowscr[0:1, 0:16],
                                       ones11, start=True, stop=True),
          reads=[RS1, VB], writes_pre=[PB[4]], writes=[PB[4]])
        NJR = B_("njr16")
        E("vector", lambda e: e.tensor_copy(out=njr16[:], in_=PBT[4][0:16, 18:19]),
          reads=[PB[4]], writes=[NJR])

        ORF = B_("onesrowf")
        pe_drain()
        E("tensor", lambda e: e.matmul(PBT[4][0:1, 128:256], vcol("ones"), ident,
                                       start=True, stop=True),
          reads=[VB], writes_pre=[PB[4]], writes=[PB[4]])
        E("scalar", lambda e: e.activation(out=onesrowf[:], in_=PBT[4][0:1, 128:256],
                                           func=AF.Identity),
          reads=[PB[4]], writes=[ORF])
        B2R = B_("b2row")
        for g in range(2):
            for k in range(4):
                dt = g * 4 + k
                pe_drain()
                E("tensor",
                  (lambda dt_, k_: lambda e: e.matmul(
                      PBT[4][0:1, k_ * 128:(k_ + 1) * 128], vcol("b2c", dt_),
                      ident, start=True, stop=True))(dt, k),
                  reads=[VB], writes_pre=([PB[4]] if k == 0 else []),
                  writes=([PB[4]] if k == 3 else []), inc=True)
            E("scalar",
              (lambda g_: lambda e: e.activation(
                  out=b2row[0:1, g_ * 512:(g_ + 1) * 512],
                  in_=PBT[4][0:1, 0:512], func=AF.Identity))(g),
              reads=[PB[4]],
              writes=([B2R] if g == 1 else []),
              writes_pre=([B2R] if g == 0 else []), inc=True)

        kb.barrier()

        GA, GB_b, GC = B_("genA"), B_("genB"), B_("genC")
        W1S, W2S = B_("w1s"), B_("w2s")
        KB1B = B_("KB1")
        wblk = genC[:].rearrange("p (a n) -> p a n", a=KD)
        wblkb = genB[:].bitcast(BF16).rearrange("p (a n) -> p a n", a=KD)
        w2stg = genA[:]
        w2stgb = genC[:, 0:512].bitcast(BF16)

        def p0b_w1(j):
            dma(wblk, w1T3[:, :, j * 128:(j + 1) * 128], writes=[GC], q="qpb")
            for kt in range(KD):
                E("vector",
                  (lambda kt_: lambda e: e.tensor_scalar(
                      out=wblkb[:, kt_, :], in0=wblk[:, kt_, :],
                      scalar1=modd[:, 8 + kt_:9 + kt_], scalar2=None,
                      op0=ALU.mult))(kt),
                  reads=[GC, MODD],
                  writes=([GB_b] if kt == KD - 1 else []),
                  writes_pre=([GB_b] if kt == 0 else []),
                  inc=(kt == KD - 1))
            pe_drain()
            for kt in range(KD):
                E("tensor",
                  (lambda kt_: lambda e: e.matmul(
                      PBT[3][0:1, 128:256], sbpb[:, kt_:kt_ + 1], wblkb[:, kt_, :],
                      start=(kt_ == 0), stop=(kt_ == KD - 1)))(kt),
                  reads=[GB_b, SBPB], writes_pre=([PB[3]] if kt == 0 else []),
                  writes=([PB[3]] if kt == KD - 1 else []), inc=(kt == KD - 1))
            E("scalar",
              (lambda j_: lambda e: e.activation(
                  out=rowscr2[0:1, (j_ % 4) * 128:(j_ % 4) * 128 + 128],
                  in_=PBT[3][0:1, 128:256], func=AF.Identity))(j),
              reads=[PB[3]], writes=[RS2])
            if j % 4 == 3:
                for k in range(4):
                    pe_drain()
                    E("tensor",
                      (lambda k_: lambda e: e.matmul(
                          PBT[4][:, 300 + k_:301 + k_],
                          rowscr2[0:1, k_ * 128:(k_ + 1) * 128], ones11,
                          start=True, stop=True))(k),
                      reads=[RS2, VB], writes_pre=([PB[4]] if k == 0 else []),
                      writes=([PB[4]] if k == 3 else []), inc=True)
                E("vector",
                  (lambda j_: lambda e: e.tensor_copy(
                      out=KB1t[:, j_ - 3:j_ + 1], in_=PBT[4][:, 300:304]))(j),
                  reads=[PB[4]],
                  writes=([KB1B] if j == FT - 1 else []),
                  writes_pre=([KB1B] if j == 3 else []), inc=True)
            dma(w1s[:, j * 1024:(j + 1) * 1024],
                wblkb.rearrange("p a n -> p (a n)"), reads=[GB_b],
                writes=([W1S] if j == FT - 1 else []),
                writes_pre=([W1S] if j == 0 else []), q="qs1")

        def p0b_w2(ot):
            dma(w2stg, w2T3[:, ot, :], writes=[GA], q="qpb2")
            E("scalar", lambda e: e.copy(out=w2stgb, in_=w2stg),
              reads=[GA], writes=[GC])
            dma(w2s[:, ot * 1024:(ot + 1) * 1024], w2stgb, reads=[GC],
                writes=([W2S] if ot == FT - 1 else []),
                writes_pre=([W2S] if ot == 0 else []), q="qs2")

        p0b_jobs = [lambda j=j: p0b_w1(j) for j in range(FT)] + \
                   [lambda o=o: p0b_w2(o) for o in range(FT)]
        p0b_state = {"i": 0, "done": False}

        def p0b_fill(n):
            i = p0b_state["i"]
            for _ in range(n):
                if i >= len(p0b_jobs):
                    break
                p0b_jobs[i]()
                i += 1
            p0b_state["i"] = i
            if i >= len(p0b_jobs) and not p0b_state["done"]:
                p0b_state["done"] = True
                E("vector", lambda e: e.tensor_tensor(out=KB1t[:], in0=KB1t[:],
                                                      in1=vcols("b1c", 0, 32),
                                                      op=ALU.add),
                  reads=[VB], writes=[KB1B])

        XRAB, XRBB = B_("XRA"), B_("XRB")
        LABs = [B_("LA0"), B_("LA1")]
        AABs = [B_("AA0"), B_("AA1")]
        T1Bs = [B_("T10"), B_("T11")]
        HSBBs = [B_("HSB0"), B_("HSB1")]
        RSTB = B_("rstdB")
        XCBB = B_("XCB")
        RBS = [B_("Rb0"), B_("Rb1")]
        IBS = [B_("Ib0"), B_("Ib1")]
        XBRS = [B_("xbr0"), B_("xbr1")]
        XSQS = [B_("xsq0"), B_("xsq1")]
        OSQB = B_("OSQ")
        X2B = B_("X2")
        ROWB = B_("rowbf")

        def ln1_down(q, s):
            t0 = s * SC
            for kt in range(KD):
                sl = kt % 2
                E("scalar",
                  (lambda kt_, sl_, t0_: lambda e: e.copy(
                      out=xbr[:, sl_, :], in_=XBt[:, kt_, t0_:t0_ + SC]))(kt, sl, t0),
                  reads=[XBS[s]], writes=[XBRS[sl]],
                  writes_pre=([OSQB] if kt < 2 else []))
                E("scalar",
                  (lambda kt_, sl_, t0_: lambda e: e.activation(
                      out=xsqr[:, sl_, :], in_=XBt[:, kt_, t0_:t0_ + SC],
                      func=AF.Square))(kt, sl, t0),
                  reads=[XBS[s]], writes=[XSQS[sl]])
                E("tensor",
                  (lambda kt_, sl_: lambda e: e.matmul(
                      PBT[3][0:1, 0:SC], onescol_bf, xbr[:, sl_, :],
                      start=(kt_ == 0), stop=(kt_ == KD - 1)))(kt, sl),
                  reads=[XBRS[sl], CB], writes_pre=([PB[3]] if kt == 0 else []),
                  writes=([PB[3]] if kt == KD - 1 else []), inc=True)
                E("tensor",
                  (lambda kt_, sl_: lambda e: e.matmul(
                      PBT[7][0:1, 0:SC], onescol_bf, xsqr[:, sl_, :],
                      start=(kt_ == 0), stop=(kt_ == KD - 1)))(kt, sl),
                  reads=[XSQS[sl], CB], writes_pre=([PB[7]] if kt == 0 else []),
                  writes=([PB[7]] if kt == KD - 1 else []), inc=True)
                for ot in range(KO):
                    E("tensor",
                      (lambda kt_, ot_, sl_: lambda e: e.matmul(
                          PBT[ot_][:, 0:SC],
                          dwTb[:, (kt_ * KO + ot_) * 128:(kt_ * KO + ot_ + 1) * 128],
                          xbr[:, sl_, :],
                          start=(kt_ == 0), stop=False))(kt, ot, sl),
                      reads=[XBRS[sl], DWB],
                      writes_pre=([PB[ot]] if kt == 0 else []),
                      inc=True)
            E("vector", lambda e: e.tensor_scalar(out=rowscr[:], in0=PBT[3][0:1, 0:SC],
                                                  scalar1=1.0 / D, scalar2=None,
                                                  op0=ALU.mult),
              reads=[PB[3]], writes=[RS1])
            E("scalar", lambda e: e.activation(out=rowscr2[:], in_=rowscr[:],
                                               func=AF.Square),
              reads=[RS1], writes=[RS2])
            E("vector", lambda e: e.scalar_tensor_tensor(
                out=rowscr2[:], in0=PBT[7][0:1, 0:SC], scalar=1.0 / D,
                in1=rowscr2[:], op0=ALU.mult, op1=ALU.subtract),
              reads=[PB[7]], writes=[RS2])
            if q == 0 and s == 0:
                tap("var00row", rowscr2[:], [1, SC], reads=[RS2])
            E("scalar", lambda e: e.activation(out=rowscr2[:], in_=rowscr2[:],
                                               func=AF.Ln, bias=vrow0("epsln")),
              reads=[VB], writes=[RS2])
            if q == 0 and s == 0:
                tap("lnv00row", rowscr2[:], [1, SC], reads=[RS2])
            E("scalar", lambda e: e.activation(out=rowscr2[:], in_=rowscr2[:],
                                               func=AF.Exp, scale=-0.5),
              writes=[RS2])
            if q == 0 and s == 0:
                tap("m00row", rowscr[:], [1, SC], reads=[RS1])
                tap("rstdrow00", rowscr2[:], [1, SC], reads=[RS2])
                tap("ones00", onesrowf[:], [1, 128], reads=[ORF])
            E("vector", lambda e: e.tensor_copy(out=rowbf[:], in_=rowscr[:]),
              reads=[RS1], writes=[ROWB])
            for ot in range(KO):
                E("tensor",
                  (lambda ot_: lambda e: e.matmul(
                      PBT[ot_][:, 0:SC], negJrow[0:1, ot_ * 128:(ot_ + 1) * 128],
                      rowbf[:], start=False, stop=True))(ot),
                  reads=[ROWB, NJ], writes=[PB[ot]], inc=True)
            pe_drain()
            E("tensor", lambda e: e.matmul(PBT[2][:, 0:SC], onesrowf[:], rowscr2[:],
                                           start=True, stop=True),
              reads=[RS2, ORF], writes_pre=[PB[2]], writes=[PB[2]])
            E("scalar", lambda e: e.activation(out=rstdB[:], in_=PBT[2][:, 0:SC],
                                               func=AF.Identity),
              reads=[PB[2]], writes=[RSTB])
            for ot in range(KO):
                E("vector",
                  (lambda ot_, t0_: lambda e: e.tensor_tensor(
                      out=XRA[:, ot_, 3 + t0_:3 + t0_ + SC], in0=PBT[ot_][:, 0:SC],
                      in1=rstdB[:], op=ALU.mult))(ot, t0),
                  reads=[PB[ot], RSTB],
                  writes=([XRAB] if ot == KO - 1 else []),
                  writes_pre=([XRAB] if ot == 0 else []), inc=True)
            if q == 0 and s == 0:
                tap("xr00", XRA[:, 0, 3:3 + SC], [128, SC], reads=[XRAB])
                tap("rstd00", rstdB[:], [128, SC], reads=[RSTB])

        def layer(q, i):
            IN, INB = (XRA, XRAB) if i % 2 == 0 else (XRB, XRBB)
            OUT, OUTB = (XRB, XRBB) if i % 2 == 0 else (XRA, XRAB)
            wr_off = lambda kt, ot: ((i * KO + kt) * KO + ot) * 128
            E("vector",
              (lambda i_, IN_: lambda e: e.tensor_copy(
                  out=IN_[:, :, 0:3], in_=tails[:, i_, :, :]))(i, IN),
              reads=[TL], writes=[INB])
            E("scalar", lambda e: preload(e, SET_SIG), inc=False)
            for k in range(4):
                for ot in range(KO):
                    acc = T1s[ot][:].rearrange("p o t -> p (o t)")
                    cbap = cb1p[:, ot:ot + 1] if i == 0 else vcol("cbv", i * 2 + ot)
                    if k == 0:
                        E("vector",
                          (lambda i_, ot_, cbap_, IN_, acc_: lambda e:
                           e.tensor_scalar(
                              out=acc_, in0=IN_[:, ot_, 0:QT],
                              scalar1=vcol("cw", i_ * 8 + 0 * 2 + ot_),
                              scalar2=cbap_,
                              op0=ALU.mult, op1=ALU.add))(i, ot, cbap, IN, acc),
                          reads=[INB, VB, CB1], writes=[T1Bs[ot]])
                    else:
                        last = (k == 3)
                        E("vector",
                          (lambda i_, ot_, k_, IN_, last_, acc_: lambda e:
                           e.scalar_tensor_tensor(
                              out=(XCB[:, ot_, 0:QT] if last_ else acc_),
                              in0=IN_[:, ot_, k_:k_ + QT],
                              scalar=vcol("cw", i_ * 8 + k_ * 2 + ot_),
                              in1=acc_, op0=ALU.mult,
                              op1=ALU.add))(i, ot, k, IN, last, acc),
                          reads=[INB, VB, T1Bs[ot]],
                          writes=([T1Bs[ot], XCBB] if last else [T1Bs[ot]]))
            for s in range(2):
                t0 = s * SC
                for (wt, wbuf, dst, dstb, bname, bank0) in (
                        (wrTb, WRB_, Rb, RBS[s], "wrb", 4),
                        (wiTb, WIB_, Ib, IBS[s], "wib", 6)):
                    for ot in range(KO):
                        bank = bank0 + ot
                        for kt in range(KO):
                            E("tensor",
                              (lambda wt_, kt_, ot_, t0_, bank_: lambda e: e.matmul(
                                  PBT[bank_][:, 0:SC],
                                  wt_[:, wr_off(kt_, ot_):wr_off(kt_, ot_) + 128],
                                  XCB[:, kt_, t0_:t0_ + SC],
                                  start=(kt_ == 0), stop=(kt_ == KO - 1)))(
                                      wt, kt, ot, t0, bank),
                              reads=[XCBB, wbuf],
                              writes_pre=([PB[bank]] if kt == 0 else []),
                              writes=([PB[bank]] if kt == KO - 1 else []),
                              inc=(kt == KO - 1))
                        E("scalar",
                          (lambda dst_, i_, ot_, t0_, bank_, bn_: lambda e:
                           e.activation(
                              out=dst_[:, ot_, t0_:t0_ + SC],
                              in_=PBT[bank_][:, 0:SC], func=AF.Sigmoid,
                              bias=vcol(bn_, i_ * 2 + ot_)))(
                                  dst, i, ot, t0, bank, bname),
                          reads=[PB[bank], VB],
                          writes=([dstb] if ot == KO - 1 else []),
                          writes_pre=([dstb] if ot == 0 else []),
                          inc=True)
            E("scalar", lambda e: preload(e, SET_NLE), inc=False)
            wr = lambda s: dict(LAv=LAs[s], LAB=LABs[s], AAv=AAs[s],
                                AAB=AABs[s], T1v=T1s[s], T1B=T1Bs[s],
                                HSB=HSBs[s], HSBB=HSBBs[s], t0=s * SC)
            C = [wr(0), wr(1)]
            for s in range(2):
                c = C[s]
                E("vector",
                  (lambda t0_, LAv_: lambda e: e.tensor_tensor(
                      out=LAv_[:], in0=Ib[:, :, t0_:t0_ + SC],
                      in1=XCB[:, :, t0_:t0_ + SC], op=ALU.mult))(c["t0"], c["LAv"]),
                  reads=[IBS[s], XCBB], writes=[c["LAB"]])
            for s in range(2):
                c = C[s]
                for ot in range(KO):
                    E("scalar",
                      (lambda i_, ot_, t0_, AAv_: lambda e: e.activation(
                          out=AAv_[:, ot_, :], in_=Rb[:, ot_, t0_:t0_ + SC],
                          func=AF.Exp,
                          scale=m8ls[:, i_ * 2 + ot_:i_ * 2 + ot_ + 1]))(
                              i, ot, c["t0"], c["AAv"]),
                      reads=[RBS[s]], sreads=[M8],
                      writes=([c["AAB"]] if ot == KO - 1 else []),
                      writes_pre=([c["AAB"]] if ot == 0 else []), inc=True)
            for s in range(2):
                c = C[s]
                E("scalar",
                  (lambda AAv_, T1v_: lambda e: e.activation(
                      out=T1v_[:], in_=AAv_[:], func=AF.Square))(c["AAv"], c["T1v"]),
                  reads=[c["AAB"]], writes=[c["T1B"]])
            for s in range(2):
                c = C[s]
                E("vector",
                  (lambda T1v_: lambda e: e.tensor_scalar(
                      out=T1v_[:], in0=T1v_[:], scalar1=-1.0, scalar2=1.0,
                      op0=ALU.mult, op1=ALU.add))(c["T1v"]),
                  writes=[c["T1B"]])
                E("vector",
                  (lambda T1v_: lambda e: e.tensor_scalar(
                      out=T1v_[:], in0=T1v_[:], scalar1=1e-38, scalar2=None,
                      op0=ALU.max))(c["T1v"]),
                  writes=[c["T1B"]])
            for s in range(2):
                c = C[s]
                E("scalar",
                  (lambda T1v_: lambda e: e.activation(
                      out=T1v_[:], in_=T1v_[:], func=AF.Ln))(c["T1v"]),
                  reads=[VB], writes=[c["T1B"]])
                E("scalar",
                  (lambda T1v_: lambda e: e.activation(
                      out=T1v_[:], in_=T1v_[:], func=AF.Exp, scale=0.5))(c["T1v"]),
                  writes=[c["T1B"]])
            if q == 0 and i == 0:
                tap("a00", AAs[0][:, 0, :], [128, SC], reads=[AABs[0]])
                tap("gate00", T1s[0][:, 0, :], [128, SC], reads=[T1Bs[0]])
            for s in range(2):
                c = C[s]
                E("vector",
                  (lambda LAv_, T1v_: lambda e: e.tensor_tensor(
                      out=LAv_[:], in0=LAv_[:], in1=T1v_[:], op=ALU.mult))(
                          c["LAv"], c["T1v"]),
                  reads=[c["T1B"]], writes=[c["LAB"]])
            if q == 0 and i == 0:
                tap("b00", LAs[0][:, 0, :], [128, SC], BF16, reads=[LABs[0]])
            for s in range(2):
                c = C[s]
                for ot in range(KO):
                    E("vector",
                      (lambda i_, ot_, AAv_, LAv_, HSB_: lambda e:
                       e.tensor_tensor_scan(
                          out=HSB_[:, ot_, :], data0=AAv_[:, ot_, :],
                          data1=LAv_[:, ot_, :],
                          initial=state[:, i_ * 2 + ot_:i_ * 2 + ot_ + 1],
                          op0=ALU.mult, op1=ALU.add))(
                              i, ot, c["AAv"], c["LAv"], c["HSB"]),
                      reads=[c["AAB"], c["LAB"]], sreads=[ST],
                      writes=([c["HSBB"]] if ot == KO - 1 else []),
                      writes_pre=([c["HSBB"]] if ot == 0 else []), inc=True)
                    E("vector",
                      (lambda i_, ot_, HSB_: lambda e: e.tensor_copy(
                          out=state[:, i_ * 2 + ot_:i_ * 2 + ot_ + 1],
                          in_=HSB_[:, ot_, SC - 1:SC]))(i, ot, c["HSB"]),
                      writes=[ST])
            if q == 0 and i == 0:
                tap("hs00", HSBs[0][:, 0, :], [128, SC], BF16, reads=[HSBBs[0]])
                tap("r00", Rb[:, 0, 0:SC], [128, SC], BF16, reads=[RBS[0]])
                tap("xcb00", XCB[:, 0, 0:SC], [128, SC], BF16, reads=[XCBB])
            for s in range(2):
                c = C[s]
                for ot in range(KO):
                    bank = 4 + 2 * s + ot
                    for kt in range(KO):
                        E("tensor",
                          (lambda kt_, ot_, bank_, HSB_: lambda e: e.matmul(
                              PBT[bank_][:, 0:SC],
                              owTb[:, wr_off(kt_, ot_):wr_off(kt_, ot_) + 128],
                              HSB_[:, kt_, :],
                              start=(kt_ == 0), stop=(kt_ == KO - 1)))(
                                  kt, ot, bank, c["HSB"]),
                          reads=[c["HSBB"], OWB_],
                          writes_pre=([PB[bank]] if kt == 0 else []),
                          writes=([PB[bank]] if kt == KO - 1 else []),
                          inc=(kt == KO - 1))
            for s in range(2):
                c = C[s]
                for ot in range(KO):
                    bank = 4 + 2 * s + ot
                    E("scalar",
                      (lambda ot_, bank_, s_: lambda e: e.activation(
                          out=OSQ[:, ot_, s_ * SC // 2:s_ * SC // 2 + SC],
                          in_=PBT[bank_][:, 0:SC],
                          func=AF.Square))(ot, bank, 0),
                      reads=[PB[bank]],
                      writes=([OSQB] if ot == KO - 1 else []),
                      writes_pre=(([OSQB] + XBRS + XSQS) if ot == 0 else []),
                      inc=True) if False else None
                    E("scalar",
                      (lambda ot_, bank_: lambda e: e.activation(
                          out=OSQ[:, ot_, :], in_=PBT[bank_][:, 0:SC],
                          func=AF.Square))(ot, bank),
                      reads=[PB[bank]],
                      writes=([OSQB] if ot == KO - 1 else []),
                      writes_pre=(([OSQB] + XBRS + XSQS) if ot == 0 else []),
                      inc=True)
                    E("scalar",
                      (lambda ot_, bank_, AAv_: lambda e: e.activation(
                          out=AAv_[:, ot_, :], in_=PBT[bank_][:, 0:SC],
                          func=AF.Identity))(ot, bank, c["AAv"]),
                      reads=[PB[bank]],
                      writes=([c["AAB"]] if ot == KO - 1 else []),
                      writes_pre=([c["AAB"]] if ot == 0 else []), inc=True)
                rowrms = rowscr if s == 0 else rowscr3
                RSX = RS1 if s == 0 else RS3
                rbank = 3 if s == 0 else 2
                for ot in range(KO):
                    E("tensor",
                      (lambda ot_, rb_: lambda e: e.matmul(
                          PBT[rb_][0:1, 0:SC], onescol_bf,
                          OSQ[:, ot_, :],
                          start=(ot_ == 0), stop=(ot_ == KO - 1)))(ot, rbank),
                      reads=[OSQB, CB],
                      writes_pre=([PB[rbank]] if ot == 0 else []),
                      writes=([PB[rbank]] if ot == KO - 1 else []),
                      inc=(ot == KO - 1))
                E("scalar",
                  (lambda rr, rb_: lambda e: e.activation(
                      out=rr[:], in_=PBT[rb_][0:1, 0:SC],
                      func=AF.Ln, bias=vrow0("epsrmsd")))(rowrms, rbank),
                  reads=[PB[rbank], VB], writes=[RSX])
                E("scalar",
                  (lambda rr: lambda e: e.activation(
                      out=rr[:], in_=rr[:], func=AF.Exp, scale=-0.5))(rowrms),
                  writes=[RSX])
                pe_drain()
                E("tensor",
                  (lambda rr, s_: lambda e: e.matmul(
                      PBT[2][:, 0:SC], onesrowf[:], rr[:],
                      start=True, stop=True))(rowrms, s),
                  reads=[RSX, ORF], writes_pre=[PB[2]], writes=[PB[2]])
                for ot in range(KO):
                    E("vector",
                      (lambda i_, ot_, t0_, OUT_, AAv_: lambda e:
                       e.scalar_tensor_tensor(
                          out=OUT_[:, ot_, 3 + t0_:3 + t0_ + SC],
                          in0=AAv_[:, ot_, :],
                          scalar=vcol("nwv", i_ * 2 + ot_), in1=PBT[2][:, 0:SC],
                          op0=ALU.mult, op1=ALU.mult))(i, ot, c["t0"], OUT,
                                                       c["AAv"]),
                      reads=[c["AAB"], VB, PB[2]],
                      writes=([OUTB] if ot == KO - 1 else []),
                      writes_pre=([OUTB] if ot == 0 else []), inc=True)
            if q == 0:
                tap(f"xn0L{i}", OUT[:, 0, 3:3 + SC], [128, SC], reads=[OUTB])
            E("vector",
              (lambda i_, IN_: lambda e: e.tensor_copy(
                  out=tails[:, i_, :, :], in_=IN_[:, :, QT:QT + 3]))(i, IN),
              reads=[INB], writes=[TL])

        def up_resid(q, s):
            t0 = s * SC
            h = q % 2
            E("scalar",
              (lambda t0_: lambda e: e.copy(
                  out=XCB[:, :, t0_:t0_ + SC],
                  in_=XRB[:, :, 3 + t0_:3 + t0_ + SC]))(t0),
              reads=[XRBB], writes=[XCBB])
            for dt in range(KD):
                bank = dt % 2
                for kt in range(KO):
                    E("tensor",
                      (lambda kt_, dt_, t0_, bank_: lambda e: e.matmul(
                          PBT[bank_][:, 0:SC],
                          upTb[:, kt_ * D + dt_ * 128:kt_ * D + (dt_ + 1) * 128],
                          XCB[:, kt_, t0_:t0_ + SC],
                          start=(kt_ == 0), stop=(kt_ == KO - 1)))(kt, dt, t0, bank),
                      reads=[XCBB, UPB],
                      writes_pre=([PB[bank]] if kt == 0 else []),
                      writes=([PB[bank]] if kt == KO - 1 else []),
                      inc=(kt == KO - 1))
                E("vector",
                  (lambda dt_, t0_, h_, bank_: lambda e: e.scalar_tensor_tensor(
                      out=X2t[:, dt_, h_ * QT + t0_:h_ * QT + t0_ + SC],
                      in0=PBT[bank_][:, 0:SC], scalar=modd[:, 16 + dt_:17 + dt_],
                      in1=XBt[:, dt_, t0_:t0_ + SC], op0=ALU.mult, op1=ALU.add))(
                          dt, t0, h, dt % 2),
                  reads=[PB[dt % 2], XBS[s]], sreads=[MODD],
                  writes=([X2B] if dt == KD - 1 else []),
                  writes_pre=([X2B] if dt == 0 else []), inc=True)
            if q == 0 and s == 0:
                tap("x2q0", X2t[:, 0, 0:SC], [128, SC], reads=[X2B])

        def s1_quarter(q, filler=None):
            for s in range(2):
                ln1_down(q, s)
            if filler:
                filler(3)
            for i in range(NL):
                layer(q, i)
                if filler:
                    filler(3)
            for s in range(2):
                up_resid(q, s)
                if q + 1 < NQ:
                    dma(XBt[:, :, s * SC:(s + 1) * SC],
                        x3[:, :, (q + 1) * QT + s * SC:(q + 1) * QT + (s + 1) * SC],
                        writes=[XBS[s]], q="qx0" if s == 0 else "qx1")

        X2NB, HHB = B_("x2n"), B_("hh")
        U32B = B_("u32a")
        X2BS = [B_("x2br0"), B_("x2br1")]
        X2SQS = [B_("x2sq0"), B_("x2sq1")]
        Z16B, ZTMB, MM1B, TSCB, MEMB = (B_("z16"), B_("ztm"), B_("mm1"),
                                        B_("tsc"), B_("mask_em"))
        MM1BB = B_("mm1b")
        w1bufb = genA[:].bitcast(BF16).rearrange("p (a n) -> p a n", a=2)
        w1bufc = genB[:].bitcast(BF16)
        w2bufb = genC[:].bitcast(BF16).rearrange("p (a n) -> p a n", a=2)
        W1SL = [B_("w1sl0"), B_("w1sl1"), B_("w1sl2")]
        W2SL = [B_("w2sl0"), B_("w2sl1")]

        def s2_chunk(q, c, first=False):
            h = q % 2
            t0 = h * QT + c * SC
            gtok = q * QT + c * SC
            for kt in range(KD):
                sl = kt % 2
                E("scalar",
                  (lambda kt_, sl_, t0_: lambda e: e.copy(
                      out=x2br[:, sl_, :], in_=X2t[:, kt_, t0_:t0_ + SC]))(kt, sl, t0),
                  reads=[X2B], writes=[X2BS[sl]])
                E("scalar",
                  (lambda kt_, sl_, t0_: lambda e: e.activation(
                      out=x2sqr[:, sl_, :], in_=X2t[:, kt_, t0_:t0_ + SC],
                      func=AF.Square))(kt, sl, t0),
                  reads=[X2B], writes=[X2SQS[sl]])
                E("tensor",
                  (lambda kt_, sl_: lambda e: e.matmul(
                      PBT[3][0:1, 0:SC], onescol_bf, x2br[:, sl_, :],
                      start=(kt_ == 0), stop=(kt_ == KD - 1)))(kt, sl),
                  reads=[X2BS[sl], CB], writes_pre=([PB[3]] if kt == 0 else []),
                  writes=([PB[3]] if kt == KD - 1 else []), inc=True)
                E("tensor",
                  (lambda kt_, sl_: lambda e: e.matmul(
                      PBT[7][0:1, 0:SC], onescol_bf, x2sqr[:, sl_, :],
                      start=(kt_ == 0), stop=(kt_ == KD - 1)))(kt, sl),
                  reads=[X2SQS[sl], CB], writes_pre=([PB[7]] if kt == 0 else []),
                  writes=([PB[7]] if kt == KD - 1 else []), inc=True)
                E("tensor",
                  (lambda kt_, t0_: lambda e: e.matmul(
                      PBT[6][0:16, 0:SC], rwTp[:, kt_ * NB:(kt_ + 1) * NB],
                      X2t[:, kt_, t0_:t0_ + SC],
                      start=(kt_ == 0), stop=(kt_ == KD - 1)))(kt, t0),
                  reads=[X2B, RWP], writes_pre=([PB[6]] if kt == 0 else []),
                  writes=([PB[6]] if kt == KD - 1 else []), inc=True)
            E("vector", lambda e: e.tensor_scalar(out=rowscr[:], in0=PBT[3][0:1, 0:SC],
                                                  scalar1=1.0 / D, scalar2=None,
                                                  op0=ALU.mult),
              reads=[PB[3]], writes=[RS1])
            E("scalar", lambda e: e.activation(out=rowscr2[:], in_=rowscr[:],
                                               func=AF.Square),
              reads=[RS1], writes=[RS2])
            E("vector", lambda e: e.scalar_tensor_tensor(
                out=rowscr2[:], in0=PBT[7][0:1, 0:SC], scalar=1.0 / D,
                in1=rowscr2[:], op0=ALU.mult, op1=ALU.subtract),
              reads=[PB[7]], writes=[RS2])
            E("scalar", lambda e: e.activation(out=rowscr2[:], in_=rowscr2[:],
                                               func=AF.Ln, bias=vrow0("epsln")),
              reads=[VB], writes=[RS2])
            E("scalar", lambda e: e.activation(out=rowscr2[:], in_=rowscr2[:],
                                               func=AF.Exp, scale=-0.5),
              writes=[RS2])
            E("vector", lambda e: e.tensor_tensor(out=rowscr[:], in0=rowscr[:],
                                                  in1=rowscr2[:], op=ALU.mult),
              reads=[RS2], writes=[RS1])
            pe_drain()
            E("tensor", lambda e: e.matmul(PBT[2][:, 0:SC], onesrowf[:], rowscr2[:],
                                           start=True, stop=True),
              reads=[RS2, ORF], writes_pre=[PB[2]], writes=[PB[2]])
            E("tensor", lambda e: e.matmul(PBT[5][:, 0:SC], onesrowf[:], rowscr[:],
                                           start=True, stop=True),
              reads=[RS1, ORF], writes_pre=[PB[5]], writes=[PB[5]])
            for hbk in range(2):
                E("vector",
                  (lambda hbk_, t0_: lambda e: e.tensor_tensor(
                      out=u32a[:], in0=X2t[:, hbk_ * 4:(hbk_ + 1) * 4, t0_:t0_ + SC],
                      in1=PBT[2][:, 0:SC].unsqueeze(1).broadcast_to([128, 4, SC]),
                      op=ALU.mult))(hbk, t0),
                  reads=[X2B, PB[2]], writes=[U32B])
                E("vector",
                  (lambda hbk_: lambda e: e.tensor_tensor(
                      out=x2n[:, hbk_ * 4:(hbk_ + 1) * 4, :], in0=u32a[:],
                      in1=PBT[5][:, 0:SC].unsqueeze(1).broadcast_to([128, 4, SC]),
                      op=ALU.subtract))(hbk),
                  reads=[U32B, PB[5]],
                  writes=([X2NB] if hbk == 1 else []),
                  writes_pre=([X2NB] if hbk == 0 else []), inc=True)
            E("scalar", lambda e: e.activation(out=z16[0:16, :],
                                               in_=PBT[6][0:16, 0:SC],
                                               func=AF.Identity),
              reads=[PB[6]], writes=[Z16B])
            E("vector", lambda e: e.tensor_tensor(out=z16[0:16, :], in0=z16[0:16, :],
                                                  in1=PBT[2][0:16, 0:SC],
                                                  op=ALU.mult),
              reads=[PB[2]], writes=[Z16B])
            E("vector", lambda e: e.scalar_tensor_tensor(
                out=z16[0:16, :], in0=PBT[5][0:16, 0:SC], scalar=njr16[:, 0:1],
                in1=z16[0:16, :], op0=ALU.mult, op1=ALU.add),
              reads=[PB[5]], sreads=[NJR], writes=[Z16B])
            E("vector", lambda e: e.tensor_scalar(out=z16[0:16, :], in0=z16[0:16, :],
                                                  scalar1=kb16[:, 0:1], scalar2=None,
                                                  op0=ALU.add),
              sreads=[KB16], writes=[Z16B])
            for tt in range(4):
                pe_drain()
                E("tensor",
                  (lambda tt_: lambda e: e.matmul(
                      PBT[4][:, tt_ * 16:(tt_ + 1) * 16],
                      z16[0:16, tt_ * 128:(tt_ + 1) * 128], ident[0:16, 0:16],
                      start=True, stop=True))(tt),
                  reads=[Z16B, VB], writes_pre=([PB[4]] if tt == 0 else []),
                  writes=([PB[4]] if tt == 3 else []), inc=True)
            E("vector", lambda e: e.tensor_copy(
                out=ztm[:].rearrange("p a e -> p (a e)"), in_=PBT[4][:, 0:64]),
              reads=[PB[4]], writes=[ZTMB])
            E("vector", lambda e: e.memset(mm1[:].rearrange("p a e -> p (a e)"),
                                           -1.0),
              writes=[MM1B])
            for rnd in range(TOPK):
                E("vector", lambda e: e.tensor_reduce(out=scr8[:, 8:12], in_=ztm[:],
                                                      axis=AX, op=ALU.max),
                  reads=[ZTMB], writes=[SCR])
                E("vector", lambda e: e.tensor_tensor(
                    out=tsc[:], in0=ztm[:],
                    in1=scr8[:, 8:12].unsqueeze(2).broadcast_to([128, 4, 16]),
                    op=ALU.is_ge),
                  reads=[SCR], writes=[TSCB])
                E("vector", lambda e: e.scalar_tensor_tensor(
                    out=tsc[:], in0=tsc[:], scalar=-99.0,
                    in1=vcols("iota99", 0, 16).unsqueeze(1)
                    .broadcast_to([128, 4, 16]),
                    op0=ALU.mult, op1=ALU.add),
                  reads=[VB], writes=[TSCB])
                E("vector", lambda e: e.tensor_reduce(out=scr8[:, 12:16], in_=tsc[:],
                                                      axis=AX, op=ALU.min),
                  reads=[TSCB], writes=[SCR])
                E("vector", lambda e: e.tensor_tensor(
                    out=tsc[:],
                    in0=vcols("iota16", 0, 16).unsqueeze(1)
                    .broadcast_to([128, 4, 16]),
                    in1=scr8[:, 12:16].unsqueeze(2).broadcast_to([128, 4, 16]),
                    op=ALU.is_equal),
                  reads=[SCR, VB], writes=[TSCB])
                E("vector", lambda e: e.tensor_tensor(out=mm1[:], in0=mm1[:],
                                                      in1=tsc[:], op=ALU.add),
                  reads=[TSCB], writes=[MM1B])
                if rnd < TOPK - 1:
                    E("vector", lambda e: e.scalar_tensor_tensor(
                        out=ztm[:], in0=tsc[:], scalar=-BIGNEG, in1=ztm[:],
                        op0=ALU.mult, op1=ALU.add),
                      reads=[TSCB], writes=[ZTMB])
            E("vector", lambda e: e.tensor_copy(
                out=mm1b.rearrange("p (a e) -> p a e", a=4), in_=mm1[:]),
              reads=[MM1B], writes=[MM1BB])
            for tt in range(4):
                pe_drain()
                E("tensor",
                  (lambda tt_: lambda e: e.matmul(
                      PBT[4][0:16, tt_ * 128:(tt_ + 1) * 128],
                      mm1b.rearrange("p (a e) -> p a e", a=4)[:, tt_, :],
                      identb[:], start=True, stop=True))(tt),
                  reads=[MM1BB, IDB], writes_pre=([PB[4]] if tt == 0 else []),
                  writes=([PB[4]] if tt == 3 else []), inc=True)
            E("scalar", lambda e: e.activation(out=mask_em[0:16, :],
                                               in_=PBT[4][0:16, 0:512],
                                               func=AF.Identity),
              reads=[PB[4]], writes=[MEMB])
            if q == 0 and c == 0:
                tap("z00", z16[0:16, :], [16, SC], reads=[Z16B])
                tap("mask00", mm1[:].rearrange("p a e -> p (a e)"), [128, 64],
                    reads=[MM1B])
                tap("x2n00", x2n[:, 0, :], [128, SC], BF16, reads=[X2NB])
            for j in range(FT):
                sl = j % 3
                w1slot = w1bufc if sl == 2 else w1bufb[:, sl, :]
                dma(w1slot, w1s[:, j * 1024:(j + 1) * 1024],
                    reads=([W1S, GA, GC, GB_b] if (first and j < 3) else [W1S]),
                    writes=[W1SL[sl]],
                    q=("qw10", "qw11", "qw12")[sl])
                bank = j % 4
                blk = j // 2
                for kt in range(KD):
                    E("tensor",
                      (lambda w1s_, kt_, bank_: lambda e: e.matmul(
                          PBT[bank_][:, 0:SC],
                          w1s_[:, kt_ * 128:(kt_ + 1) * 128],
                          x2n[:, kt_, :],
                          start=(kt_ == 0), stop=False))(w1slot, kt, bank),
                      reads=[W1SL[sl], X2NB],
                      writes_pre=([PB[bank]] if kt == 0 else []),
                      inc=(kt == KD - 1))
                E("tensor",
                  (lambda blk_, bank_: lambda e: e.matmul(
                      PBT[bank_][:, 0:SC], bigsel[:, blk_ * 128:(blk_ + 1) * 128],
                      mask_em[0:16, :], start=False, stop=True))(blk, bank),
                  reads=[MEMB, BS], writes=[PB[bank]], inc=True)
                E("scalar",
                  (lambda j_, bank_: lambda e: e.activation(
                      out=hh[:, j_, :], in_=PBT[bank_][:, 0:SC], func=AF.Relu,
                      bias=KB1t[:, j_:j_ + 1]))(j, bank),
                  reads=[PB[bank], KB1B],
                  writes=([HHB] if j == FT - 1 else []),
                  writes_pre=([HHB] if j == 0 else []), inc=True)
            for ot in range(FT):
                sl = ot % 2
                dma(w2bufb[:, sl, :], w2s[:, ot * 1024:(ot + 1) * 1024],
                    reads=([W2S, GC] if (first and ot < 2) else [W2S]),
                    writes=[W2SL[sl]], q="qw20" if sl == 0 else "qw21")
                for dt in range(KD):
                    E("tensor",
                      (lambda sl_, ot_, dt_: lambda e: e.matmul(
                          PBT[dt_][:, 0:SC],
                          w2bufb[:, sl_, dt_ * 128:(dt_ + 1) * 128],
                          hh[:, ot_, :],
                          start=(ot_ == 0), stop=False))(sl, ot, dt),
                      reads=[W2SL[sl], HHB],
                      writes_pre=([PB[dt]] if ot == 0 else []),
                      inc=(dt == KD - 1))
            for dt in range(KD):
                E("tensor",
                  (lambda dt_: lambda e: e.matmul(
                      PBT[dt_][:, 0:SC], b2row[0:1, dt_ * 128:(dt_ + 1) * 128],
                      onesrow512_bf, start=False, stop=True))(dt),
                  reads=[B2R, CB], writes=[PB[dt]], inc=True)
            if q == 0 and c == 0:
                tap("hh00", hh[:, 0, :], [128, SC], BF16, reads=[HHB])
            for dt in range(KD):
                E("vector",
                  (lambda dt_, t0_: lambda e: e.scalar_tensor_tensor(
                      out=X2t[:, dt_, t0_:t0_ + SC], in0=PBT[dt_][:, 0:SC],
                      scalar=modd[:, 24 + dt_:25 + dt_],
                      in1=X2t[:, dt_, t0_:t0_ + SC], op0=ALU.mult,
                      op1=ALU.add))(dt, t0),
                  reads=[PB[dt]], sreads=[MODD],
                  writes=([X2B] if dt == KD - 1 else []),
                  writes_pre=([X2B] if dt == 0 else []), inc=True)
            dma(o3[:, :, gtok:gtok + SC], X2t[:, :, t0:t0 + SC], reads=[X2B], q="qo")

        steps = [
            lambda: s1_quarter(0, filler=p0b_fill),
            lambda: s1_quarter(1, filler=p0b_fill),
            lambda: (p0b_fill(1000), kb.barrier()),
            lambda: s2_chunk(0, 0, first=True),
            lambda: s2_chunk(0, 1),
            lambda: kb.barrier(),
            lambda: s1_quarter(2),
            lambda: kb.barrier(),
            lambda: s2_chunk(1, 0),
            lambda: s2_chunk(1, 1),
            lambda: kb.barrier(),
            lambda: s1_quarter(3),
            lambda: kb.barrier(),
            lambda: s2_chunk(2, 0),
            lambda: s2_chunk(2, 1),
            lambda: s2_chunk(3, 0),
            lambda: s2_chunk(3, 1),
        ]
        import os as _os
        lim = int(_os.environ.get("KMK_STEPS", str(len(steps))))
        for st in steps[:lim]:
            st()

        kb.raw_wait("sync", [(qq, kb.cnt[qq]) for qq in QSEMS
                             if kb.cnt[qq] > 0], hard=True)

        sems = {}
        for e in list(ENGS) + list(QSEMS):
            sems[e] = ctx.enter_context(nc.semaphore("sem_" + e))
        kb.replay(nc, sems)

    return nc



def _host_vec(c_row, conv_w, conv_b, wr_b, wi_b, log_a, norm_w, b1, b2,
              router_b):
    v = np.zeros((128, VCOLS), np.float32)
    C = VEC_COLS
    v[:, C["ident"]:C["ident"] + 128] = np.eye(128, dtype=np.float32)
    v[:, C["ones"]] = 1.0
    v[:, C["epsln"]] = EPS_LN
    v[:, C["epsrms"]] = EPS_RMS
    v[:, C["epsrmsd"]] = EPS_RMS * DO
    for j in range(16):
        v[:, C["iota16"] + j] = float(j)
        v[:, C["iota99"] + j] = float(j) + 99.0
    v[:, C["c8"]:C["c8"] + 8] = c_row.reshape(8, 128).T
    for i in range(NL):
        for k in range(4):
            for ot in range(KO):
                v[:, C["cw"] + i * 8 + k * 2 + ot] = \
                    conv_w[i, ot * 128:(ot + 1) * 128, k]
        for ot in range(KO):
            v[:, C["cbv"] + i * 2 + ot] = conv_b[i, ot * 128:(ot + 1) * 128]
            v[:, C["wrb"] + i * 2 + ot] = wr_b[i, ot * 128:(ot + 1) * 128]
            v[:, C["wib"] + i * 2 + ot] = wi_b[i, ot * 128:(ot + 1) * 128]
            v[:, C["lav"] + i * 2 + ot] = log_a[i, ot * 128:(ot + 1) * 128]
            v[:, C["nwv"] + i * 2 + ot] = (norm_w[i, ot * 128:(ot + 1) * 128]
                                           * np.sqrt(float(DO)))
    v[:, C["b1c"]:C["b1c"] + 32] = b1.reshape(32, 128).T
    v[:, C["b2c"]:C["b2c"] + 8] = b2.reshape(8, 128).T
    v[0:16, C["rbc"]] = router_b
    return v


def _host_const():
    cpk = np.zeros((128, CPK_COLS), np.float32)
    cpk[:, CPK_ONES:CPK_ONES + 512] = 1.0
    cpk[:, CPK_BIG:CPK_BIG + 128] = BIGNEG
    bigsel = np.zeros((16, NB * 128), np.float32)
    for j in range(NB):
        bigsel[j, j * 128:(j + 1) * 128] = BIGNEG
    return cpk.astype(NPBF16), bigsel.astype(NPBF16)


def kernel(L0_attn_h, L0_attn_down_w, L0_attn_up_w, L0_attn_conv_w,
           L0_attn_conv_b, L0_attn_Wr_w, L0_attn_Wr_b, L0_attn_Wi_w,
           L0_attn_Wi_b, L0_attn_log_a, L0_attn_out_w, L0_attn_norm_w,
           L0_ffn_router_w, L0_ffn_router_b, L0_ffn_w1, L0_ffn_b1,
           L0_ffn_w2, L0_ffn_b2, L0_c, L0_wmod):
    f32 = np.float32
    asc = np.ascontiguousarray
    wmod_a = asc(np.asarray(L0_wmod, f32))
    dwT_a = asc(np.asarray(L0_attn_down_w, f32).T)
    upT_a = asc(np.asarray(L0_attn_up_w, f32).T)
    wrT_a = asc(np.asarray(L0_attn_Wr_w, f32).transpose(0, 2, 1))
    wiT_a = asc(np.asarray(L0_attn_Wi_w, f32).transpose(0, 2, 1))
    owT_a = asc(np.asarray(L0_attn_out_w, f32).transpose(0, 2, 1))
    w1T_a = asc(np.asarray(L0_ffn_w1, f32).T)
    w2T_a = asc(np.asarray(L0_ffn_w2, f32).T)
    rwT_a = asc(np.asarray(L0_ffn_router_w, f32).T)
    cpk_a, bigsel_a = _host_const()
    x = np.asarray(L0_attn_h, f32)

    nc = build_nc()
    in_maps = []
    for b in range(B):
        vec = _host_vec(np.asarray(L0_c, f32)[b], np.asarray(L0_attn_conv_w, f32),
                        np.asarray(L0_attn_conv_b, f32),
                        np.asarray(L0_attn_Wr_b, f32),
                        np.asarray(L0_attn_Wi_b, f32),
                        np.asarray(L0_attn_log_a, f32),
                        np.asarray(L0_attn_norm_w, f32),
                        np.asarray(L0_ffn_b1, f32), np.asarray(L0_ffn_b2, f32),
                        np.asarray(L0_ffn_router_b, f32))
        in_maps.append({
            "xT": asc(x[b].T),
            "wmod": wmod_a, "dwT": dwT_a, "upT": upT_a,
            "wrT": wrT_a, "wiT": wiT_a, "owT": owT_a,
            "w1T": w1T_a, "w2T": w2T_a, "rwT": rwT_a,
            "vec": vec, "cpk": cpk_a, "bigsel": bigsel_a,
        })
    res = run_bass_kernel_spmd(nc, in_maps, core_ids=list(range(B)))
    out = np.stack([np.ascontiguousarray(r["outT"].T) for r in res.results])
    if DEBUG_TAPS:
        kernel.last_results = res.results
    return out
